# version 01
# baseline (speedup 1.0000x reference)
import numpy as np
import concourse.bass as bass
import concourse.mybir as mybir
from concourse.bass_utils import run_bass_kernel_spmd
from contextlib import ExitStack

F32 = mybir.dt.float32
BF16 = mybir.dt.bfloat16
AF = mybir.ActivationFunctionType
ALU = mybir.AluOpType
AX = mybir.AxisListType

T = 2048
D = 1024
HID = 4096
NH = 16
FT = 256
EPS = 1e-5
ZEPS = 1e-8
NT128 = T // 128
NT512 = T // 512


class Prog:
    def __init__(self, nc, stack):
        self.nc = nc
        self._sems = {}
        self.cnt = {}
        self.stack = stack

    def sem(self, name):
        if name not in self._sems:
            self._sems[name] = self.stack.enter_context(self.nc.semaphore(name))
            self.cnt[name] = 0
        return self._sems[name]

    def bump(self, name, inst, n=1):
        inst.then_inc(self.sem(name), n)
        self.cnt[name] += n
        return self.cnt[name]

    def at(self, name):
        self.sem(name)
        return self.cnt[name]

    def wait(self, eng, name, v=None):
        if v is None:
            v = self.cnt[name]
        if v > 0:
            eng.wait_ge(self.sem(name), v)


def build_program(taps=(), stage=99):
    nc = bass.Bass("TRN2", target_bir_lowering=False, detect_race_conditions=False)
    taps = set(taps)

    X = nc.dram_tensor("x", [T, D], F32, kind="ExternalInput").ap()
    WQ = nc.dram_tensor("wq", [D, D], F32, kind="ExternalInput").ap()
    WK = nc.dram_tensor("wk", [D, D], F32, kind="ExternalInput").ap()
    WV = nc.dram_tensor("wv", [D, D], F32, kind="ExternalInput").ap()
    PROJP = nc.dram_tensor("projp", [8, 128, 512], F32, kind="ExternalInput").ap()
    WOUT = nc.dram_tensor("wout", [D, D], F32, kind="ExternalInput").ap()
    BOUTR = nc.dram_tensor("boutrow", [1, D], F32, kind="ExternalInput").ap()
    W1 = nc.dram_tensor("w1", [HID, D], F32, kind="ExternalInput").ap()
    W2 = nc.dram_tensor("w2", [D, HID], F32, kind="ExternalInput").ap()
    B1R = nc.dram_tensor("b1row", [1, HID], F32, kind="ExternalInput").ap()
    B2R = nc.dram_tensor("b2row", [1, D], F32, kind="ExternalInput").ap()
    G1F = nc.dram_tensor("g1F", [128, 8], F32, kind="ExternalInput").ap()
    B1LNF = nc.dram_tensor("b1lnF", [128, 8], F32, kind="ExternalInput").ap()
    G2F = nc.dram_tensor("g2F", [128, 32], F32, kind="ExternalInput").ap()
    LN2BF = nc.dram_tensor("ln2bF", [128, 32], F32, kind="ExternalInput").ap()
    IDENTB = nc.dram_tensor("identb", [128, 128], BF16, kind="ExternalInput").ap()
    ONESB = nc.dram_tensor("onesb", [1, 128], BF16, kind="ExternalInput").ap()
    ONESF = nc.dram_tensor("onesf", [128, 1], F32, kind="ExternalInput").ap()
    ONESROWF = nc.dram_tensor("onesrowf", [1, 128], F32, kind="ExternalInput").ap()
    E16 = nc.dram_tensor("e16", [16, D], F32, kind="ExternalInput").ap()
    OUT = nc.dram_tensor("out", [T, D], F32, kind="ExternalOutput").ap()

    X1D = nc.dram_tensor("x1d", [T, D], F32)
    cc_in = nc.dram_tensor("cc_in", [65, NH * FT], F32)
    cc_out = nc.dram_tensor("cc_out", [65, NH * FT], F32)

    tap_aps = {}
    if "kv" in taps:
        tap_aps["t_kv"] = nc.dram_tensor("t_kv", [65, NH * FT], F32, kind="ExternalOutput").ap()
    if "kvloc" in taps:
        tap_aps["t_kvloc"] = nc.dram_tensor("t_kvloc", [65, NH * FT], F32, kind="ExternalOutput").ap()
    if "x1" in taps:
        tap_aps["t_x1"] = nc.dram_tensor("t_x1", [T, D], F32, kind="ExternalOutput").ap()
    if "kp" in taps:
        tap_aps["t_xt"] = nc.dram_tensor("t_xt", [128, 1024], BF16, kind="ExternalOutput").ap()
        tap_aps["t_st"] = nc.dram_tensor("t_st", [128, 8], F32, kind="ExternalOutput").ap()
        tap_aps["t_kp0"] = nc.dram_tensor("t_kp0", [128, NH * FT], BF16, kind="ExternalOutput").ap()
        tap_aps["t_kp1"] = nc.dram_tensor("t_kp1", [128, NH * FT], BF16, kind="ExternalOutput").ap()
        tap_aps["t_kF"] = nc.dram_tensor("t_kF", [128, 8 * 512], BF16, kind="ExternalOutput").ap()
        tap_aps["t_va0"] = nc.dram_tensor("t_va0", [128, NH * 66], BF16, kind="ExternalOutput").ap()
        tap_aps["t_va1"] = nc.dram_tensor("t_va1", [128, NH * 66], BF16, kind="ExternalOutput").ap()
        tap_aps["t_xf"] = nc.dram_tensor("t_xf", [128, 8 * 512], BF16, kind="ExternalOutput").ap()
    if "pc" in taps:
        tap_aps["t_xlnC"] = nc.dram_tensor("t_xlnC", [128, D], BF16, kind="ExternalOutput").ap()
        tap_aps["t_hg"] = nc.dram_tensor("t_hg", [128, HID], F32, kind="ExternalOutput").ap()
        tap_aps["t_hn"] = nc.dram_tensor("t_hn", [128, HID], BF16, kind="ExternalOutput").ap()
        tap_aps["t_ac"] = nc.dram_tensor("t_ac", [128, 2 * NT128], F32, kind="ExternalOutput").ap()
        tap_aps["t_r2"] = nc.dram_tensor("t_r2", [128, 1], F32, kind="ExternalOutput").ap()
        tap_aps["t_hgs"] = nc.dram_tensor("t_hgs", [128, 8], F32, kind="ExternalOutput").ap()
        tap_aps["t_hgss"] = nc.dram_tensor("t_hgss", [128, 8], F32, kind="ExternalOutput").ap()
    if "qc" in taps:
        tap_aps["t_m"] = nc.dram_tensor("t_m", [1, 8], F32, kind="ExternalOutput").ap()
        tap_aps["t_c1"] = nc.dram_tensor("t_c1", [1, HID], BF16, kind="ExternalOutput").ap()
        tap_aps["t_c2"] = nc.dram_tensor("t_c2", [128, D], F32, kind="ExternalOutput").ap()
        tap_aps["t_w1"] = nc.dram_tensor("t_w1", [128, HID], BF16, kind="ExternalOutput").ap()
        tap_aps["t_w2"] = nc.dram_tensor("t_w2", [128, D], BF16, kind="ExternalOutput").ap()

    st = ExitStack()
    with st:
        en = st.enter_context
        p = Prog(nc, st)

        identb = en(nc.sbuf_tensor("identb_s", [128, 128], BF16))
        onesb = en(nc.sbuf_tensor("onesb_s", [1, 128], BF16))
        onescol = en(nc.sbuf_tensor("onescol_s", [128, 1], F32))
        onesrow = en(nc.sbuf_tensor("onesrow_s", [1, 128], F32))
        g1F = en(nc.sbuf_tensor("g1F_s", [128, 8], F32))
        g2F = en(nc.sbuf_tensor("g2F_s", [128, 32], F32))
        b1lnB = en(nc.sbuf_tensor("b1lnB_s", [128, 8], BF16))
        ln2bB = en(nc.sbuf_tensor("ln2bB_s", [128, 32], BF16))
        thr1 = en(nc.sbuf_tensor("thr1", [128, 1], F32))
        nthr1 = en(nc.sbuf_tensor("nthr1", [128, 1], F32))
        thr2 = en(nc.sbuf_tensor("thr2", [128, 1], F32))
        nthr2 = en(nc.sbuf_tensor("nthr2", [128, 1], F32))
        k1b = en(nc.sbuf_tensor("k1b", [128, 1], F32))
        k2b = en(nc.sbuf_tensor("k2b", [128, 1], F32))
        m1s = en(nc.sbuf_tensor("m1s", [1, 1], F32))
        m2s = en(nc.sbuf_tensor("m2s", [1, 1], F32))
        im1 = en(nc.sbuf_tensor("im1", [1, 1], F32))
        aC = en(nc.sbuf_tensor("aC", [128, NT128], F32))
        cC = en(nc.sbuf_tensor("cC", [128, NT128], F32))
        junk1k = en(nc.sbuf_tensor("junk1k", [128, 1024], F32))
        s1t = en(nc.sbuf_tensor("s1t", [128, 1], F32))
        s2t = en(nc.sbuf_tensor("s2t", [128, 1], F32))
        mut = en(nc.sbuf_tensor("mut", [128, 1], F32))
        vart = en(nc.sbuf_tensor("vart", [128, 1], F32))
        rt = en(nc.sbuf_tensor("rt", [128, 1], F32))
        bt = en(nc.sbuf_tensor("bt", [128, 1], F32))
        tsc = en(nc.sbuf_tensor("tsc", [128, 1], F32))

        ps_a = [en(nc.psum_tensor(f"ps_a{i}", [128, 512], F32)) for i in range(2)]
        ps_tr = en(nc.psum_tensor("ps_tr", [128, 1024], BF16))
        ps_tr2 = en(nc.psum_tensor("ps_tr2", [128, 1024], BF16))

        st_ab = ExitStack()
        ab = st_ab.enter_context
        e16 = ab(nc.sbuf_tensor("e16_s", [16, D], F32))
        projp = ab(nc.sbuf_tensor("projp_s", [128, 8 * 512], BF16))
        woutB = ab(nc.sbuf_tensor("wout_s", [128, 8 * 1024], BF16))
        wqB = ab(nc.sbuf_tensor("wq_s", [128, 8 * 1024], BF16))
        boutr = ab(nc.sbuf_tensor("boutr_s", [1, D], BF16))
        kvT = ab(nc.sbuf_tensor("kvT", [128, 32 * 65], BF16))
        ksblk = ab(nc.sbuf_tensor("ksblk", [128, 32 * 16], BF16))
        xlnT = [ab(nc.sbuf_tensor(f"xlnT{i}", [128, 1024], BF16)) for i in range(2)]
        xlnF = ab(nc.sbuf_tensor("xlnF", [128, 8 * 512], BF16))
        ebuf = ab(nc.sbuf_tensor("ebuf", [128, 512], F32))
        mbuf = ab(nc.sbuf_tensor("mbuf", [128, 512], F32))
        rbuf = ab(nc.sbuf_tensor("rbuf", [128, 512], F32))

        def emit_ln_chain(block, v_s1, s_s2, n=1024.0):
            def vc1(vector):
                p.wait(vector, "se", s_s2)
                p.wait(vector, "ve", v_s1)
                vector.tensor_scalar_mul(mut[:, 0:1], s1t[:, 0:1], 1.0 / n)
                vector.tensor_tensor(tsc[:, 0:1], s1t[:, 0:1], s1t[:, 0:1], ALU.mult)
                i = vector.tensor_scalar_mul(vart[:, 0:1], s2t[:, 0:1], 1.0 / n)
                p.bump("ve", i)
            block.vector(vc1)

            def vc2(vector, fa=p.at("ve")):
                p.wait(vector, "ve", fa)
                i = vector.scalar_tensor_tensor(tsc[:, 0:1], tsc[:, 0:1], 1.0 / (n * n),
                                                vart[:, 0:1], ALU.mult, ALU.subtract)
                p.bump("ve", i)
            block.vector(vc2)

            def vc3(vector, fa=p.at("ve")):
                p.wait(vector, "ve", fa)
                i = vector.tensor_scalar(vart[:, 0:1], tsc[:, 0:1], -1.0, float(EPS),
                                         ALU.mult, ALU.add)
                p.bump("ve", i)
            block.vector(vc3)
            vv = p.at("ve")

            def sc(scalar):
                p.wait(scalar, "ve", vv)
                scalar.activation(rt[:, 0:1], vart[:, 0:1], AF.Ln)
                i = scalar.activation(rt[:, 0:1], rt[:, 0:1], AF.Exp, scale=-0.5)
                p.bump("se", i)
            block.scalar(sc)
            sv = p.at("se")

            def vb(vector):
                p.wait(vector, "se", sv)
                i = vector.tensor_scalar(bt[:, 0:1], mut[:, 0:1], -1.0, rt[:, 0:1],
                                         ALU.mult, ALU.mult)
                p.bump("ve", i)
            block.vector(vb)
            return p.at("ve")

        st_a = ExitStack()
        a_ = st_a.enter_context
        wkB = a_(nc.sbuf_tensor("wk_s", [128, 8 * 1024], BF16))
        wvB = a_(nc.sbuf_tensor("wv_s", [128, 8 * 1024], BF16))
        kv_sb = a_(nc.sbuf_tensor("kv_sb", [65, NH * FT], F32))
        kF = a_(nc.sbuf_tensor("kF", [128, 8 * 512], BF16))
        vaug = [a_(nc.sbuf_tensor(f"vaug{s}", [128, NH * 66], BF16)) for s in range(2)]
        kpT = [a_(nc.sbuf_tensor(f"kpT{s}", [128, NH * FT], BF16)) for s in range(2)]
        xbuf = [a_(nc.sbuf_tensor(f"xbuf{i}", [128, 1024], F32)) for i in range(2)]

        st_p0 = ExitStack()
        stg0 = [st_p0.enter_context(nc.sbuf_tensor(f"stg0_{i}", [128, 1024], F32))
                for i in range(2)]
        with nc.Block() as block:
            def _(sync):
                for ap_d, ap_s in [(identb[:], IDENTB), (onesb[:], ONESB),
                                   (onescol[:], ONESF), (onesrow[:], ONESROWF),
                                   (e16[:], E16), (g1F[:], G1F), (g2F[:], G2F)]:
                    p.bump("sL", sync.dma_start(ap_d, ap_s), 16)
                p.bump("sL", sync.dma_start(junk1k[0:128, 0:8], B1LNF), 16)
                p.bump("sL", sync.dma_start(junk1k[0:128, 8:40], LN2BF), 16)
                p.bump("sL", sync.dma_start(stg0[0][0:1, 0:D], BOUTR), 16)
            block.sync(_)
            lv = p.at("sL")

            def _(vector):
                p.wait(vector, "sL", lv)
                vector.tensor_copy(b1lnB[:], junk1k[:, 0:8])
                vector.tensor_copy(ln2bB[:], junk1k[:, 8:40])
                i = vector.tensor_copy(boutr[:], stg0[0][0:1, 0:D])
                p.bump("ve", i)
            block.vector(_)

        with nc.Block() as block:
            seq = ([("wq", c, WQ, wqB, 1024) for c in range(8)] +
                   [("wk", c, WK, wkB, 1024) for c in range(8)] +
                   [("wv", c, WV, wvB, 1024) for c in range(8)] +
                   [("wo", c, WOUT, woutB, 1024) for c in range(8)] +
                   [("pp", c, None, projp, 512) for c in range(8)])
            free_at = [p.at("ve"), p.at("ve")]
            for idx, (kind, c, src, dst, ncol) in enumerate(seq):
                par = idx % 2
                buf = stg0[par]
                src_ap = PROJP[c] if kind == "pp" else src[c * 128:(c + 1) * 128, :]

                def load(sync, buf=buf, src_ap=src_ap, ncol=ncol, fa=free_at[par]):
                    p.wait(sync, "ve", fa)
                    p.bump("sL", sync.dma_start(buf[:, 0:ncol], src_ap), 16)
                block.sync(load)
                lv = p.at("sL")

                def cast(vector, buf=buf, dst=dst, c=c, ncol=ncol, lv=lv):
                    p.wait(vector, "sL", lv)
                    p.bump("ve", vector.tensor_copy(dst[:, c * ncol:(c + 1) * ncol],
                                                    buf[:, 0:ncol]))
                block.vector(cast)
                free_at[par] = p.at("ve")
        st_p0.close()

        st_pa = ExitStack()
        ps_kv = [st_pa.enter_context(nc.psum_tensor(f"ps_kv{i}", [65, 512], F32))
                 for i in range(4)]
        with nc.Block() as block:
            def _(vector):
                for s in range(2):
                    va = vaug[s][:].rearrange("p (h c) -> p h c", c=66)
                    vector.memset(va[:, :, 64:66], 1.0)
                i = vector.memset(ksblk[:], 0.0)
                p.bump("ve", i)
            block.vector(_)

            x_free_v = [p.at("ve")] * 2
            x_free_s = [p.at("se")] * 2
            psA_free_v = [p.at("ve")] * 2
            psA_free_s = [p.at("se")] * 2
            tr_free = p.at("ve")
            kv_free = [p.at("ve")] * 4
            prev_pe = p.at("pe")
            for t512 in range(NT512):
                for s in range(4):
                    i128 = t512 * 4 + s
                    par = i128 % 2

                    def load_x(sync, i128=i128, par=par, fv=x_free_v[par], fs=x_free_s[par]):
                        p.wait(sync, "ve", fv)
                        p.wait(sync, "se", fs)
                        p.bump("sL", sync.dma_start(xbuf[par][:],
                                                    X[i128 * 128:(i128 + 1) * 128, :]), 16)
                    block.sync(load_x)
                    lv = p.at("sL")

                    def vs(vector, par=par, lv=lv):
                        p.wait(vector, "sL", lv)
                        p.bump("ve", vector.tensor_reduce(s1t[:, 0:1], xbuf[par][:],
                                                          AX.X, ALU.add))
                    block.vector(vs)
                    v_s1 = p.at("ve")

                    def ss(scalar, par=par, lv=lv):
                        p.wait(scalar, "sL", lv)
                        p.bump("se", scalar.activation(junk1k[:, :], xbuf[par][:], AF.Square,
                                                       accum_out=s2t[:, 0:1]))
                    block.scalar(ss)
                    s_s2 = p.at("se")
                    v_bt = emit_ln_chain(block, v_s1, s_s2)

                    def sa(scalar, par=par, v_bt=v_bt, pef=prev_pe):
                        p.wait(scalar, "ve", v_bt)
                        p.wait(scalar, "pe", pef)
                        i = scalar.activation(xlnT[par][:], xbuf[par][:], AF.Identity,
                                              bias=bt[:, 0:1], scale=rt[:, 0:1])
                        p.bump("se", i)
                    block.scalar(sa)
                    s_ap = p.at("se")
                    x_free_v[par] = v_s1
                    x_free_s[par] = s_ap

                    def trs(tensor, par=par, s_ap=s_ap, fa=tr_free):
                        p.wait(tensor, "se", s_ap)
                        p.wait(tensor, "ve", fa)
                        for c in range(8):
                            i = tensor.transpose(ps_tr[:, c * 128:(c + 1) * 128],
                                                 xlnT[par][:, c * 128:(c + 1) * 128], identb[:])
                        p.bump("pe", i)
                    block.tensor(trs)
                    pe_tr = p.at("pe")

                    def evtr(vector, s=s, pe_tr=pe_tr):
                        p.wait(vector, "pe", pe_tr)
                        src = ps_tr[:].rearrange("p (c x) -> p c x", x=128)
                        dst = xlnF[:].rearrange("p (c x) -> p c x", x=512)[:, :, s * 128:(s + 1) * 128]
                        p.bump("ve", vector.tensor_copy(dst, src))
                    block.vector(evtr)
                    tr_free = p.at("ve")

                xlnF_ready = p.at("ve")
                for m in range(8):
                    bank = m % 2

                    def mmk(tensor, m=m, bank=bank, fv=psA_free_v[bank], fs=psA_free_s[bank],
                            xr=xlnF_ready):
                        p.wait(tensor, "ve", max(fv, xr))
                        p.wait(tensor, "se", fs)
                        for kc in range(8):
                            i = tensor.matmul(ps_a[bank][:],
                                              wkB[:, kc * 1024 + m * 128: kc * 1024 + (m + 1) * 128],
                                              xlnF[:, kc * 512:(kc + 1) * 512],
                                              start=(kc == 0), stop=(kc == 7))
                        p.bump("pe", i)
                    block.tensor(mmk)
                    pv = p.at("pe")

                    def evk(vector, m=m, bank=bank, pv=pv):
                        p.wait(vector, "pe", pv)
                        p.bump("ve", vector.tensor_copy(kF[:, m * 512:(m + 1) * 512],
                                                        ps_a[bank][:]))
                    block.vector(evk)
                    psA_free_v[bank] = p.at("ve")
                kF_ready = p.at("ve")

                for half in range(2):
                    for s2 in range(2):
                        s = half * 2 + s2
                        for nh2 in range(2):
                            bank = (s2 * 2 + nh2) % 2

                            def mmv(tensor, s=s, nh2=nh2, bank=bank,
                                    fv=psA_free_v[bank], fs=psA_free_s[bank]):
                                p.wait(tensor, "ve", fv)
                                p.wait(tensor, "se", fs)
                                for kc in range(8):
                                    i = tensor.matmul(
                                        ps_a[bank][:],
                                        xlnF[:, kc * 512 + s * 128: kc * 512 + (s + 1) * 128],
                                        wvB[:, kc * 1024 + nh2 * 512: kc * 1024 + (nh2 + 1) * 512],
                                        start=(kc == 0), stop=(kc == 7))
                                p.bump("pe", i)
                            block.tensor(mmv)
                            pv = p.at("pe")

                            def evv(vector, s2=s2, nh2=nh2, bank=bank, pv=pv):
                                p.wait(vector, "pe", pv)
                                src = ps_a[bank][:].rearrange("p (h x) -> p h x", x=64)
                                dst = vaug[s2][:].rearrange("p (h c) -> p h c", c=66)[:, nh2 * 8:(nh2 + 1) * 8, 0:64]
                                p.bump("ve", vector.tensor_copy(dst, src))
                            block.vector(evv)
                            psA_free_v[bank] = p.at("ve")

                    for j in range(8):
                        for s2 in range(2):
                            s = half * 2 + s2
                            bank = (j * 2 + s2) % 2

                            def mmkp(tensor, j=j, s=s, bank=bank, fv=psA_free_v[bank],
                                     fs=psA_free_s[bank], kr=kF_ready):
                                p.wait(tensor, "ve", max(fv, kr))
                                p.wait(tensor, "se", fs)
                                i = tensor.matmul(ps_a[bank][:],
                                                  kF[:, j * 512 + s * 128: j * 512 + (s + 1) * 128],
                                                  projp[:, j * 512:(j + 1) * 512],
                                                  start=True, stop=True)
                                p.bump("pe", i)
                            block.tensor(mmkp)
                            pv = p.at("pe")

                            def vmin(vector, bank=bank, pv=pv):
                                p.wait(vector, "pe", pv)
                                p.bump("ve", vector.tensor_scalar_min(mbuf[:], ps_a[bank][:], 0.0))
                            block.vector(vmin)
                            v_m = p.at("ve")

                            def srel(scalar, bank=bank, pv=pv, v_m=v_m):
                                p.wait(scalar, "pe", pv)
                                p.wait(scalar, "ve", v_m)
                                scalar.activation(rbuf[:], ps_a[bank][:], AF.Relu)
                                i = scalar.activation(ebuf[:], mbuf[:], AF.Exp)
                                p.bump("se", i)
                            block.scalar(srel)
                            s_re = p.at("se")

                            def vadd(vector, j=j, s2=s2, s_re=s_re):
                                p.wait(vector, "se", s_re)
                                i = vector.tensor_tensor(kpT[s2][:, j * 512:(j + 1) * 512],
                                                         rbuf[:], ebuf[:], ALU.add)
                                p.bump("ve", i)
                            block.vector(vadd)
                            psA_free_v[bank] = v_m
                            psA_free_s[bank] = s_re

                    kp_ready = p.at("ve")
                    for g in range(2):
                        for jj in range(4):
                            j = g * 4 + jj

                            def mmkv(tensor, j=j, jj=jj, fa=kv_free[jj], kr=kp_ready):
                                p.wait(tensor, "ve", max(fa, kr))
                                for hh in range(2):
                                    h = 2 * j + hh
                                    for s2 in range(2):
                                        i = tensor.matmul(
                                            ps_kv[jj][:, hh * 256:(hh + 1) * 256],
                                            vaug[s2][:, h * 66: h * 66 + 65],
                                            kpT[s2][:, h * 256:(h + 1) * 256],
                                            start=(s2 == 0), stop=(s2 == 1),
                                            skip_group_check=True)
                                p.bump("pe", i)
                            block.tensor(mmkv)
                            pv = p.at("pe")

                            def vkv(vector, j=j, jj=jj, first=(t512 == 0 and half == 0), pv=pv):
                                p.wait(vector, "pe", pv)
                                if first:
                                    i = vector.tensor_copy(kv_sb[:, j * 512:(j + 1) * 512],
                                                           ps_kv[jj][:])
                                else:
                                    i = vector.tensor_tensor(kv_sb[:, j * 512:(j + 1) * 512],
                                                             kv_sb[:, j * 512:(j + 1) * 512],
                                                             ps_kv[jj][:], ALU.add)
                                p.bump("ve", i)
                            block.vector(vkv)
                            kv_free[jj] = p.at("ve")
                prev_pe = p.at("pe")

            if "kp" in taps:
                def vstt(vector):
                    vector.tensor_copy(junk1k[:, 0:1], s1t[:, 0:1])
                    vector.tensor_copy(junk1k[:, 1:2], s2t[:, 0:1])
                    vector.tensor_copy(junk1k[:, 2:3], mut[:, 0:1])
                    vector.tensor_copy(junk1k[:, 3:4], vart[:, 0:1])
                    vector.tensor_copy(junk1k[:, 4:5], rt[:, 0:1])
                    i = vector.tensor_copy(junk1k[:, 5:6], bt[:, 0:1])
                    p.bump("ve", i)
                block.vector(vstt)

                def tkp(g):
                    p.wait(g, "ve", p.at("ve"))
                    p.bump("sG", g.dma_start(tap_aps["t_xt"], xlnT[1][:]), 16)
                    p.bump("sG", g.dma_start(tap_aps["t_st"], junk1k[:, 0:8]), 16)
                    p.bump("sG", g.dma_start(tap_aps["t_kp0"], kpT[0][:]), 16)
                    p.bump("sG", g.dma_start(tap_aps["t_kp1"], kpT[1][:]), 16)
                    p.bump("sG", g.dma_start(tap_aps["t_kF"], kF[:]), 16)
                    p.bump("sG", g.dma_start(tap_aps["t_va0"], vaug[0][:]), 16)
                    p.bump("sG", g.dma_start(tap_aps["t_va1"], vaug[1][:]), 16)
                    p.bump("sG", g.dma_start(tap_aps["t_xf"], xlnF[:]), 16)
                    p.wait(g, "sG", p.at("sG"))
                block.gpsimd(tkp)

        kv_done_v = p.at("ve")
        with nc.Block() as block:
            def _(g):
                p.wait(g, "ve", kv_done_v)
                p.bump("sG", g.dma_start(cc_in.ap(), kv_sb[:]), 16)
                p.wait(g, "sG", p.at("sG"))
                i = g.collective_compute(
                    "AllReduce", ALU.add,
                    replica_groups=[[0, 1, 2, 3], [4, 5, 6, 7]],
                    ins=[cc_in.ap().opt()],
                    outs=[cc_out.ap().opt()])
                p.bump("cc", i)
            block.gpsimd(_)
            if "kvloc" in taps:
                def _(g):
                    p.bump("sG", g.dma_start(tap_aps["t_kvloc"], kv_sb[:]), 16)
                block.gpsimd(_)
        st_pa.close()
        st_a.close()

        st_pr1 = ExitStack()
        wstg = [st_pr1.enter_context(nc.sbuf_tensor(f"wstg{i}", [128, 2048], F32))
                for i in range(2)]
        acc80 = st_pr1.enter_context(nc.sbuf_tensor("acc80", [128, 80], F32))
        ps_sc = st_pr1.enter_context(nc.psum_tensor("ps_sc", [128, 16], F32))
        with nc.Block() as block:
            cols = ([("w1", c, 0) for c in range(32)] +
                    [("w2", c, hc) for c in range(8) for hc in range(2)])
            free_at = [p.at("ve"), p.at("ve")]
            for idx, (kind, c, hc) in enumerate(cols):
                par = idx % 2
                ncol = 1024 if kind == "w1" else 2048
                src = (W1[c * 128:(c + 1) * 128, :] if kind == "w1"
                       else W2[c * 128:(c + 1) * 128, hc * 2048:(hc + 1) * 2048])

                def load(sync, src=src, ncol=ncol, par=par, fa=free_at[par]):
                    p.wait(sync, "ve", fa)
                    p.bump("sL", sync.dma_start(wstg[par][:, 0:ncol], src), 16)
                block.sync(load)
                lv = p.at("sL")

                def red(vector, idx=idx, ncol=ncol, par=par, lv=lv):
                    p.wait(vector, "sL", lv)
                    i = vector.tensor_reduce(acc80[:, idx:idx + 1], wstg[par][:, 0:ncol],
                                             AX.X, ALU.add, apply_absolute_value=True)
                    p.bump("ve", i)
                block.vector(red)
                free_at[par] = p.at("ve")

            def vtot(vector, fa=p.at("ve")):
                p.wait(vector, "ve", fa)
                vector.tensor_reduce(acc80[:, 48:49], acc80[:, 0:32], AX.X, ALU.add)
                i = vector.tensor_reduce(acc80[:, 49:50], acc80[:, 32:48], AX.X, ALU.add)
                p.bump("ve", i)
            block.vector(vtot)

            def vtot2(vector, fa=p.at("ve")):
                p.wait(vector, "ve", fa)
                vector.tensor_copy(acc80[:, 0:1], acc80[:, 48:49])
                i = vector.tensor_copy(acc80[:, 1:2], acc80[:, 49:50])
                p.bump("ve", i)
            block.vector(vtot2)
            vv = p.at("ve")

            def mm(tensor, vv=vv):
                p.wait(tensor, "ve", vv)
                i = tensor.matmul(ps_sc[0:1, 0:2], onescol[:, 0:1], acc80[:, 0:2],
                                  start=True, stop=True)
                p.bump("pe", i)
            block.tensor(mm)
            pv = p.at("pe")

            def vsc(vector, pv=pv):
                p.wait(vector, "pe", pv)
                vector.tensor_scalar(m1s[:, 0:1], ps_sc[0:1, 0:1], 1.0 / (HID * D), 1e-5,
                                     ALU.mult, ALU.max)
                i = vector.tensor_scalar(m2s[:, 0:1], ps_sc[0:1, 1:2], 1.0 / (HID * D), 1e-5,
                                         ALU.mult, ALU.max)
                p.bump("ve", i)
            block.vector(vsc)

            def vsc2(vector, fa=p.at("ve")):
                p.wait(vector, "ve", fa)
                i = vector.reciprocal(im1[:, 0:1], m1s[:, 0:1])
                p.bump("ve", i)
            block.vector(vsc2)

            def vsc3(vector, fa=p.at("ve")):
                p.wait(vector, "ve", fa)
                i = vector.tensor_scalar_mul(im1[:, 0:1], im1[:, 0:1], 2.0)
                p.bump("ve", i)
            block.vector(vsc3)
            vv2 = p.at("ve")

            def mb(tensor, vv2=vv2):
                p.wait(tensor, "ve", vv2)
                tensor.matmul(ps_sc[:, 2:3], onesrow[:, :], m1s[:, 0:1], start=True, stop=True)
                i = tensor.matmul(ps_sc[:, 3:4], onesrow[:, :], m2s[:, 0:1], start=True, stop=True)
                p.bump("pe", i)
            block.tensor(mb)
            pv2 = p.at("pe")

            def vbb(vector, pv2=pv2):
                p.wait(vector, "pe", pv2)
                vector.tensor_scalar_mul(thr1[:, 0:1], ps_sc[:, 2:3], 0.5)
                vector.tensor_scalar_mul(nthr1[:, 0:1], ps_sc[:, 2:3], -0.5)
                vector.tensor_scalar_mul(k1b[:, 0:1], ps_sc[:, 2:3], 0.5)
                vector.tensor_scalar_mul(thr2[:, 0:1], ps_sc[:, 3:4], 0.5)
                vector.tensor_scalar_mul(nthr2[:, 0:1], ps_sc[:, 3:4], -0.5)
                i = vector.tensor_scalar_mul(k2b[:, 0:1], ps_sc[:, 3:4], 0.5)
                p.bump("ve", i)
            block.vector(vbb)
        st_pr1.close()

        st_kt = ExitStack()
        kv_red = st_kt.enter_context(nc.sbuf_tensor("kv_red", [65, NH * FT], F32))
        kv_redb = st_kt.enter_context(nc.sbuf_tensor("kv_redb", [65, NH * FT], BF16))
        with nc.Block() as block:
            def _(scalar):
                p.wait(scalar, "cc", 1)
                p.bump("sS", scalar.dma_start(kv_red[:], cc_out.ap()), 16)
            block.scalar(_)
            lv = p.at("sS")
            if "kv" in taps:
                def _(g, lv=lv):
                    p.wait(g, "sS", lv)
                    p.bump("sG", g.dma_start(tap_aps["t_kv"], kv_red[:]), 16)
                block.gpsimd(_)

            def vc(vector, lv=lv):
                p.wait(vector, "sS", lv)
                p.bump("ve", vector.tensor_copy(kv_redb[:], kv_red[:]))
            block.vector(vc)
            vv = p.at("ve")

            ev_free = [p.at("ve")] * 2
            for c in range(32):
                slot = c % 2
                pst = ps_tr if slot == 0 else ps_tr2

                def tr1(tensor, c=c, pst=pst, vv=vv, fa=ev_free[slot]):
                    p.wait(tensor, "ve", max(vv, fa))
                    i = tensor.transpose(pst[:, 0:65], kv_redb[:, c * 128:(c + 1) * 128],
                                         identb[0:65, 0:65])
                    p.bump("pe", i)
                block.tensor(tr1)
                pv = p.at("pe")

                def ev1(vector, c=c, pst=pst, pv=pv):
                    p.wait(vector, "pe", pv)
                    vector.tensor_copy(kvT[:, c * 65:(c + 1) * 65], pst[:, 0:65])
                    i = vector.tensor_copy(ksblk[:, c * 16 + c // 2: c * 16 + c // 2 + 1],
                                           pst[:, 64:65])
                    p.bump("ve", i)
                block.vector(ev1)
                ev_free[slot] = p.at("ve")
        st_kt.close()

        if stage < 2:
            with nc.Block() as block:
                def _(g):
                    p.wait(g, "ve", p.at("ve"))
                    p.wait(g, "sG", p.at("sG"))
                block.gpsimd(_)
            st_ab.close()
            return nc, tap_aps

        st_pb = ExitStack()
        pb = st_pb.enter_context
        qF = pb(nc.sbuf_tensor("qF", [128, 8 * 512], BF16))
        qpF = pb(nc.sbuf_tensor("qpF", [128, 32 * 512], BF16))
        attF = pb(nc.sbuf_tensor("attF", [128, 8 * 512], BF16))
        zrow = pb(nc.sbuf_tensor("zrow", [16, 512], F32))
        z2s = pb(nc.sbuf_tensor("z2s", [128, 512], F32))
        x1t = pb(nc.sbuf_tensor("x1t", [128, 1024], F32))
        xb4 = [pb(nc.sbuf_tensor(f"xb4_{s}", [128, 1024], F32)) for s in range(4)]
        s1c = pb(nc.sbuf_tensor("s1c", [128, NT128], F32))
        s2c = pb(nc.sbuf_tensor("s2c", [128, NT128], F32))
        muC = pb(nc.sbuf_tensor("muC", [128, NT128], F32))
        v1C = pb(nc.sbuf_tensor("v1C", [128, NT128], F32))
        u1C = pb(nc.sbuf_tensor("u1C", [128, NT128], F32))
        r1C = pb(nc.sbuf_tensor("r1C", [128, NT128], F32))
        u2C = pb(nc.sbuf_tensor("u2C", [128, NT128], F32))
        ps_den = pb(nc.psum_tensor("ps_den", [16, 512], F32))
        ps_z2 = pb(nc.psum_tensor("ps_z2", [128, 512], F32))
        ps_o = [pb(nc.psum_tensor(f"ps_o{i}", [128, 512], F32)) for i in range(2)]

        with nc.Block() as block:
            psA_free_v = [p.at("ve")] * 2
            psA_free_s = [p.at("se")] * 2
            o_free = [p.at("ve")] * 2
            tr_free = p.at("ve")
            z2_free = p.at("ve")
            den_free = p.at("ve")
            xb_free_v = [p.at("ve")] * 4
            xb_free_s = [p.at("se")] * 4
            xln_free = [p.at("pe")] * 2
            x1_gate = p.at("sG")
            prev_pe = p.at("pe")
            for t512 in range(NT512):
                for s in range(4):
                    i128 = t512 * 4 + s
                    par = i128 % 2

                    def load_x(sync, i128=i128, s=s, fv=xb_free_v[s], fs=xb_free_s[s]):
                        p.wait(sync, "ve", fv)
                        p.wait(sync, "se", fs)
                        p.bump("sL", sync.dma_start(xb4[s][:],
                                                    X[i128 * 128:(i128 + 1) * 128, :]), 16)
                    block.sync(load_x)
                    lv = p.at("sL")

                    def vs(vector, s=s, lv=lv):
                        p.wait(vector, "sL", lv)
                        p.bump("ve", vector.tensor_reduce(s1t[:, 0:1], xb4[s][:],
                                                          AX.X, ALU.add))
                    block.vector(vs)
                    v_s1 = p.at("ve")

                    def ss(scalar, s=s, lv=lv):
                        p.wait(scalar, "sL", lv)
                        p.bump("se", scalar.activation(junk1k[:, :], xb4[s][:], AF.Square,
                                                       accum_out=s2t[:, 0:1]))
                    block.scalar(ss)
                    s_s2 = p.at("se")
                    v_bt = emit_ln_chain(block, v_s1, s_s2)

                    def sa(scalar, s=s, par=par, v_bt=v_bt, fa=xln_free[par]):
                        p.wait(scalar, "ve", v_bt)
                        p.wait(scalar, "pe", fa)
                        i = scalar.activation(xlnT[par][:], xb4[s][:], AF.Identity,
                                              bias=bt[:, 0:1], scale=rt[:, 0:1])
                        p.bump("se", i)
                    block.scalar(sa)
                    s_ap = p.at("se")
                    xb_free_s[s] = s_ap
                    xb_free_v[s] = v_s1

                    def trs(tensor, par=par, s_ap=s_ap, fa=tr_free):
                        p.wait(tensor, "se", s_ap)
                        p.wait(tensor, "ve", fa)
                        for c in range(8):
                            i = tensor.transpose(ps_tr[:, c * 128:(c + 1) * 128],
                                                 xlnT[par][:, c * 128:(c + 1) * 128], identb[:])
                        p.bump("pe", i)
                    block.tensor(trs)
                    pe_tr = p.at("pe")
                    xln_free[par] = pe_tr

                    def evtr(vector, s=s, pe_tr=pe_tr):
                        p.wait(vector, "pe", pe_tr)
                        src = ps_tr[:].rearrange("p (c x) -> p c x", x=128)
                        dst = xlnF[:].rearrange("p (c x) -> p c x", x=512)[:, :, s * 128:(s + 1) * 128]
                        p.bump("ve", vector.tensor_copy(dst, src))
                    block.vector(evtr)
                    tr_free = p.at("ve")

                xlnF_ready = p.at("ve")
                for m in range(8):
                    bank = m % 2

                    def mmq(tensor, m=m, bank=bank, fv=psA_free_v[bank], fs=psA_free_s[bank],
                            xr=xlnF_ready):
                        p.wait(tensor, "ve", max(fv, xr))
                        p.wait(tensor, "se", fs)
                        for kc in range(8):
                            i = tensor.matmul(ps_a[bank][:],
                                              wqB[:, kc * 1024 + m * 128: kc * 1024 + (m + 1) * 128],
                                              xlnF[:, kc * 512:(kc + 1) * 512],
                                              start=(kc == 0), stop=(kc == 7))
                        p.bump("pe", i)
                    block.tensor(mmq)
                    pv = p.at("pe")

                    def evq(vector, m=m, bank=bank, pv=pv, pef=prev_pe):
                        p.wait(vector, "pe", max(pv, pef))
                        p.bump("ve", vector.tensor_copy(qF[:, m * 512:(m + 1) * 512],
                                                        ps_a[bank][:]))
                    block.vector(evq)
                    psA_free_v[bank] = p.at("ve")
                qF_ready = p.at("ve")

                for j in range(8):
                    for c4 in range(4):
                        cf = 4 * j + c4
                        bank = cf % 2

                        def mmqp(tensor, j=j, c4=c4, bank=bank, fv=psA_free_v[bank],
                                 fs=psA_free_s[bank], qr=qF_ready):
                            p.wait(tensor, "ve", max(fv, qr))
                            p.wait(tensor, "se", fs)
                            i = tensor.matmul(ps_a[bank][:],
                                              projp[:, j * 512 + c4 * 128: j * 512 + (c4 + 1) * 128],
                                              qF[:, j * 512:(j + 1) * 512],
                                              start=True, stop=True)
                            p.bump("pe", i)
                        block.tensor(mmqp)
                        pv = p.at("pe")

                        def vmin(vector, bank=bank, pv=pv):
                            p.wait(vector, "pe", pv)
                            p.bump("ve", vector.tensor_scalar_min(mbuf[:], ps_a[bank][:], 0.0))
                        block.vector(vmin)
                        v_m = p.at("ve")

                        def srel(scalar, bank=bank, pv=pv, v_m=v_m):
                            p.wait(scalar, "pe", pv)
                            p.wait(scalar, "ve", v_m)
                            scalar.activation(rbuf[:], ps_a[bank][:], AF.Relu)
                            i = scalar.activation(ebuf[:], mbuf[:], AF.Exp)
                            p.bump("se", i)
                        block.scalar(srel)
                        s_re = p.at("se")

                        def vadd(vector, cf=cf, s_re=s_re, pef=prev_pe):
                            p.wait(vector, "se", s_re)
                            p.wait(vector, "pe", pef)
                            i = vector.tensor_tensor(qpF[:, cf * 512:(cf + 1) * 512],
                                                     rbuf[:], ebuf[:], ALU.add)
                            p.bump("ve", i)
                        block.vector(vadd)
                        psA_free_v[bank] = v_m
                        psA_free_s[bank] = s_re
                qp_ready = p.at("ve")

                def mmden(tensor, qr=qp_ready, fa=den_free):
                    p.wait(tensor, "ve", max(qr, fa))
                    for c in range(32):
                        i = tensor.matmul(ps_den[:], ksblk[:, c * 16:(c + 1) * 16],
                                          qpF[:, c * 512:(c + 1) * 512],
                                          start=(c == 0), stop=(c == 31))
                    p.bump("pe", i)
                block.tensor(mmden)
                pv_den = p.at("pe")

                def vz(vector, pv=pv_den):
                    p.wait(vector, "pe", pv)
                    vector.tensor_scalar(zrow[:, :], ps_den[:, :], 1.0, float(ZEPS),
                                         ALU.mult, ALU.add)
                    i = vector.reciprocal(zrow[:, :], zrow[:, :])
                    p.bump("ve", i)
                block.vector(vz)
                v_z = p.at("ve")
                den_free = v_z

                for c2 in range(8):
                    bank = c2 % 2

                    def mmat(tensor, c2=c2, bank=bank, fa=o_free[bank], qr=qp_ready):
                        p.wait(tensor, "ve", max(fa, qr))
                        for hh in range(2):
                            h = 2 * c2 + hh
                            for cc2 in range(2):
                                c = 2 * h + cc2
                                i = tensor.matmul(ps_o[bank][hh * 64:(hh + 1) * 64, :],
                                                  kvT[:, c * 65: c * 65 + 64],
                                                  qpF[:, c * 512:(c + 1) * 512],
                                                  start=(cc2 == 0), stop=(cc2 == 1),
                                                  skip_group_check=True)
                        p.bump("pe", i)
                    block.tensor(mmat)
                    pv_at = p.at("pe")

                    def mmz2(tensor, c2=c2, v_z=v_z, fa=z2_free):
                        p.wait(tensor, "ve", max(v_z, fa))
                        i = tensor.matmul(ps_z2[:], e16[:, c2 * 128:(c2 + 1) * 128],
                                          zrow[:, :], start=True, stop=True)
                        p.bump("pe", i)
                    block.tensor(mmz2)
                    pv_z2 = p.at("pe")

                    def evz2(vector, pv=pv_z2):
                        p.wait(vector, "pe", pv)
                        p.bump("ve", vector.tensor_copy(z2s[:], ps_z2[:]))
                    block.vector(evz2)
                    v_z2 = p.at("ve")
                    z2_free = v_z2

                    def evat(vector, c2=c2, bank=bank, pv=pv_at, v_z2=v_z2, pef=prev_pe):
                        p.wait(vector, "pe", max(pv, pef))
                        p.wait(vector, "ve", v_z2)
                        i = vector.tensor_tensor(attF[:, c2 * 512:(c2 + 1) * 512],
                                                 ps_o[bank][:], z2s[:], ALU.mult)
                        p.bump("ve", i)
                    block.vector(evat)
                    o_free[bank] = p.at("ve")
                att_ready = p.at("ve")

                for s in range(4):
                    i128 = t512 * 4 + s
                    pv_o = [0, 0]
                    for nh2 in range(2):
                        def mmo(tensor, s=s, nh2=nh2, ar=att_ready, fa=o_free[nh2]):
                            p.wait(tensor, "ve", max(ar, fa))
                            for kc in range(8):
                                tensor.matmul(
                                    ps_o[nh2][:],
                                    attF[:, kc * 512 + s * 128: kc * 512 + (s + 1) * 128],
                                    woutB[:, kc * 1024 + nh2 * 512: kc * 1024 + (nh2 + 1) * 512],
                                    start=(kc == 0), stop=False)
                            i = tensor.matmul(ps_o[nh2][:], onesb[:, 0:128],
                                              boutr[:, nh2 * 512:(nh2 + 1) * 512],
                                              start=False, stop=True)
                            p.bump("pe", i)
                        block.tensor(mmo)
                        pv_o[nh2] = p.at("pe")

                    for nh2 in range(2):
                        def vres(vector, s=s, nh2=nh2, pv=pv_o[nh2], xg=x1_gate):
                            p.wait(vector, "pe", pv)
                            p.wait(vector, "sG", xg)
                            i = vector.tensor_tensor(x1t[:, nh2 * 512:(nh2 + 1) * 512],
                                                     xb4[s][:, nh2 * 512:(nh2 + 1) * 512],
                                                     ps_o[nh2][:], ALU.add)
                            p.bump("ve", i)
                        block.vector(vres)
                        o_free[nh2] = p.at("ve")
                    v_x1 = p.at("ve")
                    xb_free_v[s] = v_x1

                    def vst(vector, i128=i128):
                        p.bump("ve", vector.tensor_reduce(s1c[:, i128:i128 + 1], x1t[:],
                                                          AX.X, ALU.add))
                    block.vector(vst)

                    def sst(scalar, i128=i128, v_x1=v_x1):
                        p.wait(scalar, "ve", v_x1)
                        p.bump("se", scalar.activation(junk1k[:, :], x1t[:], AF.Square,
                                                       accum_out=s2c[:, i128:i128 + 1]))
                    block.scalar(sst)
                    s_st = p.at("se")

                    def gx1(g, i128=i128, s_st=s_st):
                        p.wait(g, "se", s_st)
                        p.bump("sG", g.dma_start(X1D.ap()[i128 * 128:(i128 + 1) * 128, :],
                                                 x1t[:]), 16)
                    block.gpsimd(gx1)
                    x1_gate = p.at("sG")
                prev_pe = p.at("pe")

            def _gap(vector):
                i = vector.nop()
            steps = [
                lambda v: v.tensor_scalar_mul(muC[:], s1c[:], 1.0 / 1024.0),
                lambda v: v.tensor_tensor(v1C[:], muC[:], muC[:], ALU.mult),
                lambda v: v.tensor_scalar_mul(u1C[:], s2c[:], 1.0 / 1024.0),
                lambda v: v.tensor_tensor(v1C[:], u1C[:], v1C[:], ALU.subtract),
                lambda v: v.tensor_scalar(u1C[:], v1C[:], 1.0, float(EPS), ALU.mult, ALU.add),
            ]
            first_se = p.at("se")
            for stp in steps:
                def vstep(vector, stp=stp, fa=p.at("ve"), fs=first_se):
                    p.wait(vector, "ve", fa)
                    p.wait(vector, "se", fs)
                    p.bump("ve", stp(vector))
                block.vector(vstep)
            vv = p.at("ve")

            def scC(scalar, vv=vv):
                p.wait(scalar, "ve", vv)
                scalar.activation(r1C[:], u1C[:], AF.Ln)
                i = scalar.activation(r1C[:], r1C[:], AF.Exp, scale=-0.5)
                p.bump("se", i)
            block.scalar(scC)
            sv = p.at("se")

            steps2 = [
                lambda v: v.reciprocal(u2C[:], u1C[:]),
                lambda v: v.tensor_tensor(u2C[:], v1C[:], u2C[:], ALU.mult),
                lambda v: v.tensor_scalar(u2C[:], u2C[:], 1.0, float(EPS), ALU.mult, ALU.add),
            ]
            for stp in steps2:
                def vstep2(vector, stp=stp, fa=p.at("ve"), sv=sv):
                    p.wait(vector, "ve", fa)
                    p.wait(vector, "se", sv)
                    p.bump("ve", stp(vector))
                block.vector(vstep2)
            vv2 = p.at("ve")

            def scC2(scalar, vv2=vv2):
                p.wait(scalar, "ve", vv2)
                scalar.activation(u2C[:], u2C[:], AF.Ln)
                i = scalar.activation(u2C[:], u2C[:], AF.Exp, scale=-0.5)
                p.bump("se", i)
            block.scalar(scC2)
            sv2 = p.at("se")

            steps3 = [
                lambda v: v.tensor_tensor(aC[:], r1C[:], u2C[:], ALU.mult),
                lambda v: v.tensor_scalar_mul(cC[:], muC[:], -1.0),
                lambda v: v.tensor_tensor(cC[:], cC[:], aC[:], ALU.mult),
            ]
            for stp in steps3:
                def vstep3(vector, stp=stp, fa=p.at("ve"), sv2=sv2):
                    p.wait(vector, "ve", fa)
                    p.wait(vector, "se", sv2)
                    p.bump("ve", stp(vector))
                block.vector(vstep3)
        st_pb.close()
        st_ab.close()

        if "x1" in taps:
            with nc.Block() as block:
                def _(scalar):
                    p.wait(scalar, "sG", p.at("sG"))
                    p.bump("sS", scalar.dma_start(tap_aps["t_x1"], X1D.ap()), 16)
                block.scalar(_)

        if stage < 3:
            with nc.Block() as block:
                def _(g):
                    p.wait(g, "sG", p.at("sG"))
                    p.wait(g, "sS", p.at("sS"))
                block.gpsimd(_)
            return nc, tap_aps

        st_w = ExitStack()
        w_ = st_w.enter_context
        w1gT = w_(nc.sbuf_tensor("w1gT", [128, 8 * 4096], BF16))
        w2gT = w_(nc.sbuf_tensor("w2gT", [128, 32 * 1024], BF16))
        c1row = w_(nc.sbuf_tensor("c1row", [1, HID], BF16))
        c2b = w_(nc.sbuf_tensor("c2b", [128, D], F32))

        st_pr2 = ExitStack()
        r2_ = st_pr2.enter_context
        wstg = [r2_(nc.sbuf_tensor(f"wstg2_{i}", [128, 2048], F32)) for i in range(2)]
        sgn1 = r2_(nc.sbuf_tensor("sgn1", [128, 2048], BF16))
        sgn2 = r2_(nc.sbuf_tensor("sgn2", [128, 2048], BF16))
        tq = r2_(nc.sbuf_tensor("tq", [128, 4096], BF16))
        tqT = r2_(nc.sbuf_tensor("tqT", [128, 1024], BF16))
        cb1row = r2_(nc.sbuf_tensor("cb1row", [1, HID], F32))
        cb2row = r2_(nc.sbuf_tensor("cb2row", [1, D], F32))
        ps_cb = r2_(nc.psum_tensor("ps_cb", [1, 512], F32))
        ps_oc = [r2_(nc.psum_tensor(f"ps_oc{i}", [128, 512], F32)) for i in range(2)]

        with nc.Block() as block:
            stg_free = [p.at("se"), p.at("se")]
            tq_free_pe = p.at("pe")
            for c in range(32):
                par = c % 2

                def load(sync, c=c, par=par, fa=stg_free[par]):
                    p.wait(sync, "se", fa)
                    p.bump("sL", sync.dma_start(wstg[par][:, 0:1024],
                                                W1[c * 128:(c + 1) * 128, :]), 16)
                block.sync(load)
                lv = p.at("sL")

                def sgn(scalar, par=par, lv=lv, fv=p.at("ve")):
                    p.wait(scalar, "sL", lv)
                    p.wait(scalar, "ve", fv)
                    scalar.activation(sgn1[:, 0:1024], wstg[par][:, 0:1024], AF.Sign,
                                      bias=nthr1[:, 0:1])
                    i = scalar.activation(sgn2[:, 0:1024], wstg[par][:, 0:1024], AF.Sign,
                                          bias=thr1[:, 0:1])
                    p.bump("se", i)
                block.scalar(sgn)
                sv = p.at("se")
                stg_free[par] = sv

                def vq(vector, sv=sv, tf=tq_free_pe):
                    p.wait(vector, "se", sv)
                    p.wait(vector, "pe", tf)
                    i = vector.tensor_tensor(tq[:, 0:1024], sgn1[:, 0:1024],
                                             sgn2[:, 0:1024], ALU.add)
                    p.bump("ve", i)
                block.vector(vq)
                v_tq = p.at("ve")

                def trq(tensor, v_tq=v_tq, fa=p.at("ve")):
                    p.wait(tensor, "ve", max(v_tq, fa))
                    for cd in range(8):
                        i = tensor.transpose(ps_tr[:, cd * 128:(cd + 1) * 128],
                                             tq[:, cd * 128:(cd + 1) * 128], identb[:])
                    p.bump("pe", i)
                block.tensor(trq)
                pv_tr = p.at("pe")
                tq_free_pe = pv_tr

                def evu(vector, pv=pv_tr):
                    p.wait(vector, "pe", pv)
                    i = vector.tensor_copy(tqT[:, 0:1024], ps_tr[:])
                    p.bump("ve", i)
                block.vector(evu)
                v_un = p.at("ve")

                def mcb(tensor, v_un=v_un):
                    p.wait(tensor, "ve", v_un)
                    for cd in range(8):
                        i = tensor.matmul(ps_cb[0:1, 0:128], b1lnB[:, cd:cd + 1],
                                          tqT[:, cd * 128:(cd + 1) * 128],
                                          start=(cd == 0), stop=(cd == 7))
                    p.bump("pe", i)
                block.tensor(mcb)
                pv_cb = p.at("pe")

                def evs(vector, c=c, pv_cb=pv_cb):
                    p.wait(vector, "pe", pv_cb)
                    vector.tensor_copy(cb1row[:, c * 128:(c + 1) * 128], ps_cb[0:1, 0:128])
                    for cd in range(8):
                        i = vector.tensor_scalar_mul(
                            w1gT[:, cd * 4096 + c * 128: cd * 4096 + (c + 1) * 128],
                            tqT[:, cd * 128:(cd + 1) * 128], g1F[:, cd:cd + 1])
                    p.bump("ve", i)
                block.vector(evs)

            for hh in range(2):
                def loadb(sync, hh=hh, fa=p.at("ve")):
                    p.wait(sync, "ve", fa)
                    p.bump("sL", sync.dma_start(wstg[0][0:1, 0:2048],
                                                B1R[0:1, hh * 2048:(hh + 1) * 2048]), 16)
                block.sync(loadb)
                lvb = p.at("sL")

                def vfin1(vector, hh=hh, lvb=lvb):
                    p.wait(vector, "sL", lvb)
                    vector.tensor_scalar_mul(wstg[0][0:1, 0:2048], wstg[0][0:1, 0:2048],
                                             im1[0:1, 0:1])
                    vector.tensor_tensor(wstg[0][0:1, 0:2048], wstg[0][0:1, 0:2048],
                                         cb1row[:, hh * 2048:(hh + 1) * 2048], ALU.add)
                    i = vector.tensor_copy(c1row[:, hh * 2048:(hh + 1) * 2048],
                                           wstg[0][0:1, 0:2048])
                    p.bump("ve", i)
                block.vector(vfin1)

            for c in range(8):
                for hc in range(2):
                    par = hc

                    def load(sync, c=c, hc=hc, par=par, fa=p.at("se")):
                        p.wait(sync, "se", fa)
                        p.wait(sync, "ve", p.at("ve"))
                        p.bump("sL", sync.dma_start(
                            wstg[par][:, 0:2048],
                            W2[c * 128:(c + 1) * 128, hc * 2048:(hc + 1) * 2048]), 16)
                    block.sync(load)
                    lv = p.at("sL")

                    def sgn(scalar, par=par, lv=lv, fv=p.at("ve")):
                        p.wait(scalar, "sL", lv)
                        p.wait(scalar, "ve", fv)
                        scalar.activation(sgn1[:, 0:2048], wstg[par][:, 0:2048], AF.Sign,
                                          bias=nthr2[:, 0:1])
                        i = scalar.activation(sgn2[:, 0:2048], wstg[par][:, 0:2048], AF.Sign,
                                              bias=thr2[:, 0:1])
                        p.bump("se", i)
                    block.scalar(sgn)
                    sv = p.at("se")

                    def vq(vector, hc=hc, sv=sv, tf=tq_free_pe):
                        p.wait(vector, "se", sv)
                        p.wait(vector, "pe", tf)
                        i = vector.tensor_tensor(tq[:, hc * 2048:(hc + 1) * 2048],
                                                 sgn1[:, 0:2048], sgn2[:, 0:2048], ALU.add)
                        p.bump("ve", i)
                    block.vector(vq)
                v_tq = p.at("ve")

                for g4 in range(4):
                    def trq(tensor, g4=g4, v_tq=v_tq, fa=p.at("ve")):
                        p.wait(tensor, "ve", max(v_tq, fa))
                        for k8 in range(8):
                            ch = g4 * 8 + k8
                            i = tensor.transpose(ps_tr[:, k8 * 128:(k8 + 1) * 128],
                                                 tq[:, ch * 128:(ch + 1) * 128], identb[:])
                        p.bump("pe", i)
                    block.tensor(trq)
                    pv_tr = p.at("pe")

                    def evu(vector, pv=pv_tr):
                        p.wait(vector, "pe", pv)
                        i = vector.tensor_copy(tqT[:, 0:1024], ps_tr[:])
                        p.bump("ve", i)
                    block.vector(evu)
                    v_un = p.at("ve")

                    def mcb(tensor, g4=g4, v_un=v_un):
                        p.wait(tensor, "ve", v_un)
                        for k8 in range(8):
                            ch = g4 * 8 + k8
                            i = tensor.matmul(ps_cb[0:1, 0:128], ln2bB[:, ch:ch + 1],
                                              tqT[:, k8 * 128:(k8 + 1) * 128],
                                              start=(ch == 0), stop=(ch == 31),
                                              skip_group_check=True)
                        p.bump("pe", i)
                    block.tensor(mcb)
                    pv_cb = p.at("pe")

                    def evs(vector, c=c, g4=g4, pv=pv_cb):
                        p.wait(vector, "pe", pv)
                        for k8 in range(8):
                            ch = g4 * 8 + k8
                            i = vector.tensor_scalar_mul(
                                w2gT[:, ch * 1024 + c * 128: ch * 1024 + (c + 1) * 128],
                                tqT[:, k8 * 128:(k8 + 1) * 128], g2F[:, ch:ch + 1])
                        p.bump("ve", i)
                    block.vector(evs)
                tq_free_pe = p.at("pe")

                def evcb2(vector, c=c, fa=p.at("pe")):
                    p.wait(vector, "pe", fa)
                    i = vector.tensor_copy(cb2row[:, c * 128:(c + 1) * 128], ps_cb[0:1, 0:128])
                    p.bump("ve", i)
                block.vector(evcb2)

            def loadb2(sync, fa=p.at("ve")):
                p.wait(sync, "ve", fa)
                p.bump("sL", sync.dma_start(wstg[0][0:1, 0:D], B2R), 16)
            block.sync(loadb2)
            lvb2 = p.at("sL")

            def vfin2(vector, lvb2=lvb2):
                p.wait(vector, "sL", lvb2)
                vector.tensor_scalar_mul(cb2row[:, 0:D], cb2row[:, 0:D], m2s[0:1, 0:1])
                vector.tensor_scalar_mul(cb2row[:, 0:D], cb2row[:, 0:D], 0.5)
                i = vector.tensor_tensor(cb2row[:, 0:D], cb2row[:, 0:D],
                                         wstg[0][0:1, 0:D], ALU.add)
                p.bump("ve", i)
            block.vector(vfin2)
            vv = p.at("ve")

            def mmc2(tensor, vv=vv):
                p.wait(tensor, "ve", vv)
                tensor.matmul(ps_oc[0][:], onesrow[:, :], cb2row[:, 0:512],
                              start=True, stop=True)
                i = tensor.matmul(ps_oc[1][:], onesrow[:, :], cb2row[:, 512:1024],
                                  start=True, stop=True)
                p.bump("pe", i)
            block.tensor(mmc2)
            pv = p.at("pe")

            def evc2(vector, pv=pv):
                p.wait(vector, "pe", pv)
                vector.tensor_copy(c2b[:, 0:512], ps_oc[0][:])
                i = vector.tensor_copy(c2b[:, 512:1024], ps_oc[1][:])
                p.bump("ve", i)
            block.vector(evc2)

            if "qc" in taps:
                def vm(vector):
                    vector.tensor_copy(junk1k[0:1, 0:1], m1s[:, :])
                    vector.tensor_copy(junk1k[0:1, 2:3], im1[:, :])
                    vector.tensor_copy(junk1k[0:1, 3:4], thr1[0:1, 0:1])
                    vector.tensor_copy(junk1k[0:1, 4:5], wstg[0][0:1, 5:6])
                    vector.tensor_copy(junk1k[0:1, 5:6], cb1row[0:1, 5:6])
                    i = vector.tensor_copy(junk1k[0:1, 1:2], m2s[:, :])
                    p.bump("ve", i)
                block.vector(vm)

                def tqc(g):
                    p.wait(g, "ve", p.at("ve"))
                    p.bump("sG", g.dma_start(tap_aps["t_c1"], c1row[:]), 16)
                    p.bump("sG", g.dma_start(tap_aps["t_c2"], c2b[:]), 16)
                    p.bump("sG", g.dma_start(tap_aps["t_w1"], w1gT[:, 0:4096]), 16)
                    p.bump("sG", g.dma_start(tap_aps["t_w2"], w2gT[:, 0:1024]), 16)
                    p.bump("sG", g.dma_start(tap_aps["t_m"], junk1k[0:1, 0:8]), 16)
                block.gpsimd(tqc)
        st_pr2.close()

        st_pc = ExitStack()
        pc = st_pc.enter_context
        ps_o = [pc(nc.psum_tensor(f"ps_op{i}", [128, 512], F32)) for i in range(2)]
        hg = pc(nc.sbuf_tensor("hg", [128, HID], F32))
        hn = pc(nc.sbuf_tensor("hn", [128, HID], BF16))
        hnT = pc(nc.sbuf_tensor("hnT", [128, 32 * 128], BF16))
        hgs = pc(nc.sbuf_tensor("hgs", [128, 8], F32))
        hgss = pc(nc.sbuf_tensor("hgss", [128, 8], F32))
        x1b = [pc(nc.sbuf_tensor(f"x1b{i}", [128, 1024], F32)) for i in range(2)]
        outb = pc(nc.sbuf_tensor("outb", [128, 1024], F32))
        r2k2 = pc(nc.sbuf_tensor("r2k2", [128, 1], F32))
        xlnC = pc(nc.sbuf_tensor("xlnC", [128, 1024], BF16))
        xlnCF = pc(nc.sbuf_tensor("xlnCF", [128, 8 * 128], BF16))

        with nc.Block() as block:
            x1_free_v = [p.at("ve")] * 2
            x1_free_s = [p.at("se")] * 2
            psA_free = [p.at("se")] * 2
            o_free = [p.at("ve")] * 2
            out_gate = p.at("sG")
            hn_free = p.at("pe")
            xlnC_free = p.at("pe")
            x1_store_all = p.at("sG")
            for i in range(NT128):
                par = i % 2

                def loadx1(sync, i=i, par=par, fv=x1_free_v[par], fs=x1_free_s[par],
                           xs=x1_store_all):
                    p.wait(sync, "sG", xs)
                    p.wait(sync, "ve", fv)
                    p.wait(sync, "se", fs)
                    p.bump("sL", sync.dma_start(x1b[par][:],
                                                X1D.ap()[i * 128:(i + 1) * 128, :]), 16)
                block.sync(loadx1)
                lv = p.at("sL")

                def sap1(scalar, i=i, par=par, lv=lv, fa=xlnC_free):
                    p.wait(scalar, "sL", lv)
                    p.wait(scalar, "pe", fa)
                    j = scalar.activation(xlnC[:], x1b[par][:], AF.Identity,
                                          bias=cC[:, i:i + 1], scale=aC[:, i:i + 1])
                    p.bump("se", j)
                block.scalar(sap1)
                s_ap = p.at("se")
                x1_free_s[par] = s_ap

                def trC(tensor, s_ap=s_ap, fa=p.at("ve")):
                    p.wait(tensor, "se", s_ap)
                    p.wait(tensor, "ve", fa)
                    for c in range(8):
                        j = tensor.transpose(ps_tr[:, c * 128:(c + 1) * 128],
                                             xlnC[:, c * 128:(c + 1) * 128], identb[:])
                    p.bump("pe", j)
                block.tensor(trC)
                pe_tr = p.at("pe")
                xlnC_free = pe_tr

                def evtrC(vector, pe_tr=pe_tr):
                    p.wait(vector, "pe", pe_tr)
                    p.bump("ve", vector.tensor_copy(xlnCF[:, 0:1024], ps_tr[:]))
                block.vector(evtrC)
                v_xf = p.at("ve")

                for n in range(8):
                    bank = n % 2

                    def mm1(tensor, n=n, bank=bank, v_xf=v_xf, fa=psA_free[bank]):
                        p.wait(tensor, "ve", v_xf)
                        p.wait(tensor, "se", fa)
                        for kc in range(8):
                            tensor.matmul(
                                ps_a[bank][:],
                                xlnCF[:, kc * 128:(kc + 1) * 128],
                                w1gT[:, kc * 4096 + n * 512: kc * 4096 + (n + 1) * 512],
                                start=(kc == 0), stop=False)
                        j = tensor.matmul(ps_a[bank][:], onesb[:, 0:128],
                                          c1row[:, n * 512:(n + 1) * 512],
                                          start=False, stop=True)
                        p.bump("pe", j)
                    block.tensor(mm1)
                    pv = p.at("pe")

                    def sg(scalar, n=n, bank=bank, pv=pv):
                        p.wait(scalar, "pe", pv)
                        j = scalar.activation(hg[:, n * 512:(n + 1) * 512], ps_a[bank][:],
                                              AF.Gelu, scale=k1b[:, 0:1],
                                              accum_out=hgs[:, n:n + 1])
                        p.bump("se", j)
                    block.scalar(sg)
                    psA_free[bank] = p.at("se")
                s_gelu = p.at("se")

                def vss(vector, s_gelu=s_gelu):
                    p.wait(vector, "se", s_gelu)
                    j = None
                    for n in range(8):
                        j = vector.scalar_tensor_tensor(junk1k[:, 0:512],
                                                        hg[:, n * 512:(n + 1) * 512], 1.0,
                                                        hg[:, n * 512:(n + 1) * 512],
                                                        ALU.mult, ALU.mult,
                                                        accum_out=hgss[:, n:n + 1])
                    p.bump("ve", j)
                block.vector(vss)
                v_acc = p.at("ve")

                def vss2(vector, v_acc=v_acc):
                    p.wait(vector, "ve", v_acc)
                    vector.tensor_reduce(s1t[:, 0:1], hgs[:], AX.X, ALU.add)
                    j = vector.tensor_reduce(s2t[:, 0:1], hgss[:], AX.X, ALU.add)
                    p.bump("ve", j)
                block.vector(vss2)

                def vss3(vector, fa=p.at("ve")):
                    p.wait(vector, "ve", fa)
                    vector.tensor_scalar_mul(mut[:, 0:1], s1t[:, 0:1], 1.0 / HID)
                    vector.tensor_tensor(tsc[:, 0:1], s1t[:, 0:1], s1t[:, 0:1], ALU.mult)
                    j = vector.tensor_scalar_mul(vart[:, 0:1], s2t[:, 0:1], 1.0 / HID)
                    p.bump("ve", j)
                block.vector(vss3)

                def vss4(vector, fa=p.at("ve")):
                    p.wait(vector, "ve", fa)
                    j = vector.scalar_tensor_tensor(tsc[:, 0:1], tsc[:, 0:1],
                                                    1.0 / (float(HID) * HID),
                                                    vart[:, 0:1], ALU.mult, ALU.subtract)
                    p.bump("ve", j)
                block.vector(vss4)

                def vss5(vector, fa=p.at("ve")):
                    p.wait(vector, "ve", fa)
                    vector.tensor_scalar(vart[:, 0:1], tsc[:, 0:1], -1.0, float(EPS),
                                         ALU.mult, ALU.add)
                    j = vector.tensor_scalar_mul(bt[:, 0:1], mut[:, 0:1], -1.0)
                    p.bump("ve", j)
                block.vector(vss5)
                v_st2 = p.at("ve")

                def sap2(scalar, v_st2=v_st2, hf=hn_free):
                    p.wait(scalar, "ve", v_st2)
                    p.wait(scalar, "pe", hf)
                    j = scalar.activation(hn[:], hg[:], AF.Identity, bias=bt[:, 0:1])
                    p.bump("se", j)
                block.scalar(sap2)
                s_hn = p.at("se")

                def sr2(scalar):
                    scalar.activation(rt[:, 0:1], vart[:, 0:1], AF.Ln)
                    j = scalar.activation(rt[:, 0:1], rt[:, 0:1], AF.Exp, scale=-0.5)
                    p.bump("se", j)
                block.scalar(sr2)
                s_r2 = p.at("se")

                def vr2(vector, s_r2=s_r2):
                    p.wait(vector, "se", s_r2)
                    j = vector.tensor_tensor(r2k2[:, 0:1], rt[:, 0:1], k2b[:, 0:1], ALU.mult)
                    p.bump("ve", j)
                block.vector(vr2)
                v_r2 = p.at("ve")

                for g4 in range(4):
                    pst = ps_tr if g4 % 2 == 0 else ps_tr2

                    def trh(tensor, g4=g4, pst=pst, s_hn=s_hn, fa=p.at("ve")):
                        p.wait(tensor, "se", s_hn)
                        p.wait(tensor, "ve", fa)
                        for k8 in range(8):
                            ch = g4 * 8 + k8
                            j = tensor.transpose(pst[:, k8 * 128:(k8 + 1) * 128],
                                                 hn[:, ch * 128:(ch + 1) * 128], identb[:])
                        p.bump("pe", j)
                    block.tensor(trh)
                    pvh = p.at("pe")

                    def evh(vector, g4=g4, pst=pst, pvh=pvh):
                        p.wait(vector, "pe", pvh)
                        p.bump("ve", vector.tensor_copy(hnT[:, g4 * 1024:(g4 + 1) * 1024],
                                                        pst[:]))
                    block.vector(evh)
                hnT_ready = p.at("ve")
                hn_free = p.at("pe")

                pv_o = [0, 0]
                for nh2 in range(2):
                    def mm2(tensor, nh2=nh2, hr=hnT_ready, fa=o_free[nh2]):
                        p.wait(tensor, "ve", max(hr, fa))
                        for ch in range(32):
                            j = tensor.matmul(
                                ps_o[nh2][:],
                                hnT[:, ch * 128:(ch + 1) * 128],
                                w2gT[:, ch * 1024 + nh2 * 512: ch * 1024 + (nh2 + 1) * 512],
                                start=(ch == 0), stop=(ch == 31))
                        p.bump("pe", j)
                    block.tensor(mm2)
                    pv_o[nh2] = p.at("pe")

                for nh2 in range(2):
                    def vep(vector, par=par, nh2=nh2, pv=pv_o[nh2], v_r2=v_r2, og=out_gate):
                        p.wait(vector, "pe", pv)
                        p.wait(vector, "ve", v_r2)
                        p.wait(vector, "sG", og)
                        vector.scalar_tensor_tensor(outb[:, nh2 * 512:(nh2 + 1) * 512],
                                                    ps_o[nh2][:], r2k2[:, 0:1],
                                                    x1b[par][:, nh2 * 512:(nh2 + 1) * 512],
                                                    ALU.mult, ALU.add)
                        j = vector.tensor_tensor(outb[:, nh2 * 512:(nh2 + 1) * 512],
                                                 outb[:, nh2 * 512:(nh2 + 1) * 512],
                                                 c2b[:, nh2 * 512:(nh2 + 1) * 512], ALU.add)
                        p.bump("ve", j)
                    block.vector(vep)
                    o_free[nh2] = p.at("ve")
                v_out = p.at("ve")
                x1_free_v[par] = v_out

                def gout(g, i=i, v_out=v_out):
                    p.wait(g, "ve", v_out)
                    p.bump("sG", g.dma_start(OUT[i * 128:(i + 1) * 128, :], outb[:]), 16)
                block.gpsimd(gout)
                out_gate = p.at("sG")
            if "pc" in taps:
                def tpc(g):
                    p.wait(g, "ve", p.at("ve"))
                    p.wait(g, "se", p.at("se"))
                    p.bump("sG", g.dma_start(tap_aps["t_xlnC"], xlnC[:]), 16)
                    p.bump("sG", g.dma_start(tap_aps["t_hg"], hg[:]), 16)
                    p.bump("sG", g.dma_start(tap_aps["t_hn"], hn[:]), 16)
                    p.bump("sG", g.dma_start(tap_aps["t_ac"][:, 0:NT128], aC[:]), 16)
                    p.bump("sG", g.dma_start(tap_aps["t_ac"][:, NT128:2 * NT128], cC[:]), 16)
                    p.bump("sG", g.dma_start(tap_aps["t_r2"], r2k2[:]), 16)
                    p.bump("sG", g.dma_start(tap_aps["t_hgs"], hgs[:]), 16)
                    p.bump("sG", g.dma_start(tap_aps["t_hgss"], hgss[:]), 16)
                block.gpsimd(tpc)
        st_pc.close()
        st_w.close()

        with nc.Block() as block:
            def _(g):
                p.wait(g, "sG", p.at("sG"))
            block.gpsimd(_)

    return nc, tap_aps


def _layout_inputs(L0_attn_x, L0_attn_w_qkv, L0_attn_proj_matrix, L0_attn_w_out,
                   L0_attn_b_out, L0_ffn_ln1_g, L0_ffn_ln1_b, L0_ffn_w1, L0_ffn_b1,
                   L0_ffn_ln2_g, L0_ffn_ln2_b, L0_ffn_w2, L0_ffn_b2):
    import ml_dtypes
    f32 = np.float32
    x = np.asarray(L0_attn_x, f32)
    wqkv = np.asarray(L0_attn_w_qkv, f32)
    proj = np.asarray(L0_attn_proj_matrix, f32)

    projp = np.zeros((8, 128, 512), f32)
    for j in range(8):
        projp[j, 0:64, 0:256] = proj[2 * j]
        projp[j, 64:128, 256:512] = proj[2 * j + 1]

    e16 = np.zeros((16, D), f32)
    for h in range(NH):
        e16[h, h * 64:(h + 1) * 64] = 1.0

    common = {
        "wq": np.ascontiguousarray(wqkv[:, 0:1024]),
        "wk": np.ascontiguousarray(wqkv[:, 1024:2048]),
        "wv": np.ascontiguousarray(wqkv[:, 2048:3072]),
        "projp": projp,
        "wout": np.ascontiguousarray(np.asarray(L0_attn_w_out, f32)),
        "boutrow": np.asarray(L0_attn_b_out, f32).reshape(1, D),
        "w1": np.ascontiguousarray(np.asarray(L0_ffn_w1, f32)),
        "w2": np.ascontiguousarray(np.asarray(L0_ffn_w2, f32)),
        "b1row": np.asarray(L0_ffn_b1, f32).reshape(1, HID),
        "b2row": np.asarray(L0_ffn_b2, f32).reshape(1, D),
        "g1F": np.ascontiguousarray(np.asarray(L0_ffn_ln1_g, f32).reshape(8, 128).T),
        "b1lnF": np.ascontiguousarray(np.asarray(L0_ffn_ln1_b, f32).reshape(8, 128).T),
        "g2F": np.ascontiguousarray(np.asarray(L0_ffn_ln2_g, f32).reshape(32, 128).T),
        "ln2bF": np.ascontiguousarray(np.asarray(L0_ffn_ln2_b, f32).reshape(32, 128).T),
        "identb": np.eye(128, dtype=f32).astype(ml_dtypes.bfloat16),
        "onesb": np.ones((1, 128), f32).astype(ml_dtypes.bfloat16),
        "onesf": np.ones((128, 1), f32),
        "onesrowf": np.ones((1, 128), f32),
        "e16": e16,
    }
    in_maps = []
    for core in range(8):
        b = core // 4
        r0 = (core % 4) * T
        m = dict(common)
        m["x"] = np.ascontiguousarray(x[b, r0:r0 + T, :])
        in_maps.append(m)
    return in_maps


_CACHED = {}


def kernel(**inputs):
    if "prog" not in _CACHED:
        _CACHED["prog"] = build_program()
    nc, _ = _CACHED["prog"]
    in_maps = _layout_inputs(**inputs)
    res = run_bass_kernel_spmd(nc, in_maps, core_ids=list(range(8)))
    out = np.empty((2, 8192, D), np.float32)
    for core in range(8):
        b = core // 4
        r0 = (core % 4) * T
        out[b, r0:r0 + T, :] = res.results[core]["out"]
    return out


# version 02
# speedup vs baseline: 1.0075x; 1.0075x over previous
import numpy as np
import concourse.bass as bass
import concourse.mybir as mybir
from concourse.bass_utils import run_bass_kernel_spmd
from contextlib import ExitStack

F32 = mybir.dt.float32
BF16 = mybir.dt.bfloat16
AF = mybir.ActivationFunctionType
ALU = mybir.AluOpType
AX = mybir.AxisListType

T = 2048
D = 1024
HID = 4096
NH = 16
FT = 256
EPS = 1e-5
ZEPS = 1e-8
NT128 = T // 128
NT512 = T // 512


class Prog:
    def __init__(self, nc, stack):
        self.nc = nc
        self._sems = {}
        self.cnt = {}
        self.stack = stack

    def sem(self, name):
        if name not in self._sems:
            self._sems[name] = self.stack.enter_context(self.nc.semaphore(name))
            self.cnt[name] = 0
        return self._sems[name]

    def bump(self, name, inst, n=1):
        inst.then_inc(self.sem(name), n)
        self.cnt[name] += n
        return self.cnt[name]

    def at(self, name):
        self.sem(name)
        return self.cnt[name]

    def wait(self, eng, name, v=None):
        if v is None:
            v = self.cnt[name]
        if v > 0:
            eng.wait_ge(self.sem(name), v)


def build_program(taps=(), stage=99):
    nc = bass.Bass("TRN2", target_bir_lowering=False, detect_race_conditions=False)
    taps = set(taps)

    X = nc.dram_tensor("x", [T, D], F32, kind="ExternalInput").ap()
    WQ = nc.dram_tensor("wq", [D, D], F32, kind="ExternalInput").ap()
    WK = nc.dram_tensor("wk", [D, D], F32, kind="ExternalInput").ap()
    WV = nc.dram_tensor("wv", [D, D], F32, kind="ExternalInput").ap()
    PROJP = nc.dram_tensor("projp", [8, 128, 512], F32, kind="ExternalInput").ap()
    WOUT = nc.dram_tensor("wout", [D, D], F32, kind="ExternalInput").ap()
    BOUTR = nc.dram_tensor("boutrow", [1, D], F32, kind="ExternalInput").ap()
    W1 = nc.dram_tensor("w1", [HID, D], F32, kind="ExternalInput").ap()
    W2 = nc.dram_tensor("w2", [D, HID], F32, kind="ExternalInput").ap()
    B1R = nc.dram_tensor("b1row", [1, HID], F32, kind="ExternalInput").ap()
    B2R = nc.dram_tensor("b2row", [1, D], F32, kind="ExternalInput").ap()
    G1F = nc.dram_tensor("g1F", [128, 8], F32, kind="ExternalInput").ap()
    B1LNF = nc.dram_tensor("b1lnF", [128, 8], F32, kind="ExternalInput").ap()
    G2F = nc.dram_tensor("g2F", [128, 32], F32, kind="ExternalInput").ap()
    LN2BF = nc.dram_tensor("ln2bF", [128, 32], F32, kind="ExternalInput").ap()
    IDENTB = nc.dram_tensor("identb", [128, 128], BF16, kind="ExternalInput").ap()
    ONESB = nc.dram_tensor("onesb", [1, 128], BF16, kind="ExternalInput").ap()
    ONESF = nc.dram_tensor("onesf", [128, 1], F32, kind="ExternalInput").ap()
    ONESROWF = nc.dram_tensor("onesrowf", [1, 128], F32, kind="ExternalInput").ap()
    E16 = nc.dram_tensor("e16", [16, D], F32, kind="ExternalInput").ap()
    OUT = nc.dram_tensor("out", [T, D], F32, kind="ExternalOutput").ap()

    X1D = nc.dram_tensor("x1d", [T, D], F32)
    cc_in = nc.dram_tensor("cc_in", [65, NH * FT], F32)
    cc_out = nc.dram_tensor("cc_out", [65, NH * FT], F32)

    tap_aps = {}
    if "kv" in taps:
        tap_aps["t_kv"] = nc.dram_tensor("t_kv", [65, NH * FT], F32, kind="ExternalOutput").ap()
    if "kvloc" in taps:
        tap_aps["t_kvloc"] = nc.dram_tensor("t_kvloc", [65, NH * FT], F32, kind="ExternalOutput").ap()
    if "x1" in taps:
        tap_aps["t_x1"] = nc.dram_tensor("t_x1", [T, D], F32, kind="ExternalOutput").ap()
    if "kp" in taps:
        tap_aps["t_xt"] = nc.dram_tensor("t_xt", [128, 1024], BF16, kind="ExternalOutput").ap()
        tap_aps["t_st"] = nc.dram_tensor("t_st", [128, 8], F32, kind="ExternalOutput").ap()
        tap_aps["t_kp0"] = nc.dram_tensor("t_kp0", [128, NH * FT], BF16, kind="ExternalOutput").ap()
        tap_aps["t_kp1"] = nc.dram_tensor("t_kp1", [128, NH * FT], BF16, kind="ExternalOutput").ap()
        tap_aps["t_kF"] = nc.dram_tensor("t_kF", [128, 8 * 512], BF16, kind="ExternalOutput").ap()
        tap_aps["t_va0"] = nc.dram_tensor("t_va0", [128, NH * 66], BF16, kind="ExternalOutput").ap()
        tap_aps["t_va1"] = nc.dram_tensor("t_va1", [128, NH * 66], BF16, kind="ExternalOutput").ap()
        tap_aps["t_xf"] = nc.dram_tensor("t_xf", [128, 8 * 512], BF16, kind="ExternalOutput").ap()
    if "pc" in taps:
        tap_aps["t_xlnC"] = nc.dram_tensor("t_xlnC", [128, D], BF16, kind="ExternalOutput").ap()
        tap_aps["t_hg"] = nc.dram_tensor("t_hg", [128, HID], F32, kind="ExternalOutput").ap()
        tap_aps["t_hn"] = nc.dram_tensor("t_hn", [128, HID], BF16, kind="ExternalOutput").ap()
        tap_aps["t_ac"] = nc.dram_tensor("t_ac", [128, 2 * NT128], F32, kind="ExternalOutput").ap()
        tap_aps["t_r2"] = nc.dram_tensor("t_r2", [128, 1], F32, kind="ExternalOutput").ap()
        tap_aps["t_hgs"] = nc.dram_tensor("t_hgs", [128, 8], F32, kind="ExternalOutput").ap()
        tap_aps["t_hgss"] = nc.dram_tensor("t_hgss", [128, 8], F32, kind="ExternalOutput").ap()
    if "qc" in taps:
        tap_aps["t_m"] = nc.dram_tensor("t_m", [1, 8], F32, kind="ExternalOutput").ap()
        tap_aps["t_c1"] = nc.dram_tensor("t_c1", [1, HID], BF16, kind="ExternalOutput").ap()
        tap_aps["t_c2"] = nc.dram_tensor("t_c2", [128, D], F32, kind="ExternalOutput").ap()
        tap_aps["t_w1"] = nc.dram_tensor("t_w1", [128, HID], BF16, kind="ExternalOutput").ap()
        tap_aps["t_w2"] = nc.dram_tensor("t_w2", [128, D], BF16, kind="ExternalOutput").ap()

    st = ExitStack()
    with st:
        en = st.enter_context
        p = Prog(nc, st)

        identb = en(nc.sbuf_tensor("identb_s", [128, 128], BF16))
        onesb = en(nc.sbuf_tensor("onesb_s", [1, 128], BF16))
        onescol = en(nc.sbuf_tensor("onescol_s", [128, 1], F32))
        onesrow = en(nc.sbuf_tensor("onesrow_s", [1, 128], F32))
        g1F = en(nc.sbuf_tensor("g1F_s", [128, 8], F32))
        g2F = en(nc.sbuf_tensor("g2F_s", [128, 32], F32))
        b1lnB = en(nc.sbuf_tensor("b1lnB_s", [128, 8], BF16))
        ln2bB = en(nc.sbuf_tensor("ln2bB_s", [128, 32], BF16))
        thr1 = en(nc.sbuf_tensor("thr1", [128, 1], F32))
        nthr1 = en(nc.sbuf_tensor("nthr1", [128, 1], F32))
        thr2 = en(nc.sbuf_tensor("thr2", [128, 1], F32))
        nthr2 = en(nc.sbuf_tensor("nthr2", [128, 1], F32))
        k1b = en(nc.sbuf_tensor("k1b", [128, 1], F32))
        k2b = en(nc.sbuf_tensor("k2b", [128, 1], F32))
        m1s = en(nc.sbuf_tensor("m1s", [1, 1], F32))
        m2s = en(nc.sbuf_tensor("m2s", [1, 1], F32))
        im1 = en(nc.sbuf_tensor("im1", [1, 1], F32))
        aC = en(nc.sbuf_tensor("aC", [128, NT128], F32))
        cC = en(nc.sbuf_tensor("cC", [128, NT128], F32))
        junk1k = en(nc.sbuf_tensor("junk1k", [128, 1024], F32))
        s1t = en(nc.sbuf_tensor("s1t", [128, 1], F32))
        s2t = en(nc.sbuf_tensor("s2t", [128, 1], F32))
        mut = en(nc.sbuf_tensor("mut", [128, 1], F32))
        vart = en(nc.sbuf_tensor("vart", [128, 1], F32))
        rt = en(nc.sbuf_tensor("rt", [128, 1], F32))
        bt = en(nc.sbuf_tensor("bt", [128, 1], F32))
        tsc = en(nc.sbuf_tensor("tsc", [128, 1], F32))

        ps_a = [en(nc.psum_tensor(f"ps_a{i}", [128, 512], F32)) for i in range(2)]
        ps_tr = en(nc.psum_tensor("ps_tr", [128, 1024], BF16))
        ps_tr2 = en(nc.psum_tensor("ps_tr2", [128, 1024], BF16))

        st_ab = ExitStack()
        ab = st_ab.enter_context
        e16 = ab(nc.sbuf_tensor("e16_s", [16, D], F32))
        projp = ab(nc.sbuf_tensor("projp_s", [128, 8 * 512], BF16))
        woutB = ab(nc.sbuf_tensor("wout_s", [128, 8 * 1024], BF16))
        wqB = ab(nc.sbuf_tensor("wq_s", [128, 8 * 1024], BF16))
        boutr = ab(nc.sbuf_tensor("boutr_s", [1, D], BF16))
        kvT = ab(nc.sbuf_tensor("kvT", [128, 32 * 65], BF16))
        ksblk = ab(nc.sbuf_tensor("ksblk", [128, 32 * 16], BF16))
        xlnT = [ab(nc.sbuf_tensor(f"xlnT{i}", [128, 1024], BF16)) for i in range(2)]
        xlnF = ab(nc.sbuf_tensor("xlnF", [128, 8 * 512], BF16))
        ebuf = ab(nc.sbuf_tensor("ebuf", [128, 512], F32))
        mbuf = ab(nc.sbuf_tensor("mbuf", [128, 512], F32))
        rbuf = ab(nc.sbuf_tensor("rbuf", [128, 512], F32))

        def emit_ln_chain(block, v_s1, s_s2, n=1024.0):
            def vc1(vector):
                p.wait(vector, "se", s_s2)
                p.wait(vector, "ve", v_s1)
                vector.tensor_scalar_mul(mut[:, 0:1], s1t[:, 0:1], 1.0 / n)
                vector.tensor_tensor(tsc[:, 0:1], s1t[:, 0:1], s1t[:, 0:1], ALU.mult)
                i = vector.tensor_scalar_mul(vart[:, 0:1], s2t[:, 0:1], 1.0 / n)
                p.bump("ve", i)
            block.vector(vc1)

            def vc2(vector, fa=p.at("ve")):
                p.wait(vector, "ve", fa)
                i = vector.scalar_tensor_tensor(tsc[:, 0:1], tsc[:, 0:1], 1.0 / (n * n),
                                                vart[:, 0:1], ALU.mult, ALU.subtract)
                p.bump("ve", i)
            block.vector(vc2)

            def vc3(vector, fa=p.at("ve")):
                p.wait(vector, "ve", fa)
                i = vector.tensor_scalar(vart[:, 0:1], tsc[:, 0:1], -1.0, float(EPS),
                                         ALU.mult, ALU.add)
                p.bump("ve", i)
            block.vector(vc3)
            vv = p.at("ve")

            def sc(scalar):
                p.wait(scalar, "ve", vv)
                scalar.activation(rt[:, 0:1], vart[:, 0:1], AF.Ln)
                i = scalar.activation(rt[:, 0:1], rt[:, 0:1], AF.Exp, scale=-0.5)
                p.bump("se", i)
            block.scalar(sc)
            sv = p.at("se")

            def vb(vector):
                p.wait(vector, "se", sv)
                i = vector.tensor_scalar(bt[:, 0:1], mut[:, 0:1], -1.0, rt[:, 0:1],
                                         ALU.mult, ALU.mult)
                p.bump("ve", i)
            block.vector(vb)
            return p.at("ve")

        st_a = ExitStack()
        a_ = st_a.enter_context
        wkB = a_(nc.sbuf_tensor("wk_s", [128, 8 * 1024], BF16))
        wvB = a_(nc.sbuf_tensor("wv_s", [128, 8 * 1024], BF16))
        kv_sb = a_(nc.sbuf_tensor("kv_sb", [65, NH * FT], F32))
        kF = a_(nc.sbuf_tensor("kF", [128, 8 * 512], BF16))
        vaug = [a_(nc.sbuf_tensor(f"vaug{s}", [128, NH * 66], BF16)) for s in range(2)]
        kpT = [a_(nc.sbuf_tensor(f"kpT{s}", [128, NH * FT], BF16)) for s in range(2)]
        xbuf = [a_(nc.sbuf_tensor(f"xbuf{i}", [128, 1024], F32)) for i in range(2)]

        st_p0 = ExitStack()
        stg0 = [st_p0.enter_context(nc.sbuf_tensor(f"stg0_{i}", [128, 1024], F32))
                for i in range(2)]
        with nc.Block() as block:
            def _(sync):
                for ap_d, ap_s in [(identb[:], IDENTB), (onesb[:], ONESB),
                                   (onescol[:], ONESF), (onesrow[:], ONESROWF),
                                   (e16[:], E16), (g1F[:], G1F), (g2F[:], G2F)]:
                    p.bump("sL", sync.dma_start(ap_d, ap_s), 16)
                p.bump("sL", sync.dma_start(junk1k[0:128, 0:8], B1LNF), 16)
                p.bump("sL", sync.dma_start(junk1k[0:128, 8:40], LN2BF), 16)
                p.bump("sL", sync.dma_start(stg0[0][0:1, 0:D], BOUTR), 16)
            block.sync(_)
            lv = p.at("sL")

            def _(vector):
                p.wait(vector, "sL", lv)
                vector.tensor_copy(b1lnB[:], junk1k[:, 0:8])
                vector.tensor_copy(ln2bB[:], junk1k[:, 8:40])
                i = vector.tensor_copy(boutr[:], stg0[0][0:1, 0:D])
                p.bump("ve", i)
            block.vector(_)

        with nc.Block() as block:
            seq = ([("wq", c, WQ, wqB, 1024) for c in range(8)] +
                   [("wk", c, WK, wkB, 1024) for c in range(8)] +
                   [("wv", c, WV, wvB, 1024) for c in range(8)] +
                   [("wo", c, WOUT, woutB, 1024) for c in range(8)] +
                   [("pp", c, None, projp, 512) for c in range(8)])
            free_at = [p.at("ve"), p.at("ve")]
            for idx, (kind, c, src, dst, ncol) in enumerate(seq):
                par = idx % 2
                buf = stg0[par]
                src_ap = PROJP[c] if kind == "pp" else src[c * 128:(c + 1) * 128, :]

                def load(sync, buf=buf, src_ap=src_ap, ncol=ncol, fa=free_at[par]):
                    p.wait(sync, "ve", fa)
                    p.bump("sL", sync.dma_start(buf[:, 0:ncol], src_ap), 16)
                block.sync(load)
                lv = p.at("sL")

                def cast(vector, buf=buf, dst=dst, c=c, ncol=ncol, lv=lv):
                    p.wait(vector, "sL", lv)
                    p.bump("ve", vector.tensor_copy(dst[:, c * ncol:(c + 1) * ncol],
                                                    buf[:, 0:ncol]))
                block.vector(cast)
                free_at[par] = p.at("ve")
        st_p0.close()

        st_pa = ExitStack()
        ps_kv = [st_pa.enter_context(nc.psum_tensor(f"ps_kv{i}", [65, 512], F32))
                 for i in range(4)]
        with nc.Block() as block:
            def _(vector):
                for s in range(2):
                    va = vaug[s][:].rearrange("p (h c) -> p h c", c=66)
                    vector.memset(va[:, :, 64:66], 1.0)
                i = vector.memset(ksblk[:], 0.0)
                p.bump("ve", i)
            block.vector(_)

            x_free_v = [p.at("ve")] * 2
            x_free_s = [p.at("se")] * 2
            psA_free_v = [p.at("ve")] * 2
            psA_free_s = [p.at("se")] * 2
            tr_free = p.at("ve")
            kv_free = [p.at("ve")] * 4
            prev_pe = p.at("pe")
            for t512 in range(NT512):
                for s in range(4):
                    i128 = t512 * 4 + s
                    par = i128 % 2

                    def load_x(sync, i128=i128, par=par, fv=x_free_v[par], fs=x_free_s[par]):
                        p.wait(sync, "ve", fv)
                        p.wait(sync, "se", fs)
                        p.bump("sL", sync.dma_start(xbuf[par][:],
                                                    X[i128 * 128:(i128 + 1) * 128, :]), 16)
                    block.sync(load_x)
                    lv = p.at("sL")

                    def vs(vector, par=par, lv=lv):
                        p.wait(vector, "sL", lv)
                        p.bump("ve", vector.tensor_reduce(s1t[:, 0:1], xbuf[par][:],
                                                          AX.X, ALU.add))
                    block.vector(vs)
                    v_s1 = p.at("ve")

                    def ss(scalar, par=par, lv=lv):
                        p.wait(scalar, "sL", lv)
                        p.bump("se", scalar.activation(junk1k[:, :], xbuf[par][:], AF.Square,
                                                       accum_out=s2t[:, 0:1]))
                    block.scalar(ss)
                    s_s2 = p.at("se")
                    v_bt = emit_ln_chain(block, v_s1, s_s2)

                    def sa(scalar, par=par, v_bt=v_bt, pef=prev_pe):
                        p.wait(scalar, "ve", v_bt)
                        p.wait(scalar, "pe", pef)
                        i = scalar.activation(xlnT[par][:], xbuf[par][:], AF.Identity,
                                              bias=bt[:, 0:1], scale=rt[:, 0:1])
                        p.bump("se", i)
                    block.scalar(sa)
                    s_ap = p.at("se")
                    x_free_v[par] = v_s1
                    x_free_s[par] = s_ap

                    def trs(tensor, par=par, s_ap=s_ap, fa=tr_free):
                        p.wait(tensor, "se", s_ap)
                        p.wait(tensor, "ve", fa)
                        for c in range(8):
                            i = tensor.transpose(ps_tr[:, c * 128:(c + 1) * 128],
                                                 xlnT[par][:, c * 128:(c + 1) * 128], identb[:])
                        p.bump("pe", i)
                    block.tensor(trs)
                    pe_tr = p.at("pe")

                    def evtr(vector, s=s, pe_tr=pe_tr):
                        p.wait(vector, "pe", pe_tr)
                        src = ps_tr[:].rearrange("p (c x) -> p c x", x=128)
                        dst = xlnF[:].rearrange("p (c x) -> p c x", x=512)[:, :, s * 128:(s + 1) * 128]
                        p.bump("ve", vector.tensor_copy(dst, src))
                    block.vector(evtr)
                    tr_free = p.at("ve")

                xlnF_ready = p.at("ve")
                for m in range(8):
                    bank = m % 2

                    def mmk(tensor, m=m, bank=bank, fv=psA_free_v[bank], fs=psA_free_s[bank],
                            xr=xlnF_ready):
                        p.wait(tensor, "ve", max(fv, xr))
                        p.wait(tensor, "se", fs)
                        for kc in range(8):
                            i = tensor.matmul(ps_a[bank][:],
                                              wkB[:, kc * 1024 + m * 128: kc * 1024 + (m + 1) * 128],
                                              xlnF[:, kc * 512:(kc + 1) * 512],
                                              start=(kc == 0), stop=(kc == 7))
                        p.bump("pe", i)
                    block.tensor(mmk)
                    pv = p.at("pe")

                    def evk(vector, m=m, bank=bank, pv=pv):
                        p.wait(vector, "pe", pv)
                        p.bump("ve", vector.tensor_copy(kF[:, m * 512:(m + 1) * 512],
                                                        ps_a[bank][:]))
                    block.vector(evk)
                    psA_free_v[bank] = p.at("ve")
                kF_ready = p.at("ve")

                for half in range(2):
                    for s2 in range(2):
                        s = half * 2 + s2
                        for nh2 in range(2):
                            bank = (s2 * 2 + nh2) % 2

                            def mmv(tensor, s=s, nh2=nh2, bank=bank,
                                    fv=psA_free_v[bank], fs=psA_free_s[bank]):
                                p.wait(tensor, "ve", fv)
                                p.wait(tensor, "se", fs)
                                for kc in range(8):
                                    i = tensor.matmul(
                                        ps_a[bank][:],
                                        xlnF[:, kc * 512 + s * 128: kc * 512 + (s + 1) * 128],
                                        wvB[:, kc * 1024 + nh2 * 512: kc * 1024 + (nh2 + 1) * 512],
                                        start=(kc == 0), stop=(kc == 7))
                                p.bump("pe", i)
                            block.tensor(mmv)
                            pv = p.at("pe")

                            def evv(vector, s2=s2, nh2=nh2, bank=bank, pv=pv):
                                p.wait(vector, "pe", pv)
                                src = ps_a[bank][:].rearrange("p (h x) -> p h x", x=64)
                                dst = vaug[s2][:].rearrange("p (h c) -> p h c", c=66)[:, nh2 * 8:(nh2 + 1) * 8, 0:64]
                                p.bump("ve", vector.tensor_copy(dst, src))
                            block.vector(evv)
                            psA_free_v[bank] = p.at("ve")

                    for j in range(8):
                        for s2 in range(2):
                            s = half * 2 + s2
                            bank = (j * 2 + s2) % 2

                            def mmkp(tensor, j=j, s=s, bank=bank, fv=psA_free_v[bank],
                                     fs=psA_free_s[bank], kr=kF_ready):
                                p.wait(tensor, "ve", max(fv, kr))
                                p.wait(tensor, "se", fs)
                                i = tensor.matmul(ps_a[bank][:],
                                                  kF[:, j * 512 + s * 128: j * 512 + (s + 1) * 128],
                                                  projp[:, j * 512:(j + 1) * 512],
                                                  start=True, stop=True)
                                p.bump("pe", i)
                            block.tensor(mmkp)
                            pv = p.at("pe")

                            def vmin(vector, bank=bank, pv=pv):
                                p.wait(vector, "pe", pv)
                                p.bump("ve", vector.tensor_scalar_min(mbuf[:], ps_a[bank][:], 0.0))
                            block.vector(vmin)
                            v_m = p.at("ve")

                            def srel(scalar, bank=bank, pv=pv, v_m=v_m):
                                p.wait(scalar, "pe", pv)
                                p.wait(scalar, "ve", v_m)
                                scalar.activation(rbuf[:], ps_a[bank][:], AF.Relu)
                                i = scalar.activation(ebuf[:], mbuf[:], AF.Exp)
                                p.bump("se", i)
                            block.scalar(srel)
                            s_re = p.at("se")

                            def vadd(vector, j=j, s2=s2, s_re=s_re):
                                p.wait(vector, "se", s_re)
                                i = vector.tensor_tensor(kpT[s2][:, j * 512:(j + 1) * 512],
                                                         rbuf[:], ebuf[:], ALU.add)
                                p.bump("ve", i)
                            block.vector(vadd)
                            psA_free_v[bank] = v_m
                            psA_free_s[bank] = s_re

                    kp_ready = p.at("ve")
                    for g in range(2):
                        for jj in range(4):
                            j = g * 4 + jj

                            def mmkv(tensor, j=j, jj=jj, fa=kv_free[jj], kr=kp_ready):
                                p.wait(tensor, "ve", max(fa, kr))
                                for hh in range(2):
                                    h = 2 * j + hh
                                    for s2 in range(2):
                                        i = tensor.matmul(
                                            ps_kv[jj][:, hh * 256:(hh + 1) * 256],
                                            vaug[s2][:, h * 66: h * 66 + 65],
                                            kpT[s2][:, h * 256:(h + 1) * 256],
                                            start=(s2 == 0), stop=(s2 == 1),
                                            skip_group_check=True)
                                p.bump("pe", i)
                            block.tensor(mmkv)
                            pv = p.at("pe")

                            def vkv(vector, j=j, jj=jj, first=(t512 == 0 and half == 0), pv=pv):
                                p.wait(vector, "pe", pv)
                                if first:
                                    i = vector.tensor_copy(kv_sb[:, j * 512:(j + 1) * 512],
                                                           ps_kv[jj][:])
                                else:
                                    i = vector.tensor_tensor(kv_sb[:, j * 512:(j + 1) * 512],
                                                             kv_sb[:, j * 512:(j + 1) * 512],
                                                             ps_kv[jj][:], ALU.add)
                                p.bump("ve", i)
                            block.vector(vkv)
                            kv_free[jj] = p.at("ve")
                prev_pe = p.at("pe")

            if "kp" in taps:
                def vstt(vector):
                    vector.tensor_copy(junk1k[:, 0:1], s1t[:, 0:1])
                    vector.tensor_copy(junk1k[:, 1:2], s2t[:, 0:1])
                    vector.tensor_copy(junk1k[:, 2:3], mut[:, 0:1])
                    vector.tensor_copy(junk1k[:, 3:4], vart[:, 0:1])
                    vector.tensor_copy(junk1k[:, 4:5], rt[:, 0:1])
                    i = vector.tensor_copy(junk1k[:, 5:6], bt[:, 0:1])
                    p.bump("ve", i)
                block.vector(vstt)

                def tkp(g):
                    p.wait(g, "ve", p.at("ve"))
                    p.bump("sG", g.dma_start(tap_aps["t_xt"], xlnT[1][:]), 16)
                    p.bump("sG", g.dma_start(tap_aps["t_st"], junk1k[:, 0:8]), 16)
                    p.bump("sG", g.dma_start(tap_aps["t_kp0"], kpT[0][:]), 16)
                    p.bump("sG", g.dma_start(tap_aps["t_kp1"], kpT[1][:]), 16)
                    p.bump("sG", g.dma_start(tap_aps["t_kF"], kF[:]), 16)
                    p.bump("sG", g.dma_start(tap_aps["t_va0"], vaug[0][:]), 16)
                    p.bump("sG", g.dma_start(tap_aps["t_va1"], vaug[1][:]), 16)
                    p.bump("sG", g.dma_start(tap_aps["t_xf"], xlnF[:]), 16)
                    p.wait(g, "sG", p.at("sG"))
                block.gpsimd(tkp)

        kv_done_v = p.at("ve")
        with nc.Block() as block:
            def _(g):
                p.wait(g, "ve", kv_done_v)
                p.bump("sG", g.dma_start(cc_in.ap(), kv_sb[:]), 16)
                p.wait(g, "sG", p.at("sG"))
                i = g.collective_compute(
                    "AllReduce", ALU.add,
                    replica_groups=[[0, 1, 2, 3], [4, 5, 6, 7]],
                    ins=[cc_in.ap().opt()],
                    outs=[cc_out.ap().opt()])
                p.bump("cc", i)
            block.gpsimd(_)
            if "kvloc" in taps:
                def _(g):
                    p.bump("sG", g.dma_start(tap_aps["t_kvloc"], kv_sb[:]), 16)
                block.gpsimd(_)
        st_pa.close()
        st_a.close()

        st_pr1 = ExitStack()
        wstg = [st_pr1.enter_context(nc.sbuf_tensor(f"wstg{i}", [128, 2048], F32))
                for i in range(2)]
        acc80 = st_pr1.enter_context(nc.sbuf_tensor("acc80", [128, 80], F32))
        ps_sc = st_pr1.enter_context(nc.psum_tensor("ps_sc", [128, 16], F32))
        with nc.Block() as block:
            cols = ([("w1", c, 0) for c in range(32)] +
                    [("w2", c, hc) for c in range(8) for hc in range(2)])
            free_at = [p.at("ve"), p.at("ve")]
            for idx, (kind, c, hc) in enumerate(cols):
                par = idx % 2
                ncol = 1024 if kind == "w1" else 2048
                src = (W1[c * 128:(c + 1) * 128, :] if kind == "w1"
                       else W2[c * 128:(c + 1) * 128, hc * 2048:(hc + 1) * 2048])

                def load(sync, src=src, ncol=ncol, par=par, fa=free_at[par]):
                    p.wait(sync, "ve", fa)
                    p.bump("sL", sync.dma_start(wstg[par][:, 0:ncol], src), 16)
                block.sync(load)
                lv = p.at("sL")

                def red(vector, idx=idx, ncol=ncol, par=par, lv=lv):
                    p.wait(vector, "sL", lv)
                    i = vector.tensor_reduce(acc80[:, idx:idx + 1], wstg[par][:, 0:ncol],
                                             AX.X, ALU.add, apply_absolute_value=True)
                    p.bump("ve", i)
                block.vector(red)
                free_at[par] = p.at("ve")

            def vtot(vector, fa=p.at("ve")):
                p.wait(vector, "ve", fa)
                vector.tensor_reduce(acc80[:, 48:49], acc80[:, 0:32], AX.X, ALU.add)
                i = vector.tensor_reduce(acc80[:, 49:50], acc80[:, 32:48], AX.X, ALU.add)
                p.bump("ve", i)
            block.vector(vtot)

            def vtot2(vector, fa=p.at("ve")):
                p.wait(vector, "ve", fa)
                vector.tensor_copy(acc80[:, 0:1], acc80[:, 48:49])
                i = vector.tensor_copy(acc80[:, 1:2], acc80[:, 49:50])
                p.bump("ve", i)
            block.vector(vtot2)
            vv = p.at("ve")

            def mm(tensor, vv=vv):
                p.wait(tensor, "ve", vv)
                i = tensor.matmul(ps_sc[0:1, 0:2], onescol[:, 0:1], acc80[:, 0:2],
                                  start=True, stop=True)
                p.bump("pe", i)
            block.tensor(mm)
            pv = p.at("pe")

            def vsc(vector, pv=pv):
                p.wait(vector, "pe", pv)
                vector.tensor_scalar(m1s[:, 0:1], ps_sc[0:1, 0:1], 1.0 / (HID * D), 1e-5,
                                     ALU.mult, ALU.max)
                i = vector.tensor_scalar(m2s[:, 0:1], ps_sc[0:1, 1:2], 1.0 / (HID * D), 1e-5,
                                         ALU.mult, ALU.max)
                p.bump("ve", i)
            block.vector(vsc)

            def vsc2(vector, fa=p.at("ve")):
                p.wait(vector, "ve", fa)
                i = vector.reciprocal(im1[:, 0:1], m1s[:, 0:1])
                p.bump("ve", i)
            block.vector(vsc2)

            def vsc3(vector, fa=p.at("ve")):
                p.wait(vector, "ve", fa)
                i = vector.tensor_scalar_mul(im1[:, 0:1], im1[:, 0:1], 2.0)
                p.bump("ve", i)
            block.vector(vsc3)
            vv2 = p.at("ve")

            def mb(tensor, vv2=vv2):
                p.wait(tensor, "ve", vv2)
                tensor.matmul(ps_sc[:, 2:3], onesrow[:, :], m1s[:, 0:1], start=True, stop=True)
                i = tensor.matmul(ps_sc[:, 3:4], onesrow[:, :], m2s[:, 0:1], start=True, stop=True)
                p.bump("pe", i)
            block.tensor(mb)
            pv2 = p.at("pe")

            def vbb(vector, pv2=pv2):
                p.wait(vector, "pe", pv2)
                vector.tensor_scalar_mul(thr1[:, 0:1], ps_sc[:, 2:3], 0.5)
                vector.tensor_scalar_mul(nthr1[:, 0:1], ps_sc[:, 2:3], -0.5)
                vector.tensor_scalar_mul(k1b[:, 0:1], ps_sc[:, 2:3], 0.5)
                vector.tensor_scalar_mul(thr2[:, 0:1], ps_sc[:, 3:4], 0.5)
                vector.tensor_scalar_mul(nthr2[:, 0:1], ps_sc[:, 3:4], -0.5)
                i = vector.tensor_scalar_mul(k2b[:, 0:1], ps_sc[:, 3:4], 0.5)
                p.bump("ve", i)
            block.vector(vbb)
        st_pr1.close()

        st_kt = ExitStack()
        kv_red = st_kt.enter_context(nc.sbuf_tensor("kv_red", [65, NH * FT], F32))
        kv_redb = st_kt.enter_context(nc.sbuf_tensor("kv_redb", [65, NH * FT], BF16))
        with nc.Block() as block:
            def _(scalar):
                p.wait(scalar, "cc", 1)
                p.bump("sS", scalar.dma_start(kv_red[:], cc_out.ap()), 16)
            block.scalar(_)
            lv = p.at("sS")
            if "kv" in taps:
                def _(g, lv=lv):
                    p.wait(g, "sS", lv)
                    p.bump("sG", g.dma_start(tap_aps["t_kv"], kv_red[:]), 16)
                block.gpsimd(_)

            def vc(vector, lv=lv):
                p.wait(vector, "sS", lv)
                p.bump("ve", vector.tensor_copy(kv_redb[:], kv_red[:]))
            block.vector(vc)
            vv = p.at("ve")

            ev_free = [p.at("ve")] * 2
            for c in range(32):
                slot = c % 2
                pst = ps_tr if slot == 0 else ps_tr2

                def tr1(tensor, c=c, pst=pst, vv=vv, fa=ev_free[slot]):
                    p.wait(tensor, "ve", max(vv, fa))
                    i = tensor.transpose(pst[:, 0:65], kv_redb[:, c * 128:(c + 1) * 128],
                                         identb[0:65, 0:65])
                    p.bump("pe", i)
                block.tensor(tr1)
                pv = p.at("pe")

                def ev1(vector, c=c, pst=pst, pv=pv):
                    p.wait(vector, "pe", pv)
                    vector.tensor_copy(kvT[:, c * 65:(c + 1) * 65], pst[:, 0:65])
                    i = vector.tensor_copy(ksblk[:, c * 16 + c // 2: c * 16 + c // 2 + 1],
                                           pst[:, 64:65])
                    p.bump("ve", i)
                block.vector(ev1)
                ev_free[slot] = p.at("ve")
        st_kt.close()

        if stage < 2:
            with nc.Block() as block:
                def _(g):
                    p.wait(g, "ve", p.at("ve"))
                    p.wait(g, "sG", p.at("sG"))
                block.gpsimd(_)
            st_ab.close()
            return nc, tap_aps

        st_pb = ExitStack()
        pb = st_pb.enter_context
        qF = pb(nc.sbuf_tensor("qF", [128, 8 * 512], BF16))
        qpF = pb(nc.sbuf_tensor("qpF", [128, 32 * 512], BF16))
        attF = pb(nc.sbuf_tensor("attF", [128, 8 * 512], BF16))
        zrow = pb(nc.sbuf_tensor("zrow", [16, 512], F32))
        z2s = pb(nc.sbuf_tensor("z2s", [128, 512], F32))
        x1t = pb(nc.sbuf_tensor("x1t", [128, 1024], F32))
        xb4 = [pb(nc.sbuf_tensor(f"xb4_{s}", [128, 1024], F32)) for s in range(4)]
        s1c = pb(nc.sbuf_tensor("s1c", [128, NT128], F32))
        s2c = pb(nc.sbuf_tensor("s2c", [128, NT128], F32))
        muC = pb(nc.sbuf_tensor("muC", [128, NT128], F32))
        v1C = pb(nc.sbuf_tensor("v1C", [128, NT128], F32))
        u1C = pb(nc.sbuf_tensor("u1C", [128, NT128], F32))
        r1C = pb(nc.sbuf_tensor("r1C", [128, NT128], F32))
        u2C = pb(nc.sbuf_tensor("u2C", [128, NT128], F32))
        ps_den = pb(nc.psum_tensor("ps_den", [16, 512], F32))
        ps_z2 = pb(nc.psum_tensor("ps_z2", [128, 512], F32))
        ps_o = [pb(nc.psum_tensor(f"ps_o{i}", [128, 512], F32)) for i in range(2)]

        with nc.Block() as block:
            psA_free_v = [p.at("ve")] * 2
            psA_free_s = [p.at("se")] * 2
            o_free = [p.at("ve")] * 2
            tr_free = p.at("ve")
            z2_free = p.at("ve")
            den_free = p.at("ve")
            xb_free_v = [p.at("ve")] * 4
            xb_free_s = [p.at("se")] * 4
            xln_free = [p.at("pe")] * 2
            x1_gate = p.at("sG")
            prev_pe = p.at("pe")
            for t512 in range(NT512):
                for s in range(4):
                    i128 = t512 * 4 + s
                    par = i128 % 2

                    def load_x(sync, i128=i128, s=s, fv=xb_free_v[s], fs=xb_free_s[s]):
                        p.wait(sync, "ve", fv)
                        p.wait(sync, "se", fs)
                        p.bump("sL", sync.dma_start(xb4[s][:],
                                                    X[i128 * 128:(i128 + 1) * 128, :]), 16)
                    block.sync(load_x)
                    lv = p.at("sL")

                    def vs(vector, s=s, lv=lv):
                        p.wait(vector, "sL", lv)
                        p.bump("ve", vector.tensor_reduce(s1t[:, 0:1], xb4[s][:],
                                                          AX.X, ALU.add))
                    block.vector(vs)
                    v_s1 = p.at("ve")

                    def ss(scalar, s=s, lv=lv):
                        p.wait(scalar, "sL", lv)
                        p.bump("se", scalar.activation(junk1k[:, :], xb4[s][:], AF.Square,
                                                       accum_out=s2t[:, 0:1]))
                    block.scalar(ss)
                    s_s2 = p.at("se")
                    v_bt = emit_ln_chain(block, v_s1, s_s2)

                    def sa(scalar, s=s, par=par, v_bt=v_bt, fa=xln_free[par]):
                        p.wait(scalar, "ve", v_bt)
                        p.wait(scalar, "pe", fa)
                        i = scalar.activation(xlnT[par][:], xb4[s][:], AF.Identity,
                                              bias=bt[:, 0:1], scale=rt[:, 0:1])
                        p.bump("se", i)
                    block.scalar(sa)
                    s_ap = p.at("se")
                    xb_free_s[s] = s_ap
                    xb_free_v[s] = v_s1

                    def trs(tensor, par=par, s_ap=s_ap, fa=tr_free):
                        p.wait(tensor, "se", s_ap)
                        p.wait(tensor, "ve", fa)
                        for c in range(8):
                            i = tensor.transpose(ps_tr[:, c * 128:(c + 1) * 128],
                                                 xlnT[par][:, c * 128:(c + 1) * 128], identb[:])
                        p.bump("pe", i)
                    block.tensor(trs)
                    pe_tr = p.at("pe")
                    xln_free[par] = pe_tr

                    def evtr(vector, s=s, pe_tr=pe_tr):
                        p.wait(vector, "pe", pe_tr)
                        src = ps_tr[:].rearrange("p (c x) -> p c x", x=128)
                        dst = xlnF[:].rearrange("p (c x) -> p c x", x=512)[:, :, s * 128:(s + 1) * 128]
                        p.bump("ve", vector.tensor_copy(dst, src))
                    block.vector(evtr)
                    tr_free = p.at("ve")

                xlnF_ready = p.at("ve")
                for m in range(8):
                    bank = m % 2

                    def mmq(tensor, m=m, bank=bank, fv=psA_free_v[bank], fs=psA_free_s[bank],
                            xr=xlnF_ready):
                        p.wait(tensor, "ve", max(fv, xr))
                        p.wait(tensor, "se", fs)
                        for kc in range(8):
                            i = tensor.matmul(ps_a[bank][:],
                                              wqB[:, kc * 1024 + m * 128: kc * 1024 + (m + 1) * 128],
                                              xlnF[:, kc * 512:(kc + 1) * 512],
                                              start=(kc == 0), stop=(kc == 7))
                        p.bump("pe", i)
                    block.tensor(mmq)
                    pv = p.at("pe")

                    def evq(vector, m=m, bank=bank, pv=pv, pef=prev_pe):
                        p.wait(vector, "pe", max(pv, pef))
                        p.bump("ve", vector.tensor_copy(qF[:, m * 512:(m + 1) * 512],
                                                        ps_a[bank][:]))
                    block.vector(evq)
                    psA_free_v[bank] = p.at("ve")
                qF_ready = p.at("ve")

                for j in range(8):
                    for c4 in range(4):
                        cf = 4 * j + c4
                        bank = cf % 2

                        def mmqp(tensor, j=j, c4=c4, bank=bank, fv=psA_free_v[bank],
                                 fs=psA_free_s[bank], qr=qF_ready):
                            p.wait(tensor, "ve", max(fv, qr))
                            p.wait(tensor, "se", fs)
                            i = tensor.matmul(ps_a[bank][:],
                                              projp[:, j * 512 + c4 * 128: j * 512 + (c4 + 1) * 128],
                                              qF[:, j * 512:(j + 1) * 512],
                                              start=True, stop=True)
                            p.bump("pe", i)
                        block.tensor(mmqp)
                        pv = p.at("pe")

                        def vmin(vector, bank=bank, pv=pv):
                            p.wait(vector, "pe", pv)
                            p.bump("ve", vector.tensor_scalar_min(mbuf[:], ps_a[bank][:], 0.0))
                        block.vector(vmin)
                        v_m = p.at("ve")

                        def srel(scalar, bank=bank, pv=pv, v_m=v_m):
                            p.wait(scalar, "pe", pv)
                            p.wait(scalar, "ve", v_m)
                            scalar.activation(rbuf[:], ps_a[bank][:], AF.Relu)
                            i = scalar.activation(ebuf[:], mbuf[:], AF.Exp)
                            p.bump("se", i)
                        block.scalar(srel)
                        s_re = p.at("se")

                        def vadd(vector, cf=cf, s_re=s_re, pef=prev_pe):
                            p.wait(vector, "se", s_re)
                            p.wait(vector, "pe", pef)
                            i = vector.tensor_tensor(qpF[:, cf * 512:(cf + 1) * 512],
                                                     rbuf[:], ebuf[:], ALU.add)
                            p.bump("ve", i)
                        block.vector(vadd)
                        psA_free_v[bank] = v_m
                        psA_free_s[bank] = s_re
                qp_ready = p.at("ve")

                def mmden(tensor, qr=qp_ready, fa=den_free):
                    p.wait(tensor, "ve", max(qr, fa))
                    for c in range(32):
                        i = tensor.matmul(ps_den[:], ksblk[:, c * 16:(c + 1) * 16],
                                          qpF[:, c * 512:(c + 1) * 512],
                                          start=(c == 0), stop=(c == 31))
                    p.bump("pe", i)
                block.tensor(mmden)
                pv_den = p.at("pe")

                def vz(vector, pv=pv_den):
                    p.wait(vector, "pe", pv)
                    vector.tensor_scalar(zrow[:, :], ps_den[:, :], 1.0, float(ZEPS),
                                         ALU.mult, ALU.add)
                    i = vector.reciprocal(zrow[:, :], zrow[:, :])
                    p.bump("ve", i)
                block.vector(vz)
                v_z = p.at("ve")
                den_free = v_z

                for c2 in range(8):
                    bank = c2 % 2

                    def mmat(tensor, c2=c2, bank=bank, fa=o_free[bank], qr=qp_ready):
                        p.wait(tensor, "ve", max(fa, qr))
                        for hh in range(2):
                            h = 2 * c2 + hh
                            for cc2 in range(2):
                                c = 2 * h + cc2
                                i = tensor.matmul(ps_o[bank][hh * 64:(hh + 1) * 64, :],
                                                  kvT[:, c * 65: c * 65 + 64],
                                                  qpF[:, c * 512:(c + 1) * 512],
                                                  start=(cc2 == 0), stop=(cc2 == 1),
                                                  skip_group_check=True)
                        p.bump("pe", i)
                    block.tensor(mmat)
                    pv_at = p.at("pe")

                    def mmz2(tensor, c2=c2, v_z=v_z, fa=z2_free):
                        p.wait(tensor, "ve", max(v_z, fa))
                        i = tensor.matmul(ps_z2[:], e16[:, c2 * 128:(c2 + 1) * 128],
                                          zrow[:, :], start=True, stop=True)
                        p.bump("pe", i)
                    block.tensor(mmz2)
                    pv_z2 = p.at("pe")

                    def evz2(vector, pv=pv_z2):
                        p.wait(vector, "pe", pv)
                        p.bump("ve", vector.tensor_copy(z2s[:], ps_z2[:]))
                    block.vector(evz2)
                    v_z2 = p.at("ve")
                    z2_free = v_z2

                    def evat(vector, c2=c2, bank=bank, pv=pv_at, v_z2=v_z2, pef=prev_pe):
                        p.wait(vector, "pe", max(pv, pef))
                        p.wait(vector, "ve", v_z2)
                        i = vector.tensor_tensor(attF[:, c2 * 512:(c2 + 1) * 512],
                                                 ps_o[bank][:], z2s[:], ALU.mult)
                        p.bump("ve", i)
                    block.vector(evat)
                    o_free[bank] = p.at("ve")
                att_ready = p.at("ve")

                for s in range(4):
                    i128 = t512 * 4 + s
                    pv_o = [0, 0]
                    for nh2 in range(2):
                        def mmo(tensor, s=s, nh2=nh2, ar=att_ready, fa=o_free[nh2]):
                            p.wait(tensor, "ve", max(ar, fa))
                            for kc in range(8):
                                tensor.matmul(
                                    ps_o[nh2][:],
                                    attF[:, kc * 512 + s * 128: kc * 512 + (s + 1) * 128],
                                    woutB[:, kc * 1024 + nh2 * 512: kc * 1024 + (nh2 + 1) * 512],
                                    start=(kc == 0), stop=False)
                            i = tensor.matmul(ps_o[nh2][:], onesb[:, 0:128],
                                              boutr[:, nh2 * 512:(nh2 + 1) * 512],
                                              start=False, stop=True)
                            p.bump("pe", i)
                        block.tensor(mmo)
                        pv_o[nh2] = p.at("pe")

                    for nh2 in range(2):
                        def vres(vector, s=s, nh2=nh2, pv=pv_o[nh2], xg=x1_gate):
                            p.wait(vector, "pe", pv)
                            p.wait(vector, "sG", xg)
                            i = vector.tensor_tensor(x1t[:, nh2 * 512:(nh2 + 1) * 512],
                                                     xb4[s][:, nh2 * 512:(nh2 + 1) * 512],
                                                     ps_o[nh2][:], ALU.add)
                            p.bump("ve", i)
                        block.vector(vres)
                        o_free[nh2] = p.at("ve")
                    v_x1 = p.at("ve")
                    xb_free_v[s] = v_x1

                    def vst(vector, i128=i128):
                        p.bump("ve", vector.tensor_reduce(s1c[:, i128:i128 + 1], x1t[:],
                                                          AX.X, ALU.add))
                    block.vector(vst)

                    def sst(scalar, i128=i128, v_x1=v_x1):
                        p.wait(scalar, "ve", v_x1)
                        p.bump("se", scalar.activation(junk1k[:, :], x1t[:], AF.Square,
                                                       accum_out=s2c[:, i128:i128 + 1]))
                    block.scalar(sst)
                    s_st = p.at("se")

                    def gx1(g, i128=i128, s_st=s_st):
                        p.wait(g, "se", s_st)
                        p.bump("sG", g.dma_start(X1D.ap()[i128 * 128:(i128 + 1) * 128, :],
                                                 x1t[:]), 16)
                    block.gpsimd(gx1)
                    x1_gate = p.at("sG")
                prev_pe = p.at("pe")

            def _gap(vector):
                i = vector.nop()
            steps = [
                lambda v: v.tensor_scalar_mul(muC[:], s1c[:], 1.0 / 1024.0),
                lambda v: v.tensor_tensor(v1C[:], muC[:], muC[:], ALU.mult),
                lambda v: v.tensor_scalar_mul(u1C[:], s2c[:], 1.0 / 1024.0),
                lambda v: v.tensor_tensor(v1C[:], u1C[:], v1C[:], ALU.subtract),
                lambda v: v.tensor_scalar(u1C[:], v1C[:], 1.0, float(EPS), ALU.mult, ALU.add),
            ]
            first_se = p.at("se")
            for stp in steps:
                def vstep(vector, stp=stp, fa=p.at("ve"), fs=first_se):
                    p.wait(vector, "ve", fa)
                    p.wait(vector, "se", fs)
                    p.bump("ve", stp(vector))
                block.vector(vstep)
            vv = p.at("ve")

            def scC(scalar, vv=vv):
                p.wait(scalar, "ve", vv)
                scalar.activation(r1C[:], u1C[:], AF.Ln)
                i = scalar.activation(r1C[:], r1C[:], AF.Exp, scale=-0.5)
                p.bump("se", i)
            block.scalar(scC)
            sv = p.at("se")

            steps2 = [
                lambda v: v.reciprocal(u2C[:], u1C[:]),
                lambda v: v.tensor_tensor(u2C[:], v1C[:], u2C[:], ALU.mult),
                lambda v: v.tensor_scalar(u2C[:], u2C[:], 1.0, float(EPS), ALU.mult, ALU.add),
            ]
            for stp in steps2:
                def vstep2(vector, stp=stp, fa=p.at("ve"), sv=sv):
                    p.wait(vector, "ve", fa)
                    p.wait(vector, "se", sv)
                    p.bump("ve", stp(vector))
                block.vector(vstep2)
            vv2 = p.at("ve")

            def scC2(scalar, vv2=vv2):
                p.wait(scalar, "ve", vv2)
                scalar.activation(u2C[:], u2C[:], AF.Ln)
                i = scalar.activation(u2C[:], u2C[:], AF.Exp, scale=-0.5)
                p.bump("se", i)
            block.scalar(scC2)
            sv2 = p.at("se")

            steps3 = [
                lambda v: v.tensor_tensor(aC[:], r1C[:], u2C[:], ALU.mult),
                lambda v: v.tensor_scalar_mul(cC[:], muC[:], -1.0),
                lambda v: v.tensor_tensor(cC[:], cC[:], aC[:], ALU.mult),
            ]
            for stp in steps3:
                def vstep3(vector, stp=stp, fa=p.at("ve"), sv2=sv2):
                    p.wait(vector, "ve", fa)
                    p.wait(vector, "se", sv2)
                    p.bump("ve", stp(vector))
                block.vector(vstep3)
        st_pb.close()
        st_ab.close()

        if "x1" in taps:
            with nc.Block() as block:
                def _(scalar):
                    p.wait(scalar, "sG", p.at("sG"))
                    p.bump("sS", scalar.dma_start(tap_aps["t_x1"], X1D.ap()), 16)
                block.scalar(_)

        if stage < 3:
            with nc.Block() as block:
                def _(g):
                    p.wait(g, "sG", p.at("sG"))
                    p.wait(g, "sS", p.at("sS"))
                block.gpsimd(_)
            return nc, tap_aps

        st_w = ExitStack()
        w_ = st_w.enter_context
        w1gT = w_(nc.sbuf_tensor("w1gT", [128, 8 * 4096], BF16))
        w2gT = w_(nc.sbuf_tensor("w2gT", [128, 32 * 1024], BF16))
        c1row = w_(nc.sbuf_tensor("c1row", [1, HID], BF16))
        c2b = w_(nc.sbuf_tensor("c2b", [128, D], F32))

        st_pr2 = ExitStack()
        r2_ = st_pr2.enter_context
        wstg = [r2_(nc.sbuf_tensor(f"wstg2_{i}", [128, 2048], F32)) for i in range(2)]
        sgn1 = r2_(nc.sbuf_tensor("sgn1", [128, 2048], BF16))
        sgn2 = r2_(nc.sbuf_tensor("sgn2", [128, 2048], BF16))
        tq = r2_(nc.sbuf_tensor("tq", [128, 4096], BF16))
        tqT = r2_(nc.sbuf_tensor("tqT", [128, 1024], BF16))
        cb1row = r2_(nc.sbuf_tensor("cb1row", [1, HID], F32))
        cb2row = r2_(nc.sbuf_tensor("cb2row", [1, D], F32))
        ps_cb = r2_(nc.psum_tensor("ps_cb", [1, 512], F32))
        ps_oc = [r2_(nc.psum_tensor(f"ps_oc{i}", [128, 512], F32)) for i in range(2)]

        with nc.Block() as block:
            stg_free = [p.at("se"), p.at("se")]
            tq_free_pe = p.at("pe")
            for c in range(32):
                par = c % 2

                def load(sync, c=c, par=par, fa=stg_free[par]):
                    p.wait(sync, "se", fa)
                    p.bump("sL", sync.dma_start(wstg[par][:, 0:1024],
                                                W1[c * 128:(c + 1) * 128, :]), 16)
                block.sync(load)
                lv = p.at("sL")

                def sgn(scalar, par=par, lv=lv, fv=p.at("ve")):
                    p.wait(scalar, "sL", lv)
                    p.wait(scalar, "ve", fv)
                    scalar.activation(sgn1[:, 0:1024], wstg[par][:, 0:1024], AF.Sign,
                                      bias=nthr1[:, 0:1])
                    i = scalar.activation(sgn2[:, 0:1024], wstg[par][:, 0:1024], AF.Sign,
                                          bias=thr1[:, 0:1])
                    p.bump("se", i)
                block.scalar(sgn)
                sv = p.at("se")
                stg_free[par] = sv

                def vq(vector, sv=sv, tf=tq_free_pe):
                    p.wait(vector, "se", sv)
                    p.wait(vector, "pe", tf)
                    i = vector.tensor_tensor(tq[:, 0:1024], sgn1[:, 0:1024],
                                             sgn2[:, 0:1024], ALU.add)
                    p.bump("ve", i)
                block.vector(vq)
                v_tq = p.at("ve")

                def trq(tensor, v_tq=v_tq, fa=p.at("ve")):
                    p.wait(tensor, "ve", max(v_tq, fa))
                    for cd in range(8):
                        i = tensor.transpose(ps_tr[:, cd * 128:(cd + 1) * 128],
                                             tq[:, cd * 128:(cd + 1) * 128], identb[:])
                    p.bump("pe", i)
                block.tensor(trq)
                pv_tr = p.at("pe")
                tq_free_pe = pv_tr

                def evu(vector, pv=pv_tr):
                    p.wait(vector, "pe", pv)
                    i = vector.tensor_copy(tqT[:, 0:1024], ps_tr[:])
                    p.bump("ve", i)
                block.vector(evu)
                v_un = p.at("ve")

                def mcb(tensor, v_un=v_un):
                    p.wait(tensor, "ve", v_un)
                    for cd in range(8):
                        i = tensor.matmul(ps_cb[0:1, 0:128], b1lnB[:, cd:cd + 1],
                                          tqT[:, cd * 128:(cd + 1) * 128],
                                          start=(cd == 0), stop=(cd == 7))
                    p.bump("pe", i)
                block.tensor(mcb)
                pv_cb = p.at("pe")

                def evs(vector, c=c, pv_cb=pv_cb):
                    p.wait(vector, "pe", pv_cb)
                    vector.tensor_copy(cb1row[:, c * 128:(c + 1) * 128], ps_cb[0:1, 0:128])
                    for cd in range(8):
                        i = vector.tensor_scalar_mul(
                            w1gT[:, cd * 4096 + c * 128: cd * 4096 + (c + 1) * 128],
                            tqT[:, cd * 128:(cd + 1) * 128], g1F[:, cd:cd + 1])
                    p.bump("ve", i)
                block.vector(evs)

            for hh in range(2):
                def loadb(sync, hh=hh, fa=p.at("ve")):
                    p.wait(sync, "ve", fa)
                    p.bump("sL", sync.dma_start(wstg[0][0:1, 0:2048],
                                                B1R[0:1, hh * 2048:(hh + 1) * 2048]), 16)
                block.sync(loadb)
                lvb = p.at("sL")

                def vfin1(vector, hh=hh, lvb=lvb):
                    p.wait(vector, "sL", lvb)
                    vector.tensor_scalar_mul(wstg[0][0:1, 0:2048], wstg[0][0:1, 0:2048],
                                             im1[0:1, 0:1])
                    vector.tensor_tensor(wstg[0][0:1, 0:2048], wstg[0][0:1, 0:2048],
                                         cb1row[:, hh * 2048:(hh + 1) * 2048], ALU.add)
                    i = vector.tensor_copy(c1row[:, hh * 2048:(hh + 1) * 2048],
                                           wstg[0][0:1, 0:2048])
                    p.bump("ve", i)
                block.vector(vfin1)

            for c in range(8):
                for hc in range(2):
                    par = hc

                    def load(sync, c=c, hc=hc, par=par, fa=p.at("se")):
                        p.wait(sync, "se", fa)
                        p.wait(sync, "ve", p.at("ve"))
                        p.bump("sL", sync.dma_start(
                            wstg[par][:, 0:2048],
                            W2[c * 128:(c + 1) * 128, hc * 2048:(hc + 1) * 2048]), 16)
                    block.sync(load)
                    lv = p.at("sL")

                    def sgn(scalar, par=par, lv=lv, fv=p.at("ve")):
                        p.wait(scalar, "sL", lv)
                        p.wait(scalar, "ve", fv)
                        scalar.activation(sgn1[:, 0:2048], wstg[par][:, 0:2048], AF.Sign,
                                          bias=nthr2[:, 0:1])
                        i = scalar.activation(sgn2[:, 0:2048], wstg[par][:, 0:2048], AF.Sign,
                                              bias=thr2[:, 0:1])
                        p.bump("se", i)
                    block.scalar(sgn)
                    sv = p.at("se")

                    def vq(vector, hc=hc, sv=sv, tf=tq_free_pe):
                        p.wait(vector, "se", sv)
                        p.wait(vector, "pe", tf)
                        i = vector.tensor_tensor(tq[:, hc * 2048:(hc + 1) * 2048],
                                                 sgn1[:, 0:2048], sgn2[:, 0:2048], ALU.add)
                        p.bump("ve", i)
                    block.vector(vq)
                v_tq = p.at("ve")

                for g4 in range(4):
                    def trq(tensor, g4=g4, v_tq=v_tq, fa=p.at("ve")):
                        p.wait(tensor, "ve", max(v_tq, fa))
                        for k8 in range(8):
                            ch = g4 * 8 + k8
                            i = tensor.transpose(ps_tr[:, k8 * 128:(k8 + 1) * 128],
                                                 tq[:, ch * 128:(ch + 1) * 128], identb[:])
                        p.bump("pe", i)
                    block.tensor(trq)
                    pv_tr = p.at("pe")

                    def evu(vector, pv=pv_tr):
                        p.wait(vector, "pe", pv)
                        i = vector.tensor_copy(tqT[:, 0:1024], ps_tr[:])
                        p.bump("ve", i)
                    block.vector(evu)
                    v_un = p.at("ve")

                    def mcb(tensor, g4=g4, v_un=v_un):
                        p.wait(tensor, "ve", v_un)
                        for k8 in range(8):
                            ch = g4 * 8 + k8
                            i = tensor.matmul(ps_cb[0:1, 0:128], ln2bB[:, ch:ch + 1],
                                              tqT[:, k8 * 128:(k8 + 1) * 128],
                                              start=(ch == 0), stop=(ch == 31),
                                              skip_group_check=True)
                        p.bump("pe", i)
                    block.tensor(mcb)
                    pv_cb = p.at("pe")

                    def evs(vector, c=c, g4=g4, pv=pv_cb):
                        p.wait(vector, "pe", pv)
                        for k8 in range(8):
                            ch = g4 * 8 + k8
                            i = vector.tensor_scalar_mul(
                                w2gT[:, ch * 1024 + c * 128: ch * 1024 + (c + 1) * 128],
                                tqT[:, k8 * 128:(k8 + 1) * 128], g2F[:, ch:ch + 1])
                        p.bump("ve", i)
                    block.vector(evs)
                tq_free_pe = p.at("pe")

                def evcb2(vector, c=c, fa=p.at("pe")):
                    p.wait(vector, "pe", fa)
                    i = vector.tensor_copy(cb2row[:, c * 128:(c + 1) * 128], ps_cb[0:1, 0:128])
                    p.bump("ve", i)
                block.vector(evcb2)

            def loadb2(sync, fa=p.at("ve")):
                p.wait(sync, "ve", fa)
                p.bump("sL", sync.dma_start(wstg[0][0:1, 0:D], B2R), 16)
            block.sync(loadb2)
            lvb2 = p.at("sL")

            def vfin2(vector, lvb2=lvb2):
                p.wait(vector, "sL", lvb2)
                vector.tensor_scalar_mul(cb2row[:, 0:D], cb2row[:, 0:D], m2s[0:1, 0:1])
                vector.tensor_scalar_mul(cb2row[:, 0:D], cb2row[:, 0:D], 0.5)
                i = vector.tensor_tensor(cb2row[:, 0:D], cb2row[:, 0:D],
                                         wstg[0][0:1, 0:D], ALU.add)
                p.bump("ve", i)
            block.vector(vfin2)
            vv = p.at("ve")

            def mmc2(tensor, vv=vv):
                p.wait(tensor, "ve", vv)
                tensor.matmul(ps_oc[0][:], onesrow[:, :], cb2row[:, 0:512],
                              start=True, stop=True)
                i = tensor.matmul(ps_oc[1][:], onesrow[:, :], cb2row[:, 512:1024],
                                  start=True, stop=True)
                p.bump("pe", i)
            block.tensor(mmc2)
            pv = p.at("pe")

            def evc2(vector, pv=pv):
                p.wait(vector, "pe", pv)
                vector.tensor_copy(c2b[:, 0:512], ps_oc[0][:])
                i = vector.tensor_copy(c2b[:, 512:1024], ps_oc[1][:])
                p.bump("ve", i)
            block.vector(evc2)

            if "qc" in taps:
                def vm(vector):
                    vector.tensor_copy(junk1k[0:1, 0:1], m1s[:, :])
                    vector.tensor_copy(junk1k[0:1, 2:3], im1[:, :])
                    vector.tensor_copy(junk1k[0:1, 3:4], thr1[0:1, 0:1])
                    vector.tensor_copy(junk1k[0:1, 4:5], wstg[0][0:1, 5:6])
                    vector.tensor_copy(junk1k[0:1, 5:6], cb1row[0:1, 5:6])
                    i = vector.tensor_copy(junk1k[0:1, 1:2], m2s[:, :])
                    p.bump("ve", i)
                block.vector(vm)

                def tqc(g):
                    p.wait(g, "ve", p.at("ve"))
                    p.bump("sG", g.dma_start(tap_aps["t_c1"], c1row[:]), 16)
                    p.bump("sG", g.dma_start(tap_aps["t_c2"], c2b[:]), 16)
                    p.bump("sG", g.dma_start(tap_aps["t_w1"], w1gT[:, 0:4096]), 16)
                    p.bump("sG", g.dma_start(tap_aps["t_w2"], w2gT[:, 0:1024]), 16)
                    p.bump("sG", g.dma_start(tap_aps["t_m"], junk1k[0:1, 0:8]), 16)
                block.gpsimd(tqc)
        st_pr2.close()

        st_pc = ExitStack()
        pc = st_pc.enter_context
        ps_o = [pc(nc.psum_tensor(f"ps_op{i}", [128, 512], F32)) for i in range(2)]
        hg = pc(nc.sbuf_tensor("hg", [128, HID], F32))
        hn = pc(nc.sbuf_tensor("hn", [128, HID], BF16))
        hnT = pc(nc.sbuf_tensor("hnT", [128, 32 * 128], BF16))
        hgs = pc(nc.sbuf_tensor("hgs", [128, 8], F32))
        hgss = pc(nc.sbuf_tensor("hgss", [128, 8], F32))
        x1b = [pc(nc.sbuf_tensor(f"x1b{i}", [128, 1024], F32)) for i in range(2)]
        outb = pc(nc.sbuf_tensor("outb", [128, 1024], F32))
        r2k2 = pc(nc.sbuf_tensor("r2k2", [128, 1], F32))
        xlnC = pc(nc.sbuf_tensor("xlnC", [128, 1024], BF16))
        xlnCF = pc(nc.sbuf_tensor("xlnCF", [128, 8 * 128], BF16))

        with nc.Block() as block:
            st_pc_state = {
                "x1_free_v": [p.at("ve")] * 2,
                "x1_free_s": [p.at("se")] * 2,
                "xlnC_free": p.at("pe"),
                "v_xf": p.at("ve"),
            }
            psA_free = [p.at("se")] * 2
            o_free = [p.at("ve")] * 2
            out_gate = p.at("sG")
            hn_free = p.at("pe")
            x1_store_all = p.at("sG")

            def emit_front(i):
                par = i % 2

                def loadx1(sync, i=i, par=par, fv=st_pc_state["x1_free_v"][par],
                           fs=st_pc_state["x1_free_s"][par], xs=x1_store_all):
                    p.wait(sync, "sG", xs)
                    p.wait(sync, "ve", fv)
                    p.wait(sync, "se", fs)
                    p.bump("sL", sync.dma_start(x1b[par][:],
                                                X1D.ap()[i * 128:(i + 1) * 128, :]), 16)
                block.sync(loadx1)
                lv = p.at("sL")

                def sap1(scalar, i=i, par=par, lv=lv, fa=st_pc_state["xlnC_free"]):
                    p.wait(scalar, "sL", lv)
                    p.wait(scalar, "pe", fa)
                    j = scalar.activation(xlnC[:], x1b[par][:], AF.Identity,
                                          bias=cC[:, i:i + 1], scale=aC[:, i:i + 1])
                    p.bump("se", j)
                block.scalar(sap1)
                s_ap = p.at("se")
                st_pc_state["x1_free_s"][par] = s_ap

                def trC(tensor, s_ap=s_ap, fa=p.at("ve")):
                    p.wait(tensor, "se", s_ap)
                    p.wait(tensor, "ve", fa)
                    for c in range(8):
                        j = tensor.transpose(ps_tr[:, c * 128:(c + 1) * 128],
                                             xlnC[:, c * 128:(c + 1) * 128], identb[:])
                    p.bump("pe", j)
                block.tensor(trC)
                pe_tr = p.at("pe")
                st_pc_state["xlnC_free"] = pe_tr

                def evtrC(vector, pe_tr=pe_tr):
                    p.wait(vector, "pe", pe_tr)
                    p.bump("ve", vector.tensor_copy(xlnCF[:, 0:1024], ps_tr[:]))
                block.vector(evtrC)
                st_pc_state["v_xf"] = p.at("ve")

            emit_front(0)
            for i in range(NT128):
                par = i % 2
                v_xf = st_pc_state["v_xf"]

                for n in range(8):
                    bank = n % 2

                    def mm1(tensor, n=n, bank=bank, v_xf=v_xf, fa=psA_free[bank]):
                        p.wait(tensor, "ve", v_xf)
                        p.wait(tensor, "se", fa)
                        for kc in range(8):
                            tensor.matmul(
                                ps_a[bank][:],
                                xlnCF[:, kc * 128:(kc + 1) * 128],
                                w1gT[:, kc * 4096 + n * 512: kc * 4096 + (n + 1) * 512],
                                start=(kc == 0), stop=False)
                        j = tensor.matmul(ps_a[bank][:], onesb[:, 0:128],
                                          c1row[:, n * 512:(n + 1) * 512],
                                          start=False, stop=True)
                        p.bump("pe", j)
                    block.tensor(mm1)
                    pv = p.at("pe")

                    def sg(scalar, n=n, bank=bank, pv=pv):
                        p.wait(scalar, "pe", pv)
                        j = scalar.activation(hg[:, n * 512:(n + 1) * 512], ps_a[bank][:],
                                              AF.Gelu, scale=k1b[:, 0:1],
                                              accum_out=hgs[:, n:n + 1])
                        p.bump("se", j)
                    block.scalar(sg)
                    psA_free[bank] = p.at("se")
                s_gelu = p.at("se")
                if i + 1 < NT128:
                    emit_front(i + 1)

                def vss(vector, s_gelu=s_gelu):
                    p.wait(vector, "se", s_gelu)
                    j = None
                    for n in range(8):
                        j = vector.scalar_tensor_tensor(junk1k[:, 0:512],
                                                        hg[:, n * 512:(n + 1) * 512], 1.0,
                                                        hg[:, n * 512:(n + 1) * 512],
                                                        ALU.mult, ALU.mult,
                                                        accum_out=hgss[:, n:n + 1])
                    p.bump("ve", j)
                block.vector(vss)
                v_acc = p.at("ve")

                def vss2(vector, v_acc=v_acc):
                    p.wait(vector, "ve", v_acc)
                    vector.tensor_reduce(s1t[:, 0:1], hgs[:], AX.X, ALU.add)
                    j = vector.tensor_reduce(s2t[:, 0:1], hgss[:], AX.X, ALU.add)
                    p.bump("ve", j)
                block.vector(vss2)

                def vss3(vector, fa=p.at("ve")):
                    p.wait(vector, "ve", fa)
                    vector.tensor_scalar_mul(mut[:, 0:1], s1t[:, 0:1], 1.0 / HID)
                    vector.tensor_tensor(tsc[:, 0:1], s1t[:, 0:1], s1t[:, 0:1], ALU.mult)
                    j = vector.tensor_scalar_mul(vart[:, 0:1], s2t[:, 0:1], 1.0 / HID)
                    p.bump("ve", j)
                block.vector(vss3)

                def vss4(vector, fa=p.at("ve")):
                    p.wait(vector, "ve", fa)
                    j = vector.scalar_tensor_tensor(tsc[:, 0:1], tsc[:, 0:1],
                                                    1.0 / (float(HID) * HID),
                                                    vart[:, 0:1], ALU.mult, ALU.subtract)
                    p.bump("ve", j)
                block.vector(vss4)

                def vss5(vector, fa=p.at("ve")):
                    p.wait(vector, "ve", fa)
                    vector.tensor_scalar(vart[:, 0:1], tsc[:, 0:1], -1.0, float(EPS),
                                         ALU.mult, ALU.add)
                    j = vector.tensor_scalar_mul(bt[:, 0:1], mut[:, 0:1], -1.0)
                    p.bump("ve", j)
                block.vector(vss5)
                v_st2 = p.at("ve")

                def sap2(scalar, v_st2=v_st2, hf=hn_free):
                    p.wait(scalar, "ve", v_st2)
                    p.wait(scalar, "pe", hf)
                    j = scalar.activation(hn[:], hg[:], AF.Identity, bias=bt[:, 0:1])
                    p.bump("se", j)
                block.scalar(sap2)
                s_hn = p.at("se")

                def sr2(scalar):
                    scalar.activation(rt[:, 0:1], vart[:, 0:1], AF.Ln)
                    j = scalar.activation(rt[:, 0:1], rt[:, 0:1], AF.Exp, scale=-0.5)
                    p.bump("se", j)
                block.scalar(sr2)
                s_r2 = p.at("se")

                def vr2(vector, s_r2=s_r2):
                    p.wait(vector, "se", s_r2)
                    j = vector.tensor_tensor(r2k2[:, 0:1], rt[:, 0:1], k2b[:, 0:1], ALU.mult)
                    p.bump("ve", j)
                block.vector(vr2)
                v_r2 = p.at("ve")

                for g4 in range(4):
                    pst = ps_tr if g4 % 2 == 0 else ps_tr2

                    def trh(tensor, g4=g4, pst=pst, s_hn=s_hn, fa=p.at("ve")):
                        p.wait(tensor, "se", s_hn)
                        p.wait(tensor, "ve", fa)
                        for k8 in range(8):
                            ch = g4 * 8 + k8
                            j = tensor.transpose(pst[:, k8 * 128:(k8 + 1) * 128],
                                                 hn[:, ch * 128:(ch + 1) * 128], identb[:])
                        p.bump("pe", j)
                    block.tensor(trh)
                    pvh = p.at("pe")

                    def evh(vector, g4=g4, pst=pst, pvh=pvh):
                        p.wait(vector, "pe", pvh)
                        p.bump("ve", vector.tensor_copy(hnT[:, g4 * 1024:(g4 + 1) * 1024],
                                                        pst[:]))
                    block.vector(evh)
                hnT_ready = p.at("ve")
                hn_free = p.at("pe")

                pv_o = [0, 0]
                for nh2 in range(2):
                    def mm2(tensor, nh2=nh2, hr=hnT_ready, fa=o_free[nh2]):
                        p.wait(tensor, "ve", max(hr, fa))
                        for ch in range(32):
                            j = tensor.matmul(
                                ps_o[nh2][:],
                                hnT[:, ch * 128:(ch + 1) * 128],
                                w2gT[:, ch * 1024 + nh2 * 512: ch * 1024 + (nh2 + 1) * 512],
                                start=(ch == 0), stop=(ch == 31))
                        p.bump("pe", j)
                    block.tensor(mm2)
                    pv_o[nh2] = p.at("pe")

                for nh2 in range(2):
                    def vep(vector, par=par, nh2=nh2, pv=pv_o[nh2], v_r2=v_r2, og=out_gate):
                        p.wait(vector, "pe", pv)
                        p.wait(vector, "ve", v_r2)
                        p.wait(vector, "sG", og)
                        vector.scalar_tensor_tensor(outb[:, nh2 * 512:(nh2 + 1) * 512],
                                                    ps_o[nh2][:], r2k2[:, 0:1],
                                                    x1b[par][:, nh2 * 512:(nh2 + 1) * 512],
                                                    ALU.mult, ALU.add)
                        j = vector.tensor_tensor(outb[:, nh2 * 512:(nh2 + 1) * 512],
                                                 outb[:, nh2 * 512:(nh2 + 1) * 512],
                                                 c2b[:, nh2 * 512:(nh2 + 1) * 512], ALU.add)
                        p.bump("ve", j)
                    block.vector(vep)
                    o_free[nh2] = p.at("ve")
                v_out = p.at("ve")
                st_pc_state["x1_free_v"][par] = v_out

                def gout(g, i=i, v_out=v_out):
                    p.wait(g, "ve", v_out)
                    p.bump("sG", g.dma_start(OUT[i * 128:(i + 1) * 128, :], outb[:]), 16)
                block.gpsimd(gout)
                out_gate = p.at("sG")
            if "pc" in taps:
                def tpc(g):
                    p.wait(g, "ve", p.at("ve"))
                    p.wait(g, "se", p.at("se"))
                    p.bump("sG", g.dma_start(tap_aps["t_xlnC"], xlnC[:]), 16)
                    p.bump("sG", g.dma_start(tap_aps["t_hg"], hg[:]), 16)
                    p.bump("sG", g.dma_start(tap_aps["t_hn"], hn[:]), 16)
                    p.bump("sG", g.dma_start(tap_aps["t_ac"][:, 0:NT128], aC[:]), 16)
                    p.bump("sG", g.dma_start(tap_aps["t_ac"][:, NT128:2 * NT128], cC[:]), 16)
                    p.bump("sG", g.dma_start(tap_aps["t_r2"], r2k2[:]), 16)
                    p.bump("sG", g.dma_start(tap_aps["t_hgs"], hgs[:]), 16)
                    p.bump("sG", g.dma_start(tap_aps["t_hgss"], hgss[:]), 16)
                block.gpsimd(tpc)
        st_pc.close()
        st_w.close()

        with nc.Block() as block:
            def _(g):
                p.wait(g, "sG", p.at("sG"))
            block.gpsimd(_)

    return nc, tap_aps


def _layout_inputs(L0_attn_x, L0_attn_w_qkv, L0_attn_proj_matrix, L0_attn_w_out,
                   L0_attn_b_out, L0_ffn_ln1_g, L0_ffn_ln1_b, L0_ffn_w1, L0_ffn_b1,
                   L0_ffn_ln2_g, L0_ffn_ln2_b, L0_ffn_w2, L0_ffn_b2):
    import ml_dtypes
    f32 = np.float32
    x = np.asarray(L0_attn_x, f32)
    wqkv = np.asarray(L0_attn_w_qkv, f32)
    proj = np.asarray(L0_attn_proj_matrix, f32)

    projp = np.zeros((8, 128, 512), f32)
    for j in range(8):
        projp[j, 0:64, 0:256] = proj[2 * j]
        projp[j, 64:128, 256:512] = proj[2 * j + 1]

    e16 = np.zeros((16, D), f32)
    for h in range(NH):
        e16[h, h * 64:(h + 1) * 64] = 1.0

    common = {
        "wq": np.ascontiguousarray(wqkv[:, 0:1024]),
        "wk": np.ascontiguousarray(wqkv[:, 1024:2048]),
        "wv": np.ascontiguousarray(wqkv[:, 2048:3072]),
        "projp": projp,
        "wout": np.ascontiguousarray(np.asarray(L0_attn_w_out, f32)),
        "boutrow": np.asarray(L0_attn_b_out, f32).reshape(1, D),
        "w1": np.ascontiguousarray(np.asarray(L0_ffn_w1, f32)),
        "w2": np.ascontiguousarray(np.asarray(L0_ffn_w2, f32)),
        "b1row": np.asarray(L0_ffn_b1, f32).reshape(1, HID),
        "b2row": np.asarray(L0_ffn_b2, f32).reshape(1, D),
        "g1F": np.ascontiguousarray(np.asarray(L0_ffn_ln1_g, f32).reshape(8, 128).T),
        "b1lnF": np.ascontiguousarray(np.asarray(L0_ffn_ln1_b, f32).reshape(8, 128).T),
        "g2F": np.ascontiguousarray(np.asarray(L0_ffn_ln2_g, f32).reshape(32, 128).T),
        "ln2bF": np.ascontiguousarray(np.asarray(L0_ffn_ln2_b, f32).reshape(32, 128).T),
        "identb": np.eye(128, dtype=f32).astype(ml_dtypes.bfloat16),
        "onesb": np.ones((1, 128), f32).astype(ml_dtypes.bfloat16),
        "onesf": np.ones((128, 1), f32),
        "onesrowf": np.ones((1, 128), f32),
        "e16": e16,
    }
    in_maps = []
    for core in range(8):
        b = core // 4
        r0 = (core % 4) * T
        m = dict(common)
        m["x"] = np.ascontiguousarray(x[b, r0:r0 + T, :])
        in_maps.append(m)
    return in_maps


_CACHED = {}


def kernel(**inputs):
    if "prog" not in _CACHED:
        _CACHED["prog"] = build_program()
    nc, _ = _CACHED["prog"]
    in_maps = _layout_inputs(**inputs)
    res = run_bass_kernel_spmd(nc, in_maps, core_ids=list(range(8)))
    out = np.empty((2, 8192, D), np.float32)
    for core in range(8):
        b = core // 4
        r0 = (core % 4) * T
        out[b, r0:r0 + T, :] = res.results[core]["out"]
    return out


# version 03 (final)
# speedup vs baseline: 1.0136x; 1.0061x over previous
import numpy as np
import concourse.bass as bass
import concourse.mybir as mybir
from concourse.bass_utils import run_bass_kernel_spmd
from contextlib import ExitStack

F32 = mybir.dt.float32
BF16 = mybir.dt.bfloat16
AF = mybir.ActivationFunctionType
ALU = mybir.AluOpType
AX = mybir.AxisListType

T = 2048
D = 1024
HID = 4096
NH = 16
FT = 256
EPS = 1e-5
ZEPS = 1e-8
NT128 = T // 128
NT512 = T // 512


class Prog:
    def __init__(self, nc, stack):
        self.nc = nc
        self._sems = {}
        self.cnt = {}
        self.stack = stack

    def sem(self, name):
        if name not in self._sems:
            self._sems[name] = self.stack.enter_context(self.nc.semaphore(name))
            self.cnt[name] = 0
        return self._sems[name]

    def bump(self, name, inst, n=1):
        inst.then_inc(self.sem(name), n)
        self.cnt[name] += n
        return self.cnt[name]

    def at(self, name):
        self.sem(name)
        return self.cnt[name]

    def wait(self, eng, name, v=None):
        if v is None:
            v = self.cnt[name]
        if v > 0:
            eng.wait_ge(self.sem(name), v)


def build_program(taps=(), stage=99):
    nc = bass.Bass("TRN2", target_bir_lowering=False, detect_race_conditions=False)
    taps = set(taps)

    X = nc.dram_tensor("x", [T, D], F32, kind="ExternalInput").ap()
    WQ = nc.dram_tensor("wq", [D, D], F32, kind="ExternalInput").ap()
    WK = nc.dram_tensor("wk", [D, D], F32, kind="ExternalInput").ap()
    WV = nc.dram_tensor("wv", [D, D], F32, kind="ExternalInput").ap()
    PROJP = nc.dram_tensor("projp", [8, 128, 512], F32, kind="ExternalInput").ap()
    WOUT = nc.dram_tensor("wout", [D, D], F32, kind="ExternalInput").ap()
    BOUTR = nc.dram_tensor("boutrow", [1, D], F32, kind="ExternalInput").ap()
    W1 = nc.dram_tensor("w1", [HID, D], F32, kind="ExternalInput").ap()
    W2 = nc.dram_tensor("w2", [D, HID], F32, kind="ExternalInput").ap()
    B1R = nc.dram_tensor("b1row", [1, HID], F32, kind="ExternalInput").ap()
    B2R = nc.dram_tensor("b2row", [1, D], F32, kind="ExternalInput").ap()
    G1F = nc.dram_tensor("g1F", [128, 8], F32, kind="ExternalInput").ap()
    B1LNF = nc.dram_tensor("b1lnF", [128, 8], F32, kind="ExternalInput").ap()
    G2F = nc.dram_tensor("g2F", [128, 32], F32, kind="ExternalInput").ap()
    LN2BF = nc.dram_tensor("ln2bF", [128, 32], F32, kind="ExternalInput").ap()
    IDENTB = nc.dram_tensor("identb", [128, 128], BF16, kind="ExternalInput").ap()
    ONESB = nc.dram_tensor("onesb", [1, 128], BF16, kind="ExternalInput").ap()
    ONESF = nc.dram_tensor("onesf", [128, 1], F32, kind="ExternalInput").ap()
    ONESROWF = nc.dram_tensor("onesrowf", [1, 128], F32, kind="ExternalInput").ap()
    E16 = nc.dram_tensor("e16", [16, D], F32, kind="ExternalInput").ap()
    OUT = nc.dram_tensor("out", [T, D], F32, kind="ExternalOutput").ap()

    X1D = nc.dram_tensor("x1d", [T, D], F32)
    cc_in = nc.dram_tensor("cc_in", [65, NH * FT], F32)
    cc_out = nc.dram_tensor("cc_out", [65, NH * FT], F32)

    tap_aps = {}
    if "kv" in taps:
        tap_aps["t_kv"] = nc.dram_tensor("t_kv", [65, NH * FT], F32, kind="ExternalOutput").ap()
    if "kvloc" in taps:
        tap_aps["t_kvloc"] = nc.dram_tensor("t_kvloc", [65, NH * FT], F32, kind="ExternalOutput").ap()
    if "x1" in taps:
        tap_aps["t_x1"] = nc.dram_tensor("t_x1", [T, D], F32, kind="ExternalOutput").ap()
    if "kp" in taps:
        tap_aps["t_xt"] = nc.dram_tensor("t_xt", [128, 1024], BF16, kind="ExternalOutput").ap()
        tap_aps["t_st"] = nc.dram_tensor("t_st", [128, 8], F32, kind="ExternalOutput").ap()
        tap_aps["t_kp0"] = nc.dram_tensor("t_kp0", [128, NH * FT], BF16, kind="ExternalOutput").ap()
        tap_aps["t_kp1"] = nc.dram_tensor("t_kp1", [128, NH * FT], BF16, kind="ExternalOutput").ap()
        tap_aps["t_kF"] = nc.dram_tensor("t_kF", [128, 8 * 512], BF16, kind="ExternalOutput").ap()
        tap_aps["t_va0"] = nc.dram_tensor("t_va0", [128, NH * 66], BF16, kind="ExternalOutput").ap()
        tap_aps["t_va1"] = nc.dram_tensor("t_va1", [128, NH * 66], BF16, kind="ExternalOutput").ap()
        tap_aps["t_xf"] = nc.dram_tensor("t_xf", [128, 8 * 512], BF16, kind="ExternalOutput").ap()
    if "pc" in taps:
        tap_aps["t_xlnC"] = nc.dram_tensor("t_xlnC", [128, D], BF16, kind="ExternalOutput").ap()
        tap_aps["t_hg"] = nc.dram_tensor("t_hg", [128, HID], F32, kind="ExternalOutput").ap()
        tap_aps["t_hn"] = nc.dram_tensor("t_hn", [128, HID], BF16, kind="ExternalOutput").ap()
        tap_aps["t_ac"] = nc.dram_tensor("t_ac", [128, 2 * NT128], F32, kind="ExternalOutput").ap()
        tap_aps["t_r2"] = nc.dram_tensor("t_r2", [128, 1], F32, kind="ExternalOutput").ap()
        tap_aps["t_hgs"] = nc.dram_tensor("t_hgs", [128, 8], F32, kind="ExternalOutput").ap()
        tap_aps["t_hgss"] = nc.dram_tensor("t_hgss", [128, 8], F32, kind="ExternalOutput").ap()
    if "qc" in taps:
        tap_aps["t_m"] = nc.dram_tensor("t_m", [1, 8], F32, kind="ExternalOutput").ap()
        tap_aps["t_c1"] = nc.dram_tensor("t_c1", [1, HID], BF16, kind="ExternalOutput").ap()
        tap_aps["t_c2"] = nc.dram_tensor("t_c2", [128, D], F32, kind="ExternalOutput").ap()
        tap_aps["t_w1"] = nc.dram_tensor("t_w1", [128, HID], BF16, kind="ExternalOutput").ap()
        tap_aps["t_w2"] = nc.dram_tensor("t_w2", [128, D], BF16, kind="ExternalOutput").ap()

    st = ExitStack()
    with st:
        en = st.enter_context
        p = Prog(nc, st)

        identb = en(nc.sbuf_tensor("identb_s", [128, 128], BF16))
        onesb = en(nc.sbuf_tensor("onesb_s", [1, 128], BF16))
        onescol = en(nc.sbuf_tensor("onescol_s", [128, 1], F32))
        onesrow = en(nc.sbuf_tensor("onesrow_s", [1, 128], F32))
        g1F = en(nc.sbuf_tensor("g1F_s", [128, 8], F32))
        g2F = en(nc.sbuf_tensor("g2F_s", [128, 32], F32))
        b1lnB = en(nc.sbuf_tensor("b1lnB_s", [128, 8], BF16))
        ln2bB = en(nc.sbuf_tensor("ln2bB_s", [128, 32], BF16))
        thr1 = en(nc.sbuf_tensor("thr1", [128, 1], F32))
        nthr1 = en(nc.sbuf_tensor("nthr1", [128, 1], F32))
        thr2 = en(nc.sbuf_tensor("thr2", [128, 1], F32))
        nthr2 = en(nc.sbuf_tensor("nthr2", [128, 1], F32))
        k1b = en(nc.sbuf_tensor("k1b", [128, 1], F32))
        k2b = en(nc.sbuf_tensor("k2b", [128, 1], F32))
        m1s = en(nc.sbuf_tensor("m1s", [1, 1], F32))
        m2s = en(nc.sbuf_tensor("m2s", [1, 1], F32))
        im1 = en(nc.sbuf_tensor("im1", [1, 1], F32))
        aC = en(nc.sbuf_tensor("aC", [128, NT128], F32))
        cC = en(nc.sbuf_tensor("cC", [128, NT128], F32))
        junk1k = en(nc.sbuf_tensor("junk1k", [128, 1024], F32))
        s1t = en(nc.sbuf_tensor("s1t", [128, 1], F32))
        s2t = en(nc.sbuf_tensor("s2t", [128, 1], F32))
        mut = en(nc.sbuf_tensor("mut", [128, 1], F32))
        vart = en(nc.sbuf_tensor("vart", [128, 1], F32))
        rt = en(nc.sbuf_tensor("rt", [128, 1], F32))
        bt = en(nc.sbuf_tensor("bt", [128, 1], F32))
        tsc = en(nc.sbuf_tensor("tsc", [128, 1], F32))

        ps_a = [en(nc.psum_tensor(f"ps_a{i}", [128, 512], F32)) for i in range(2)]
        ps_tr = en(nc.psum_tensor("ps_tr", [128, 1024], BF16))
        ps_tr2 = en(nc.psum_tensor("ps_tr2", [128, 1024], BF16))

        st_ab = ExitStack()
        ab = st_ab.enter_context
        e16 = ab(nc.sbuf_tensor("e16_s", [16, D], F32))
        projp = ab(nc.sbuf_tensor("projp_s", [128, 8 * 512], BF16))
        woutB = ab(nc.sbuf_tensor("wout_s", [128, 8 * 1024], BF16))
        wqB = ab(nc.sbuf_tensor("wq_s", [128, 8 * 1024], BF16))
        boutr = ab(nc.sbuf_tensor("boutr_s", [1, D], BF16))
        kvT = ab(nc.sbuf_tensor("kvT", [128, 32 * 65], BF16))
        ksblk = ab(nc.sbuf_tensor("ksblk", [128, 32 * 16], BF16))
        xlnT = [ab(nc.sbuf_tensor(f"xlnT{i}", [128, 1024], BF16)) for i in range(2)]
        xlnF = ab(nc.sbuf_tensor("xlnF", [128, 8 * 512], BF16))
        ebuf = ab(nc.sbuf_tensor("ebuf", [128, 512], F32))
        mbuf = ab(nc.sbuf_tensor("mbuf", [128, 512], F32))
        rbuf = ab(nc.sbuf_tensor("rbuf", [128, 512], F32))

        def emit_ln_chain(block, v_s1, s_s2, n=1024.0):
            def vc1(vector):
                p.wait(vector, "se", s_s2)
                p.wait(vector, "ve", v_s1)
                vector.tensor_scalar_mul(mut[:, 0:1], s1t[:, 0:1], 1.0 / n)
                vector.tensor_tensor(tsc[:, 0:1], s1t[:, 0:1], s1t[:, 0:1], ALU.mult)
                i = vector.tensor_scalar_mul(vart[:, 0:1], s2t[:, 0:1], 1.0 / n)
                p.bump("ve", i)
            block.vector(vc1)

            def vc2(vector, fa=p.at("ve")):
                p.wait(vector, "ve", fa)
                i = vector.scalar_tensor_tensor(tsc[:, 0:1], tsc[:, 0:1], 1.0 / (n * n),
                                                vart[:, 0:1], ALU.mult, ALU.subtract)
                p.bump("ve", i)
            block.vector(vc2)

            def vc3(vector, fa=p.at("ve")):
                p.wait(vector, "ve", fa)
                i = vector.tensor_scalar(vart[:, 0:1], tsc[:, 0:1], -1.0, float(EPS),
                                         ALU.mult, ALU.add)
                p.bump("ve", i)
            block.vector(vc3)
            vv = p.at("ve")

            def sc(scalar):
                p.wait(scalar, "ve", vv)
                scalar.activation(rt[:, 0:1], vart[:, 0:1], AF.Ln)
                i = scalar.activation(rt[:, 0:1], rt[:, 0:1], AF.Exp, scale=-0.5)
                p.bump("se", i)
            block.scalar(sc)
            sv = p.at("se")

            def vb(vector):
                p.wait(vector, "se", sv)
                i = vector.tensor_scalar(bt[:, 0:1], mut[:, 0:1], -1.0, rt[:, 0:1],
                                         ALU.mult, ALU.mult)
                p.bump("ve", i)
            block.vector(vb)
            return p.at("ve")

        st_a = ExitStack()
        a_ = st_a.enter_context
        wkB = a_(nc.sbuf_tensor("wk_s", [128, 8 * 1024], BF16))
        wvB = a_(nc.sbuf_tensor("wv_s", [128, 8 * 1024], BF16))
        kv_sb = a_(nc.sbuf_tensor("kv_sb", [65, NH * FT], F32))
        kF = a_(nc.sbuf_tensor("kF", [128, 8 * 512], BF16))
        vaug = [a_(nc.sbuf_tensor(f"vaug{s}", [128, NH * 66], BF16)) for s in range(2)]
        kpT = [a_(nc.sbuf_tensor(f"kpT{s}", [128, NH * FT], BF16)) for s in range(2)]
        xbuf = [a_(nc.sbuf_tensor(f"xbuf{i}", [128, 1024], F32)) for i in range(2)]

        st_p0 = ExitStack()
        stg0 = [st_p0.enter_context(nc.sbuf_tensor(f"stg0_{i}", [128, 1024], F32))
                for i in range(2)]
        with nc.Block() as block:
            def _(sync):
                for ap_d, ap_s in [(identb[:], IDENTB), (onesb[:], ONESB),
                                   (onescol[:], ONESF), (onesrow[:], ONESROWF),
                                   (e16[:], E16), (g1F[:], G1F), (g2F[:], G2F)]:
                    p.bump("sL", sync.dma_start(ap_d, ap_s), 16)
                p.bump("sL", sync.dma_start(junk1k[0:128, 0:8], B1LNF), 16)
                p.bump("sL", sync.dma_start(junk1k[0:128, 8:40], LN2BF), 16)
                p.bump("sL", sync.dma_start(stg0[0][0:1, 0:D], BOUTR), 16)
            block.sync(_)
            lv = p.at("sL")

            def _(vector):
                p.wait(vector, "sL", lv)
                vector.tensor_copy(b1lnB[:], junk1k[:, 0:8])
                vector.tensor_copy(ln2bB[:], junk1k[:, 8:40])
                i = vector.tensor_copy(boutr[:], stg0[0][0:1, 0:D])
                p.bump("ve", i)
            block.vector(_)

        with nc.Block() as block:
            seq = ([("wq", c, WQ, wqB, 1024) for c in range(8)] +
                   [("wk", c, WK, wkB, 1024) for c in range(8)] +
                   [("wv", c, WV, wvB, 1024) for c in range(8)] +
                   [("wo", c, WOUT, woutB, 1024) for c in range(8)] +
                   [("pp", c, None, projp, 512) for c in range(8)])
            free_at = [p.at("ve"), p.at("ve")]
            for idx, (kind, c, src, dst, ncol) in enumerate(seq):
                par = idx % 2
                buf = stg0[par]
                src_ap = PROJP[c] if kind == "pp" else src[c * 128:(c + 1) * 128, :]

                def load(sync, buf=buf, src_ap=src_ap, ncol=ncol, fa=free_at[par]):
                    p.wait(sync, "ve", fa)
                    p.bump("sL", sync.dma_start(buf[:, 0:ncol], src_ap), 16)
                block.sync(load)
                lv = p.at("sL")

                def cast(vector, buf=buf, dst=dst, c=c, ncol=ncol, lv=lv):
                    p.wait(vector, "sL", lv)
                    p.bump("ve", vector.tensor_copy(dst[:, c * ncol:(c + 1) * ncol],
                                                    buf[:, 0:ncol]))
                block.vector(cast)
                free_at[par] = p.at("ve")
        st_p0.close()

        st_pa = ExitStack()
        ps_kv = [st_pa.enter_context(nc.psum_tensor(f"ps_kv{i}", [65, 512], F32))
                 for i in range(4)]
        with nc.Block() as block:
            def _(vector):
                for s in range(2):
                    va = vaug[s][:].rearrange("p (h c) -> p h c", c=66)
                    vector.memset(va[:, :, 64:66], 1.0)
                i = vector.memset(ksblk[:], 0.0)
                p.bump("ve", i)
            block.vector(_)

            x_free_v = [p.at("ve")] * 2
            x_free_s = [p.at("se")] * 2
            psA_free_v = [p.at("ve")] * 2
            psA_free_s = [p.at("se")] * 2
            tr_free = p.at("ve")
            kv_free = [p.at("ve")] * 4
            prev_pe = p.at("pe")
            for t512 in range(NT512):
                for s in range(4):
                    i128 = t512 * 4 + s
                    par = i128 % 2

                    def load_x(sync, i128=i128, par=par, fv=x_free_v[par], fs=x_free_s[par]):
                        p.wait(sync, "ve", fv)
                        p.wait(sync, "se", fs)
                        p.bump("sL", sync.dma_start(xbuf[par][:],
                                                    X[i128 * 128:(i128 + 1) * 128, :]), 16)
                    block.sync(load_x)
                    lv = p.at("sL")

                    def vs(vector, par=par, lv=lv):
                        p.wait(vector, "sL", lv)
                        p.bump("ve", vector.tensor_reduce(s1t[:, 0:1], xbuf[par][:],
                                                          AX.X, ALU.add))
                    block.vector(vs)
                    v_s1 = p.at("ve")

                    def ss(scalar, par=par, lv=lv):
                        p.wait(scalar, "sL", lv)
                        p.bump("se", scalar.activation(junk1k[:, :], xbuf[par][:], AF.Square,
                                                       accum_out=s2t[:, 0:1]))
                    block.scalar(ss)
                    s_s2 = p.at("se")
                    v_bt = emit_ln_chain(block, v_s1, s_s2)

                    def sa(scalar, par=par, v_bt=v_bt, pef=prev_pe):
                        p.wait(scalar, "ve", v_bt)
                        p.wait(scalar, "pe", pef)
                        i = scalar.activation(xlnT[par][:], xbuf[par][:], AF.Identity,
                                              bias=bt[:, 0:1], scale=rt[:, 0:1])
                        p.bump("se", i)
                    block.scalar(sa)
                    s_ap = p.at("se")
                    x_free_v[par] = v_s1
                    x_free_s[par] = s_ap

                    def trs(tensor, par=par, s_ap=s_ap, fa=tr_free):
                        p.wait(tensor, "se", s_ap)
                        p.wait(tensor, "ve", fa)
                        for c in range(8):
                            i = tensor.transpose(ps_tr[:, c * 128:(c + 1) * 128],
                                                 xlnT[par][:, c * 128:(c + 1) * 128], identb[:])
                        p.bump("pe", i)
                    block.tensor(trs)
                    pe_tr = p.at("pe")

                    def evtr(vector, s=s, pe_tr=pe_tr):
                        p.wait(vector, "pe", pe_tr)
                        src = ps_tr[:].rearrange("p (c x) -> p c x", x=128)
                        dst = xlnF[:].rearrange("p (c x) -> p c x", x=512)[:, :, s * 128:(s + 1) * 128]
                        p.bump("ve", vector.tensor_copy(dst, src))
                    block.vector(evtr)
                    tr_free = p.at("ve")

                xlnF_ready = p.at("ve")
                for m in range(8):
                    bank = m % 2

                    def mmk(tensor, m=m, bank=bank, fv=psA_free_v[bank], fs=psA_free_s[bank],
                            xr=xlnF_ready):
                        p.wait(tensor, "ve", max(fv, xr))
                        p.wait(tensor, "se", fs)
                        for kc in range(8):
                            i = tensor.matmul(ps_a[bank][:],
                                              wkB[:, kc * 1024 + m * 128: kc * 1024 + (m + 1) * 128],
                                              xlnF[:, kc * 512:(kc + 1) * 512],
                                              start=(kc == 0), stop=(kc == 7))
                        p.bump("pe", i)
                    block.tensor(mmk)
                    pv = p.at("pe")

                    def evk(vector, m=m, bank=bank, pv=pv):
                        p.wait(vector, "pe", pv)
                        p.bump("ve", vector.tensor_copy(kF[:, m * 512:(m + 1) * 512],
                                                        ps_a[bank][:]))
                    block.vector(evk)
                    psA_free_v[bank] = p.at("ve")
                kF_ready = p.at("ve")

                for half in range(2):
                    for s2 in range(2):
                        s = half * 2 + s2
                        for nh2 in range(2):
                            bank = (s2 * 2 + nh2) % 2

                            def mmv(tensor, s=s, nh2=nh2, bank=bank,
                                    fv=psA_free_v[bank], fs=psA_free_s[bank]):
                                p.wait(tensor, "ve", fv)
                                p.wait(tensor, "se", fs)
                                for kc in range(8):
                                    i = tensor.matmul(
                                        ps_a[bank][:],
                                        xlnF[:, kc * 512 + s * 128: kc * 512 + (s + 1) * 128],
                                        wvB[:, kc * 1024 + nh2 * 512: kc * 1024 + (nh2 + 1) * 512],
                                        start=(kc == 0), stop=(kc == 7))
                                p.bump("pe", i)
                            block.tensor(mmv)
                            pv = p.at("pe")

                            def evv(vector, s2=s2, nh2=nh2, bank=bank, pv=pv):
                                p.wait(vector, "pe", pv)
                                src = ps_a[bank][:].rearrange("p (h x) -> p h x", x=64)
                                dst = vaug[s2][:].rearrange("p (h c) -> p h c", c=66)[:, nh2 * 8:(nh2 + 1) * 8, 0:64]
                                p.bump("ve", vector.tensor_copy(dst, src))
                            block.vector(evv)
                            psA_free_v[bank] = p.at("ve")

                    for j in range(8):
                        for s2 in range(2):
                            s = half * 2 + s2
                            bank = (j * 2 + s2) % 2

                            def mmkp(tensor, j=j, s=s, bank=bank, fv=psA_free_v[bank],
                                     fs=psA_free_s[bank], kr=kF_ready):
                                p.wait(tensor, "ve", max(fv, kr))
                                p.wait(tensor, "se", fs)
                                i = tensor.matmul(ps_a[bank][:],
                                                  kF[:, j * 512 + s * 128: j * 512 + (s + 1) * 128],
                                                  projp[:, j * 512:(j + 1) * 512],
                                                  start=True, stop=True)
                                p.bump("pe", i)
                            block.tensor(mmkp)
                            pv = p.at("pe")

                            def vmin(vector, bank=bank, pv=pv):
                                p.wait(vector, "pe", pv)
                                p.bump("ve", vector.tensor_scalar_min(mbuf[:], ps_a[bank][:], 0.0))
                            block.vector(vmin)
                            v_m = p.at("ve")

                            def srel(scalar, bank=bank, pv=pv, v_m=v_m):
                                p.wait(scalar, "pe", pv)
                                p.wait(scalar, "ve", v_m)
                                scalar.activation(rbuf[:], ps_a[bank][:], AF.Relu)
                                i = scalar.activation(ebuf[:], mbuf[:], AF.Exp)
                                p.bump("se", i)
                            block.scalar(srel)
                            s_re = p.at("se")

                            def vadd(vector, j=j, s2=s2, s_re=s_re):
                                p.wait(vector, "se", s_re)
                                i = vector.tensor_tensor(kpT[s2][:, j * 512:(j + 1) * 512],
                                                         rbuf[:], ebuf[:], ALU.add)
                                p.bump("ve", i)
                            block.vector(vadd)
                            psA_free_v[bank] = v_m
                            psA_free_s[bank] = s_re

                    kp_ready = p.at("ve")
                    for g in range(2):
                        for jj in range(4):
                            j = g * 4 + jj

                            def mmkv(tensor, j=j, jj=jj, fa=kv_free[jj], kr=kp_ready):
                                p.wait(tensor, "ve", max(fa, kr))
                                for hh in range(2):
                                    h = 2 * j + hh
                                    for s2 in range(2):
                                        i = tensor.matmul(
                                            ps_kv[jj][:, hh * 256:(hh + 1) * 256],
                                            vaug[s2][:, h * 66: h * 66 + 65],
                                            kpT[s2][:, h * 256:(h + 1) * 256],
                                            start=(s2 == 0), stop=(s2 == 1),
                                            skip_group_check=True)
                                p.bump("pe", i)
                            block.tensor(mmkv)
                            pv = p.at("pe")

                            def vkv(vector, j=j, jj=jj, first=(t512 == 0 and half == 0), pv=pv):
                                p.wait(vector, "pe", pv)
                                if first:
                                    i = vector.tensor_copy(kv_sb[:, j * 512:(j + 1) * 512],
                                                           ps_kv[jj][:])
                                else:
                                    i = vector.tensor_tensor(kv_sb[:, j * 512:(j + 1) * 512],
                                                             kv_sb[:, j * 512:(j + 1) * 512],
                                                             ps_kv[jj][:], ALU.add)
                                p.bump("ve", i)
                            block.vector(vkv)
                            kv_free[jj] = p.at("ve")
                prev_pe = p.at("pe")

            if "kp" in taps:
                def vstt(vector):
                    vector.tensor_copy(junk1k[:, 0:1], s1t[:, 0:1])
                    vector.tensor_copy(junk1k[:, 1:2], s2t[:, 0:1])
                    vector.tensor_copy(junk1k[:, 2:3], mut[:, 0:1])
                    vector.tensor_copy(junk1k[:, 3:4], vart[:, 0:1])
                    vector.tensor_copy(junk1k[:, 4:5], rt[:, 0:1])
                    i = vector.tensor_copy(junk1k[:, 5:6], bt[:, 0:1])
                    p.bump("ve", i)
                block.vector(vstt)

                def tkp(g):
                    p.wait(g, "ve", p.at("ve"))
                    p.bump("sG", g.dma_start(tap_aps["t_xt"], xlnT[1][:]), 16)
                    p.bump("sG", g.dma_start(tap_aps["t_st"], junk1k[:, 0:8]), 16)
                    p.bump("sG", g.dma_start(tap_aps["t_kp0"], kpT[0][:]), 16)
                    p.bump("sG", g.dma_start(tap_aps["t_kp1"], kpT[1][:]), 16)
                    p.bump("sG", g.dma_start(tap_aps["t_kF"], kF[:]), 16)
                    p.bump("sG", g.dma_start(tap_aps["t_va0"], vaug[0][:]), 16)
                    p.bump("sG", g.dma_start(tap_aps["t_va1"], vaug[1][:]), 16)
                    p.bump("sG", g.dma_start(tap_aps["t_xf"], xlnF[:]), 16)
                    p.wait(g, "sG", p.at("sG"))
                block.gpsimd(tkp)

        kv_done_v = p.at("ve")
        with nc.Block() as block:
            def _(g):
                p.wait(g, "ve", kv_done_v)
                p.bump("sG", g.dma_start(cc_in.ap(), kv_sb[:]), 16)
                p.wait(g, "sG", p.at("sG"))
                i = g.collective_compute(
                    "AllReduce", ALU.add,
                    replica_groups=[[0, 1, 2, 3], [4, 5, 6, 7]],
                    ins=[cc_in.ap().opt()],
                    outs=[cc_out.ap().opt()])
                p.bump("cc", i)
            block.gpsimd(_)
            if "kvloc" in taps:
                def _(g):
                    p.bump("sG", g.dma_start(tap_aps["t_kvloc"], kv_sb[:]), 16)
                block.gpsimd(_)
        st_pa.close()
        st_a.close()

        st_pr1 = ExitStack()
        wstg = [st_pr1.enter_context(nc.sbuf_tensor(f"wstg{i}", [128, 2048], F32))
                for i in range(2)]
        acc80 = st_pr1.enter_context(nc.sbuf_tensor("acc80", [128, 80], F32))
        ps_sc = st_pr1.enter_context(nc.psum_tensor("ps_sc", [128, 16], F32))
        with nc.Block() as block:
            cols = ([("w1", c, 0) for c in range(32)] +
                    [("w2", c, hc) for c in range(8) for hc in range(2)])
            free_at = [p.at("ve"), p.at("ve")]
            for idx, (kind, c, hc) in enumerate(cols):
                par = idx % 2
                ncol = 1024 if kind == "w1" else 2048
                src = (W1[c * 128:(c + 1) * 128, :] if kind == "w1"
                       else W2[c * 128:(c + 1) * 128, hc * 2048:(hc + 1) * 2048])

                def load(sync, src=src, ncol=ncol, par=par, fa=free_at[par]):
                    p.wait(sync, "ve", fa)
                    p.bump("sL", sync.dma_start(wstg[par][:, 0:ncol], src), 16)
                block.sync(load)
                lv = p.at("sL")

                def red(vector, idx=idx, ncol=ncol, par=par, lv=lv):
                    p.wait(vector, "sL", lv)
                    i = vector.tensor_reduce(acc80[:, idx:idx + 1], wstg[par][:, 0:ncol],
                                             AX.X, ALU.add, apply_absolute_value=True)
                    p.bump("ve", i)
                block.vector(red)
                free_at[par] = p.at("ve")

            def vtot(vector, fa=p.at("ve")):
                p.wait(vector, "ve", fa)
                vector.tensor_reduce(acc80[:, 48:49], acc80[:, 0:32], AX.X, ALU.add)
                i = vector.tensor_reduce(acc80[:, 49:50], acc80[:, 32:48], AX.X, ALU.add)
                p.bump("ve", i)
            block.vector(vtot)

            def vtot2(vector, fa=p.at("ve")):
                p.wait(vector, "ve", fa)
                vector.tensor_copy(acc80[:, 0:1], acc80[:, 48:49])
                i = vector.tensor_copy(acc80[:, 1:2], acc80[:, 49:50])
                p.bump("ve", i)
            block.vector(vtot2)
            vv = p.at("ve")

            def mm(tensor, vv=vv):
                p.wait(tensor, "ve", vv)
                i = tensor.matmul(ps_sc[0:1, 0:2], onescol[:, 0:1], acc80[:, 0:2],
                                  start=True, stop=True)
                p.bump("pe", i)
            block.tensor(mm)
            pv = p.at("pe")

            def vsc(vector, pv=pv):
                p.wait(vector, "pe", pv)
                vector.tensor_scalar(m1s[:, 0:1], ps_sc[0:1, 0:1], 1.0 / (HID * D), 1e-5,
                                     ALU.mult, ALU.max)
                i = vector.tensor_scalar(m2s[:, 0:1], ps_sc[0:1, 1:2], 1.0 / (HID * D), 1e-5,
                                         ALU.mult, ALU.max)
                p.bump("ve", i)
            block.vector(vsc)

            def vsc2(vector, fa=p.at("ve")):
                p.wait(vector, "ve", fa)
                i = vector.reciprocal(im1[:, 0:1], m1s[:, 0:1])
                p.bump("ve", i)
            block.vector(vsc2)

            def vsc3(vector, fa=p.at("ve")):
                p.wait(vector, "ve", fa)
                i = vector.tensor_scalar_mul(im1[:, 0:1], im1[:, 0:1], 2.0)
                p.bump("ve", i)
            block.vector(vsc3)
            vv2 = p.at("ve")

            def mb(tensor, vv2=vv2):
                p.wait(tensor, "ve", vv2)
                tensor.matmul(ps_sc[:, 2:3], onesrow[:, :], m1s[:, 0:1], start=True, stop=True)
                i = tensor.matmul(ps_sc[:, 3:4], onesrow[:, :], m2s[:, 0:1], start=True, stop=True)
                p.bump("pe", i)
            block.tensor(mb)
            pv2 = p.at("pe")

            def vbb(vector, pv2=pv2):
                p.wait(vector, "pe", pv2)
                vector.tensor_scalar_mul(thr1[:, 0:1], ps_sc[:, 2:3], 0.5)
                vector.tensor_scalar_mul(nthr1[:, 0:1], ps_sc[:, 2:3], -0.5)
                vector.tensor_scalar_mul(k1b[:, 0:1], ps_sc[:, 2:3], 0.5)
                vector.tensor_scalar_mul(thr2[:, 0:1], ps_sc[:, 3:4], 0.5)
                vector.tensor_scalar_mul(nthr2[:, 0:1], ps_sc[:, 3:4], -0.5)
                i = vector.tensor_scalar_mul(k2b[:, 0:1], ps_sc[:, 3:4], 0.5)
                p.bump("ve", i)
            block.vector(vbb)
        st_pr1.close()

        st_kt = ExitStack()
        kv_red = st_kt.enter_context(nc.sbuf_tensor("kv_red", [65, NH * FT], F32))
        kv_redb = st_kt.enter_context(nc.sbuf_tensor("kv_redb", [65, NH * FT], BF16))
        with nc.Block() as block:
            def _(scalar):
                p.wait(scalar, "cc", 1)
                p.bump("sS", scalar.dma_start(kv_red[:], cc_out.ap()), 16)
            block.scalar(_)
            lv = p.at("sS")
            if "kv" in taps:
                def _(g, lv=lv):
                    p.wait(g, "sS", lv)
                    p.bump("sG", g.dma_start(tap_aps["t_kv"], kv_red[:]), 16)
                block.gpsimd(_)

            def vc(vector, lv=lv):
                p.wait(vector, "sS", lv)
                p.bump("ve", vector.tensor_copy(kv_redb[:], kv_red[:]))
            block.vector(vc)
            vv = p.at("ve")

            ev_free = [p.at("ve")] * 2
            for c in range(32):
                slot = c % 2
                pst = ps_tr if slot == 0 else ps_tr2

                def tr1(tensor, c=c, pst=pst, vv=vv, fa=ev_free[slot]):
                    p.wait(tensor, "ve", max(vv, fa))
                    i = tensor.transpose(pst[:, 0:65], kv_redb[:, c * 128:(c + 1) * 128],
                                         identb[0:65, 0:65])
                    p.bump("pe", i)
                block.tensor(tr1)
                pv = p.at("pe")

                def ev1(vector, c=c, pst=pst, pv=pv):
                    p.wait(vector, "pe", pv)
                    vector.tensor_copy(kvT[:, c * 65:(c + 1) * 65], pst[:, 0:65])
                    i = vector.tensor_copy(ksblk[:, c * 16 + c // 2: c * 16 + c // 2 + 1],
                                           pst[:, 64:65])
                    p.bump("ve", i)
                block.vector(ev1)
                ev_free[slot] = p.at("ve")
        st_kt.close()

        if stage < 2:
            with nc.Block() as block:
                def _(g):
                    p.wait(g, "ve", p.at("ve"))
                    p.wait(g, "sG", p.at("sG"))
                block.gpsimd(_)
            st_ab.close()
            return nc, tap_aps

        st_pb = ExitStack()
        pb = st_pb.enter_context
        qF = pb(nc.sbuf_tensor("qF", [128, 8 * 512], BF16))
        qpF = pb(nc.sbuf_tensor("qpF", [128, 32 * 512], BF16))
        attF = pb(nc.sbuf_tensor("attF", [128, 8 * 512], BF16))
        zrow = pb(nc.sbuf_tensor("zrow", [16, 512], F32))
        z2s = pb(nc.sbuf_tensor("z2s", [128, 512], F32))
        x1t = pb(nc.sbuf_tensor("x1t", [128, 1024], F32))
        xb4 = [pb(nc.sbuf_tensor(f"xb4_{s}", [128, 1024], F32)) for s in range(4)]
        s1c = pb(nc.sbuf_tensor("s1c", [128, NT128], F32))
        s2c = pb(nc.sbuf_tensor("s2c", [128, NT128], F32))
        muC = pb(nc.sbuf_tensor("muC", [128, NT128], F32))
        v1C = pb(nc.sbuf_tensor("v1C", [128, NT128], F32))
        u1C = pb(nc.sbuf_tensor("u1C", [128, NT128], F32))
        r1C = pb(nc.sbuf_tensor("r1C", [128, NT128], F32))
        u2C = pb(nc.sbuf_tensor("u2C", [128, NT128], F32))
        ps_den = pb(nc.psum_tensor("ps_den", [16, 512], F32))
        ps_z2 = pb(nc.psum_tensor("ps_z2", [128, 512], F32))
        ps_o = [pb(nc.psum_tensor(f"ps_o{i}", [128, 512], F32)) for i in range(2)]

        with nc.Block() as block:
            psA_free_v = [p.at("ve")] * 2
            psA_free_s = [p.at("se")] * 2
            o_free = [p.at("ve")] * 2
            tr_free = p.at("ve")
            z2_free = p.at("ve")
            den_free = p.at("ve")
            xb_free_v = [p.at("ve")] * 4
            xb_free_s = [p.at("se")] * 4
            xln_free = [p.at("pe")] * 2
            x1_gate = p.at("sG")
            prev_pe = p.at("pe")
            for t512 in range(NT512):
                for s in range(4):
                    i128 = t512 * 4 + s
                    par = i128 % 2

                    def load_x(sync, i128=i128, s=s, fv=xb_free_v[s], fs=xb_free_s[s]):
                        p.wait(sync, "ve", fv)
                        p.wait(sync, "se", fs)
                        p.bump("sL", sync.dma_start(xb4[s][:],
                                                    X[i128 * 128:(i128 + 1) * 128, :]), 16)
                    block.sync(load_x)
                    lv = p.at("sL")

                    def vs(vector, s=s, lv=lv):
                        p.wait(vector, "sL", lv)
                        p.bump("ve", vector.tensor_reduce(s1t[:, 0:1], xb4[s][:],
                                                          AX.X, ALU.add))
                    block.vector(vs)
                    v_s1 = p.at("ve")

                    def ss(scalar, s=s, lv=lv):
                        p.wait(scalar, "sL", lv)
                        p.bump("se", scalar.activation(junk1k[:, :], xb4[s][:], AF.Square,
                                                       accum_out=s2t[:, 0:1]))
                    block.scalar(ss)
                    s_s2 = p.at("se")
                    v_bt = emit_ln_chain(block, v_s1, s_s2)

                    def sa(scalar, s=s, par=par, v_bt=v_bt, fa=xln_free[par]):
                        p.wait(scalar, "ve", v_bt)
                        p.wait(scalar, "pe", fa)
                        i = scalar.activation(xlnT[par][:], xb4[s][:], AF.Identity,
                                              bias=bt[:, 0:1], scale=rt[:, 0:1])
                        p.bump("se", i)
                    block.scalar(sa)
                    s_ap = p.at("se")
                    xb_free_s[s] = s_ap
                    xb_free_v[s] = v_s1

                    def trs(tensor, par=par, s_ap=s_ap, fa=tr_free):
                        p.wait(tensor, "se", s_ap)
                        p.wait(tensor, "ve", fa)
                        for c in range(8):
                            i = tensor.transpose(ps_tr[:, c * 128:(c + 1) * 128],
                                                 xlnT[par][:, c * 128:(c + 1) * 128], identb[:])
                        p.bump("pe", i)
                    block.tensor(trs)
                    pe_tr = p.at("pe")
                    xln_free[par] = pe_tr

                    def evtr(vector, s=s, pe_tr=pe_tr):
                        p.wait(vector, "pe", pe_tr)
                        src = ps_tr[:].rearrange("p (c x) -> p c x", x=128)
                        dst = xlnF[:].rearrange("p (c x) -> p c x", x=512)[:, :, s * 128:(s + 1) * 128]
                        p.bump("ve", vector.tensor_copy(dst, src))
                    block.vector(evtr)
                    tr_free = p.at("ve")

                xlnF_ready = p.at("ve")
                for m in range(8):
                    bank = m % 2

                    def mmq(tensor, m=m, bank=bank, fv=psA_free_v[bank], fs=psA_free_s[bank],
                            xr=xlnF_ready):
                        p.wait(tensor, "ve", max(fv, xr))
                        p.wait(tensor, "se", fs)
                        for kc in range(8):
                            i = tensor.matmul(ps_a[bank][:],
                                              wqB[:, kc * 1024 + m * 128: kc * 1024 + (m + 1) * 128],
                                              xlnF[:, kc * 512:(kc + 1) * 512],
                                              start=(kc == 0), stop=(kc == 7))
                        p.bump("pe", i)
                    block.tensor(mmq)
                    pv = p.at("pe")

                    def evq(vector, m=m, bank=bank, pv=pv, pef=prev_pe):
                        p.wait(vector, "pe", max(pv, pef))
                        p.bump("ve", vector.tensor_copy(qF[:, m * 512:(m + 1) * 512],
                                                        ps_a[bank][:]))
                    block.vector(evq)
                    psA_free_v[bank] = p.at("ve")
                qF_ready = p.at("ve")

                for j in range(8):
                    for c4 in range(4):
                        cf = 4 * j + c4
                        bank = cf % 2

                        def mmqp(tensor, j=j, c4=c4, bank=bank, fv=psA_free_v[bank],
                                 fs=psA_free_s[bank], qr=qF_ready):
                            p.wait(tensor, "ve", max(fv, qr))
                            p.wait(tensor, "se", fs)
                            i = tensor.matmul(ps_a[bank][:],
                                              projp[:, j * 512 + c4 * 128: j * 512 + (c4 + 1) * 128],
                                              qF[:, j * 512:(j + 1) * 512],
                                              start=True, stop=True)
                            p.bump("pe", i)
                        block.tensor(mmqp)
                        pv = p.at("pe")

                        def vmin(vector, bank=bank, pv=pv):
                            p.wait(vector, "pe", pv)
                            p.bump("ve", vector.tensor_scalar_min(mbuf[:], ps_a[bank][:], 0.0))
                        block.vector(vmin)
                        v_m = p.at("ve")

                        def srel(scalar, bank=bank, pv=pv, v_m=v_m):
                            p.wait(scalar, "pe", pv)
                            p.wait(scalar, "ve", v_m)
                            scalar.activation(rbuf[:], ps_a[bank][:], AF.Relu)
                            i = scalar.activation(ebuf[:], mbuf[:], AF.Exp)
                            p.bump("se", i)
                        block.scalar(srel)
                        s_re = p.at("se")

                        def vadd(vector, cf=cf, s_re=s_re, pef=prev_pe):
                            p.wait(vector, "se", s_re)
                            p.wait(vector, "pe", pef)
                            i = vector.tensor_tensor(qpF[:, cf * 512:(cf + 1) * 512],
                                                     rbuf[:], ebuf[:], ALU.add)
                            p.bump("ve", i)
                        block.vector(vadd)
                        psA_free_v[bank] = v_m
                        psA_free_s[bank] = s_re
                qp_ready = p.at("ve")

                def mmden(tensor, qr=qp_ready, fa=den_free):
                    p.wait(tensor, "ve", max(qr, fa))
                    for c in range(32):
                        i = tensor.matmul(ps_den[:], ksblk[:, c * 16:(c + 1) * 16],
                                          qpF[:, c * 512:(c + 1) * 512],
                                          start=(c == 0), stop=(c == 31))
                    p.bump("pe", i)
                block.tensor(mmden)
                pv_den = p.at("pe")

                def vz(vector, pv=pv_den):
                    p.wait(vector, "pe", pv)
                    vector.tensor_scalar(zrow[:, :], ps_den[:, :], 1.0, float(ZEPS),
                                         ALU.mult, ALU.add)
                    i = vector.reciprocal(zrow[:, :], zrow[:, :])
                    p.bump("ve", i)
                block.vector(vz)
                v_z = p.at("ve")
                den_free = v_z

                for c2 in range(8):
                    bank = c2 % 2

                    def mmat(tensor, c2=c2, bank=bank, fa=o_free[bank], qr=qp_ready):
                        p.wait(tensor, "ve", max(fa, qr))
                        for hh in range(2):
                            h = 2 * c2 + hh
                            for cc2 in range(2):
                                c = 2 * h + cc2
                                i = tensor.matmul(ps_o[bank][hh * 64:(hh + 1) * 64, :],
                                                  kvT[:, c * 65: c * 65 + 64],
                                                  qpF[:, c * 512:(c + 1) * 512],
                                                  start=(cc2 == 0), stop=(cc2 == 1),
                                                  skip_group_check=True)
                        p.bump("pe", i)
                    block.tensor(mmat)
                    pv_at = p.at("pe")

                    def mmz2(tensor, c2=c2, v_z=v_z, fa=z2_free):
                        p.wait(tensor, "ve", max(v_z, fa))
                        i = tensor.matmul(ps_z2[:], e16[:, c2 * 128:(c2 + 1) * 128],
                                          zrow[:, :], start=True, stop=True)
                        p.bump("pe", i)
                    block.tensor(mmz2)
                    pv_z2 = p.at("pe")

                    def evz2(vector, pv=pv_z2):
                        p.wait(vector, "pe", pv)
                        p.bump("ve", vector.tensor_copy(z2s[:], ps_z2[:]))
                    block.vector(evz2)
                    v_z2 = p.at("ve")
                    z2_free = v_z2

                    def evat(vector, c2=c2, bank=bank, pv=pv_at, v_z2=v_z2, pef=prev_pe):
                        p.wait(vector, "pe", max(pv, pef))
                        p.wait(vector, "ve", v_z2)
                        i = vector.tensor_tensor(attF[:, c2 * 512:(c2 + 1) * 512],
                                                 ps_o[bank][:], z2s[:], ALU.mult)
                        p.bump("ve", i)
                    block.vector(evat)
                    o_free[bank] = p.at("ve")
                att_ready = p.at("ve")

                for s in range(4):
                    i128 = t512 * 4 + s
                    pv_o = [0, 0]
                    for nh2 in range(2):
                        def mmo(tensor, s=s, nh2=nh2, ar=att_ready, fa=o_free[nh2]):
                            p.wait(tensor, "ve", max(ar, fa))
                            for kc in range(8):
                                tensor.matmul(
                                    ps_o[nh2][:],
                                    attF[:, kc * 512 + s * 128: kc * 512 + (s + 1) * 128],
                                    woutB[:, kc * 1024 + nh2 * 512: kc * 1024 + (nh2 + 1) * 512],
                                    start=(kc == 0), stop=False)
                            i = tensor.matmul(ps_o[nh2][:], onesb[:, 0:128],
                                              boutr[:, nh2 * 512:(nh2 + 1) * 512],
                                              start=False, stop=True)
                            p.bump("pe", i)
                        block.tensor(mmo)
                        pv_o[nh2] = p.at("pe")

                    for nh2 in range(2):
                        def vres(vector, s=s, nh2=nh2, pv=pv_o[nh2], xg=x1_gate):
                            p.wait(vector, "pe", pv)
                            p.wait(vector, "sG", xg)
                            i = vector.tensor_tensor(x1t[:, nh2 * 512:(nh2 + 1) * 512],
                                                     xb4[s][:, nh2 * 512:(nh2 + 1) * 512],
                                                     ps_o[nh2][:], ALU.add)
                            p.bump("ve", i)
                        block.vector(vres)
                        o_free[nh2] = p.at("ve")
                    v_x1 = p.at("ve")
                    xb_free_v[s] = v_x1

                    def vst(vector, i128=i128):
                        p.bump("ve", vector.tensor_reduce(s1c[:, i128:i128 + 1], x1t[:],
                                                          AX.X, ALU.add))
                    block.vector(vst)

                    def sst(scalar, i128=i128, v_x1=v_x1):
                        p.wait(scalar, "ve", v_x1)
                        p.bump("se", scalar.activation(junk1k[:, :], x1t[:], AF.Square,
                                                       accum_out=s2c[:, i128:i128 + 1]))
                    block.scalar(sst)
                    s_st = p.at("se")

                    def gx1(g, i128=i128, s_st=s_st):
                        p.wait(g, "se", s_st)
                        p.bump("sG", g.dma_start(X1D.ap()[i128 * 128:(i128 + 1) * 128, :],
                                                 x1t[:]), 16)
                    block.gpsimd(gx1)
                    x1_gate = p.at("sG")
                prev_pe = p.at("pe")

            def _gap(vector):
                i = vector.nop()
            steps = [
                lambda v: v.tensor_scalar_mul(muC[:], s1c[:], 1.0 / 1024.0),
                lambda v: v.tensor_tensor(v1C[:], muC[:], muC[:], ALU.mult),
                lambda v: v.tensor_scalar_mul(u1C[:], s2c[:], 1.0 / 1024.0),
                lambda v: v.tensor_tensor(v1C[:], u1C[:], v1C[:], ALU.subtract),
                lambda v: v.tensor_scalar(u1C[:], v1C[:], 1.0, float(EPS), ALU.mult, ALU.add),
            ]
            first_se = p.at("se")
            for stp in steps:
                def vstep(vector, stp=stp, fa=p.at("ve"), fs=first_se):
                    p.wait(vector, "ve", fa)
                    p.wait(vector, "se", fs)
                    p.bump("ve", stp(vector))
                block.vector(vstep)
            vv = p.at("ve")

            def scC(scalar, vv=vv):
                p.wait(scalar, "ve", vv)
                scalar.activation(r1C[:], u1C[:], AF.Ln)
                i = scalar.activation(r1C[:], r1C[:], AF.Exp, scale=-0.5)
                p.bump("se", i)
            block.scalar(scC)
            sv = p.at("se")

            steps2 = [
                lambda v: v.reciprocal(u2C[:], u1C[:]),
                lambda v: v.tensor_tensor(u2C[:], v1C[:], u2C[:], ALU.mult),
                lambda v: v.tensor_scalar(u2C[:], u2C[:], 1.0, float(EPS), ALU.mult, ALU.add),
            ]
            for stp in steps2:
                def vstep2(vector, stp=stp, fa=p.at("ve"), sv=sv):
                    p.wait(vector, "ve", fa)
                    p.wait(vector, "se", sv)
                    p.bump("ve", stp(vector))
                block.vector(vstep2)
            vv2 = p.at("ve")

            def scC2(scalar, vv2=vv2):
                p.wait(scalar, "ve", vv2)
                scalar.activation(u2C[:], u2C[:], AF.Ln)
                i = scalar.activation(u2C[:], u2C[:], AF.Exp, scale=-0.5)
                p.bump("se", i)
            block.scalar(scC2)
            sv2 = p.at("se")

            steps3 = [
                lambda v: v.tensor_tensor(aC[:], r1C[:], u2C[:], ALU.mult),
                lambda v: v.tensor_scalar_mul(cC[:], muC[:], -1.0),
                lambda v: v.tensor_tensor(cC[:], cC[:], aC[:], ALU.mult),
            ]
            for stp in steps3:
                def vstep3(vector, stp=stp, fa=p.at("ve"), sv2=sv2):
                    p.wait(vector, "ve", fa)
                    p.wait(vector, "se", sv2)
                    p.bump("ve", stp(vector))
                block.vector(vstep3)
        st_pb.close()
        st_ab.close()

        if "x1" in taps:
            with nc.Block() as block:
                def _(scalar):
                    p.wait(scalar, "sG", p.at("sG"))
                    p.bump("sS", scalar.dma_start(tap_aps["t_x1"], X1D.ap()), 16)
                block.scalar(_)

        if stage < 3:
            with nc.Block() as block:
                def _(g):
                    p.wait(g, "sG", p.at("sG"))
                    p.wait(g, "sS", p.at("sS"))
                block.gpsimd(_)
            return nc, tap_aps

        st_w = ExitStack()
        w_ = st_w.enter_context
        w1gT = w_(nc.sbuf_tensor("w1gT", [128, 8 * 4096], BF16))
        w2gT = w_(nc.sbuf_tensor("w2gT", [128, 32 * 1024], BF16))
        c1row = w_(nc.sbuf_tensor("c1row", [1, HID], BF16))
        c2b = w_(nc.sbuf_tensor("c2b", [128, D], F32))

        st_pr2 = ExitStack()
        r2_ = st_pr2.enter_context
        wstg = [r2_(nc.sbuf_tensor(f"wstg2_{i}", [128, 2048], F32)) for i in range(2)]
        sgn1 = r2_(nc.sbuf_tensor("sgn1", [128, 2048], BF16))
        sgn2 = r2_(nc.sbuf_tensor("sgn2", [128, 2048], BF16))
        tq = r2_(nc.sbuf_tensor("tq", [128, 4096], BF16))
        tqT = r2_(nc.sbuf_tensor("tqT", [128, 1024], BF16))
        cb1row = r2_(nc.sbuf_tensor("cb1row", [1, HID], F32))
        cb2row = r2_(nc.sbuf_tensor("cb2row", [1, D], F32))
        ps_cb = r2_(nc.psum_tensor("ps_cb", [1, 512], F32))
        ps_oc = [r2_(nc.psum_tensor(f"ps_oc{i}", [128, 512], F32)) for i in range(2)]

        with nc.Block() as block:
            stg_free = [p.at("se"), p.at("se")]
            tq_free_pe = p.at("pe")
            for c in range(32):
                par = c % 2

                def load(sync, c=c, par=par, fa=stg_free[par]):
                    p.wait(sync, "se", fa)
                    p.bump("sL", sync.dma_start(wstg[par][:, 0:1024],
                                                W1[c * 128:(c + 1) * 128, :]), 16)
                block.sync(load)
                lv = p.at("sL")

                def sgn(scalar, par=par, lv=lv, fv=p.at("ve")):
                    p.wait(scalar, "sL", lv)
                    p.wait(scalar, "ve", fv)
                    scalar.activation(sgn1[:, 0:1024], wstg[par][:, 0:1024], AF.Sign,
                                      bias=nthr1[:, 0:1])
                    i = scalar.activation(sgn2[:, 0:1024], wstg[par][:, 0:1024], AF.Sign,
                                          bias=thr1[:, 0:1])
                    p.bump("se", i)
                block.scalar(sgn)
                sv = p.at("se")
                stg_free[par] = sv

                def vq(vector, sv=sv, tf=tq_free_pe):
                    p.wait(vector, "se", sv)
                    p.wait(vector, "pe", tf)
                    i = vector.tensor_tensor(tq[:, 0:1024], sgn1[:, 0:1024],
                                             sgn2[:, 0:1024], ALU.add)
                    p.bump("ve", i)
                block.vector(vq)
                v_tq = p.at("ve")

                def trq(tensor, v_tq=v_tq, fa=p.at("ve")):
                    p.wait(tensor, "ve", max(v_tq, fa))
                    for cd in range(8):
                        i = tensor.transpose(ps_tr[:, cd * 128:(cd + 1) * 128],
                                             tq[:, cd * 128:(cd + 1) * 128], identb[:])
                    p.bump("pe", i)
                block.tensor(trq)
                pv_tr = p.at("pe")
                tq_free_pe = pv_tr

                def evu(vector, pv=pv_tr, fs=p.at("se")):
                    p.wait(vector, "pe", pv)
                    p.wait(vector, "se", fs)
                    i = vector.tensor_copy(tqT[:, 0:1024], ps_tr[:])
                    p.bump("ve", i)
                block.vector(evu)
                v_un = p.at("ve")

                def mcb(tensor, v_un=v_un):
                    p.wait(tensor, "ve", v_un)
                    for cd in range(8):
                        i = tensor.matmul(ps_cb[0:1, 0:128], b1lnB[:, cd:cd + 1],
                                          tqT[:, cd * 128:(cd + 1) * 128],
                                          start=(cd == 0), stop=(cd == 7))
                    p.bump("pe", i)
                block.tensor(mcb)
                pv_cb = p.at("pe")

                def evs(vector, c=c, pv_cb=pv_cb):
                    p.wait(vector, "pe", pv_cb)
                    vector.tensor_copy(cb1row[:, c * 128:(c + 1) * 128], ps_cb[0:1, 0:128])
                    for cd in range(8):
                        i = vector.tensor_scalar_mul(
                            w1gT[:, cd * 4096 + c * 128: cd * 4096 + (c + 1) * 128],
                            tqT[:, cd * 128:(cd + 1) * 128], g1F[:, cd:cd + 1])
                    p.bump("ve", i)
                block.vector(evs)
                v_evs = p.at("ve")

                def gap1(tensor, v_evs=v_evs):
                    p.wait(tensor, "ve", v_evs)
                block.tensor(gap1)

            for hh in range(2):
                def loadb(sync, hh=hh, fa=p.at("ve")):
                    p.wait(sync, "ve", fa)
                    p.bump("sL", sync.dma_start(wstg[0][0:1, 0:2048],
                                                B1R[0:1, hh * 2048:(hh + 1) * 2048]), 16)
                block.sync(loadb)
                lvb = p.at("sL")

                def vfin1(vector, hh=hh, lvb=lvb):
                    p.wait(vector, "sL", lvb)
                    vector.tensor_scalar_mul(wstg[0][0:1, 0:2048], wstg[0][0:1, 0:2048],
                                             im1[0:1, 0:1])
                    vector.tensor_tensor(wstg[0][0:1, 0:2048], wstg[0][0:1, 0:2048],
                                         cb1row[:, hh * 2048:(hh + 1) * 2048], ALU.add)
                    i = vector.tensor_copy(c1row[:, hh * 2048:(hh + 1) * 2048],
                                           wstg[0][0:1, 0:2048])
                    p.bump("ve", i)
                block.vector(vfin1)

            for c in range(8):
                for hc in range(2):
                    par = hc

                    def load(sync, c=c, hc=hc, par=par, fa=p.at("se")):
                        p.wait(sync, "se", fa)
                        p.wait(sync, "ve", p.at("ve"))
                        p.bump("sL", sync.dma_start(
                            wstg[par][:, 0:2048],
                            W2[c * 128:(c + 1) * 128, hc * 2048:(hc + 1) * 2048]), 16)
                    block.sync(load)
                    lv = p.at("sL")

                    def sgn(scalar, par=par, lv=lv, fv=p.at("ve")):
                        p.wait(scalar, "sL", lv)
                        p.wait(scalar, "ve", fv)
                        scalar.activation(sgn1[:, 0:2048], wstg[par][:, 0:2048], AF.Sign,
                                          bias=nthr2[:, 0:1])
                        i = scalar.activation(sgn2[:, 0:2048], wstg[par][:, 0:2048], AF.Sign,
                                              bias=thr2[:, 0:1])
                        p.bump("se", i)
                    block.scalar(sgn)
                    sv = p.at("se")

                    def vq(vector, hc=hc, sv=sv, tf=tq_free_pe):
                        p.wait(vector, "se", sv)
                        p.wait(vector, "pe", tf)
                        i = vector.tensor_tensor(tq[:, hc * 2048:(hc + 1) * 2048],
                                                 sgn1[:, 0:2048], sgn2[:, 0:2048], ALU.add)
                        p.bump("ve", i)
                    block.vector(vq)
                v_tq = p.at("ve")

                for g4 in range(4):
                    def trq(tensor, g4=g4, v_tq=v_tq, fa=p.at("ve")):
                        p.wait(tensor, "ve", max(v_tq, fa))
                        for k8 in range(8):
                            ch = g4 * 8 + k8
                            i = tensor.transpose(ps_tr[:, k8 * 128:(k8 + 1) * 128],
                                                 tq[:, ch * 128:(ch + 1) * 128], identb[:])
                        p.bump("pe", i)
                    block.tensor(trq)
                    pv_tr = p.at("pe")

                    def evu(vector, pv=pv_tr):
                        p.wait(vector, "pe", pv)
                        i = vector.tensor_copy(tqT[:, 0:1024], ps_tr[:])
                        p.bump("ve", i)
                    block.vector(evu)
                    v_un = p.at("ve")

                    def mcb(tensor, g4=g4, v_un=v_un):
                        p.wait(tensor, "ve", v_un)
                        for k8 in range(8):
                            ch = g4 * 8 + k8
                            i = tensor.matmul(ps_cb[0:1, 0:128], ln2bB[:, ch:ch + 1],
                                              tqT[:, k8 * 128:(k8 + 1) * 128],
                                              start=(ch == 0), stop=(ch == 31),
                                              skip_group_check=True)
                        p.bump("pe", i)
                    block.tensor(mcb)
                    pv_cb = p.at("pe")

                    def evs(vector, c=c, g4=g4, pv=pv_cb):
                        p.wait(vector, "pe", pv)
                        for k8 in range(8):
                            ch = g4 * 8 + k8
                            i = vector.tensor_scalar_mul(
                                w2gT[:, ch * 1024 + c * 128: ch * 1024 + (c + 1) * 128],
                                tqT[:, k8 * 128:(k8 + 1) * 128], g2F[:, ch:ch + 1])
                        p.bump("ve", i)
                    block.vector(evs)
                tq_free_pe = p.at("pe")

                def evcb2(vector, c=c, fa=p.at("pe")):
                    p.wait(vector, "pe", fa)
                    i = vector.tensor_copy(cb2row[:, c * 128:(c + 1) * 128], ps_cb[0:1, 0:128])
                    p.bump("ve", i)
                block.vector(evcb2)

            def loadb2(sync, fa=p.at("ve")):
                p.wait(sync, "ve", fa)
                p.bump("sL", sync.dma_start(wstg[0][0:1, 0:D], B2R), 16)
            block.sync(loadb2)
            lvb2 = p.at("sL")

            def vfin2(vector, lvb2=lvb2):
                p.wait(vector, "sL", lvb2)
                vector.tensor_scalar_mul(cb2row[:, 0:D], cb2row[:, 0:D], m2s[0:1, 0:1])
                vector.tensor_scalar_mul(cb2row[:, 0:D], cb2row[:, 0:D], 0.5)
                i = vector.tensor_tensor(cb2row[:, 0:D], cb2row[:, 0:D],
                                         wstg[0][0:1, 0:D], ALU.add)
                p.bump("ve", i)
            block.vector(vfin2)
            vv = p.at("ve")

            def mmc2(tensor, vv=vv):
                p.wait(tensor, "ve", vv)
                tensor.matmul(ps_oc[0][:], onesrow[:, :], cb2row[:, 0:512],
                              start=True, stop=True)
                i = tensor.matmul(ps_oc[1][:], onesrow[:, :], cb2row[:, 512:1024],
                                  start=True, stop=True)
                p.bump("pe", i)
            block.tensor(mmc2)
            pv = p.at("pe")

            def evc2(vector, pv=pv):
                p.wait(vector, "pe", pv)
                vector.tensor_copy(c2b[:, 0:512], ps_oc[0][:])
                i = vector.tensor_copy(c2b[:, 512:1024], ps_oc[1][:])
                p.bump("ve", i)
            block.vector(evc2)

            if "qc" in taps:
                def vm(vector):
                    vector.tensor_copy(junk1k[0:1, 0:1], m1s[:, :])
                    vector.tensor_copy(junk1k[0:1, 2:3], im1[:, :])
                    vector.tensor_copy(junk1k[0:1, 3:4], thr1[0:1, 0:1])
                    vector.tensor_copy(junk1k[0:1, 4:5], wstg[0][0:1, 5:6])
                    vector.tensor_copy(junk1k[0:1, 5:6], cb1row[0:1, 5:6])
                    i = vector.tensor_copy(junk1k[0:1, 1:2], m2s[:, :])
                    p.bump("ve", i)
                block.vector(vm)

                def tqc(g):
                    p.wait(g, "ve", p.at("ve"))
                    p.bump("sG", g.dma_start(tap_aps["t_c1"], c1row[:]), 16)
                    p.bump("sG", g.dma_start(tap_aps["t_c2"], c2b[:]), 16)
                    p.bump("sG", g.dma_start(tap_aps["t_w1"], w1gT[:, 0:4096]), 16)
                    p.bump("sG", g.dma_start(tap_aps["t_w2"], w2gT[:, 0:1024]), 16)
                    p.bump("sG", g.dma_start(tap_aps["t_m"], junk1k[0:1, 0:8]), 16)
                block.gpsimd(tqc)
        st_pr2.close()

        st_pc = ExitStack()
        pc = st_pc.enter_context
        ps_o = [pc(nc.psum_tensor(f"ps_op{i}", [128, 512], F32)) for i in range(2)]
        hg = pc(nc.sbuf_tensor("hg", [128, HID], F32))
        hn = pc(nc.sbuf_tensor("hn", [128, HID], BF16))
        hnT = pc(nc.sbuf_tensor("hnT", [128, 32 * 128], BF16))
        hgs = pc(nc.sbuf_tensor("hgs", [128, 8], F32))
        hgss = pc(nc.sbuf_tensor("hgss", [128, 8], F32))
        x1b = [pc(nc.sbuf_tensor(f"x1b{i}", [128, 1024], F32)) for i in range(2)]
        outb = pc(nc.sbuf_tensor("outb", [128, 1024], F32))
        r2k2 = pc(nc.sbuf_tensor("r2k2", [128, 1], F32))
        xlnC = pc(nc.sbuf_tensor("xlnC", [128, 1024], BF16))
        xlnCF = pc(nc.sbuf_tensor("xlnCF", [128, 8 * 128], BF16))

        with nc.Block() as block:
            st_pc_state = {
                "x1_free_v": [p.at("ve")] * 2,
                "x1_free_s": [p.at("se")] * 2,
                "xlnC_free": p.at("pe"),
                "v_xf": p.at("ve"),
            }
            psA_free = [p.at("se")] * 2
            o_free = [p.at("ve")] * 2
            out_gate = p.at("sG")
            hn_free = p.at("pe")
            x1_store_all = p.at("sG")

            def emit_front(i):
                par = i % 2

                def loadx1(sync, i=i, par=par, fv=st_pc_state["x1_free_v"][par],
                           fs=st_pc_state["x1_free_s"][par], xs=x1_store_all):
                    p.wait(sync, "sG", xs)
                    p.wait(sync, "ve", fv)
                    p.wait(sync, "se", fs)
                    p.bump("sL", sync.dma_start(x1b[par][:],
                                                X1D.ap()[i * 128:(i + 1) * 128, :]), 16)
                block.sync(loadx1)
                lv = p.at("sL")

                def sap1(scalar, i=i, par=par, lv=lv, fa=st_pc_state["xlnC_free"]):
                    p.wait(scalar, "sL", lv)
                    p.wait(scalar, "pe", fa)
                    j = scalar.activation(xlnC[:], x1b[par][:], AF.Identity,
                                          bias=cC[:, i:i + 1], scale=aC[:, i:i + 1])
                    p.bump("se", j)
                block.scalar(sap1)
                s_ap = p.at("se")
                st_pc_state["x1_free_s"][par] = s_ap

                def trC(tensor, s_ap=s_ap, fa=p.at("ve")):
                    p.wait(tensor, "se", s_ap)
                    p.wait(tensor, "ve", fa)
                    for c in range(8):
                        j = tensor.transpose(ps_tr[:, c * 128:(c + 1) * 128],
                                             xlnC[:, c * 128:(c + 1) * 128], identb[:])
                    p.bump("pe", j)
                block.tensor(trC)
                pe_tr = p.at("pe")
                st_pc_state["xlnC_free"] = pe_tr

                def evtrC(vector, pe_tr=pe_tr):
                    p.wait(vector, "pe", pe_tr)
                    p.bump("ve", vector.tensor_copy(xlnCF[:, 0:1024], ps_tr[:]))
                block.vector(evtrC)
                st_pc_state["v_xf"] = p.at("ve")

            emit_front(0)
            for i in range(NT128):
                par = i % 2
                v_xf = st_pc_state["v_xf"]

                for n in range(8):
                    bank = n % 2

                    def mm1(tensor, n=n, bank=bank, v_xf=v_xf, fa=psA_free[bank]):
                        p.wait(tensor, "ve", v_xf)
                        p.wait(tensor, "se", fa)
                        for kc in range(8):
                            tensor.matmul(
                                ps_a[bank][:],
                                xlnCF[:, kc * 128:(kc + 1) * 128],
                                w1gT[:, kc * 4096 + n * 512: kc * 4096 + (n + 1) * 512],
                                start=(kc == 0), stop=False)
                        j = tensor.matmul(ps_a[bank][:], onesb[:, 0:128],
                                          c1row[:, n * 512:(n + 1) * 512],
                                          start=False, stop=True)
                        p.bump("pe", j)
                    block.tensor(mm1)
                    pv = p.at("pe")

                    def sg(scalar, n=n, bank=bank, pv=pv):
                        p.wait(scalar, "pe", pv)
                        j = scalar.activation(hg[:, n * 512:(n + 1) * 512], ps_a[bank][:],
                                              AF.Gelu, scale=k1b[:, 0:1],
                                              accum_out=hgs[:, n:n + 1])
                        p.bump("se", j)
                    block.scalar(sg)
                    psA_free[bank] = p.at("se")
                s_gelu = p.at("se")
                if i + 1 < NT128:
                    emit_front(i + 1)

                def vss(vector, s_gelu=s_gelu):
                    p.wait(vector, "se", s_gelu)
                    j = None
                    for n in range(8):
                        j = vector.scalar_tensor_tensor(junk1k[:, 0:512],
                                                        hg[:, n * 512:(n + 1) * 512], 1.0,
                                                        hg[:, n * 512:(n + 1) * 512],
                                                        ALU.mult, ALU.mult,
                                                        accum_out=hgss[:, n:n + 1])
                    p.bump("ve", j)
                block.vector(vss)
                v_acc = p.at("ve")

                def vss2(vector, v_acc=v_acc):
                    p.wait(vector, "ve", v_acc)
                    vector.tensor_reduce(s1t[:, 0:1], hgs[:], AX.X, ALU.add)
                    j = vector.tensor_reduce(s2t[:, 0:1], hgss[:], AX.X, ALU.add)
                    p.bump("ve", j)
                block.vector(vss2)

                def vss3(vector, fa=p.at("ve")):
                    p.wait(vector, "ve", fa)
                    vector.tensor_scalar_mul(mut[:, 0:1], s1t[:, 0:1], 1.0 / HID)
                    vector.tensor_tensor(tsc[:, 0:1], s1t[:, 0:1], s1t[:, 0:1], ALU.mult)
                    j = vector.tensor_scalar_mul(vart[:, 0:1], s2t[:, 0:1], 1.0 / HID)
                    p.bump("ve", j)
                block.vector(vss3)

                def vss4(vector, fa=p.at("ve")):
                    p.wait(vector, "ve", fa)
                    j = vector.scalar_tensor_tensor(tsc[:, 0:1], tsc[:, 0:1],
                                                    1.0 / (float(HID) * HID),
                                                    vart[:, 0:1], ALU.mult, ALU.subtract)
                    p.bump("ve", j)
                block.vector(vss4)

                def vss5(vector, fa=p.at("ve")):
                    p.wait(vector, "ve", fa)
                    vector.tensor_scalar(vart[:, 0:1], tsc[:, 0:1], -1.0, float(EPS),
                                         ALU.mult, ALU.add)
                    j = vector.tensor_scalar_mul(bt[:, 0:1], mut[:, 0:1], -1.0)
                    p.bump("ve", j)
                block.vector(vss5)
                v_st2 = p.at("ve")

                def sap2(scalar, v_st2=v_st2, hf=hn_free):
                    p.wait(scalar, "ve", v_st2)
                    p.wait(scalar, "pe", hf)
                    j = scalar.activation(hn[:], hg[:], AF.Identity, bias=bt[:, 0:1])
                    p.bump("se", j)
                block.scalar(sap2)
                s_hn = p.at("se")

                def sr2(scalar):
                    scalar.activation(rt[:, 0:1], vart[:, 0:1], AF.Ln)
                    j = scalar.activation(rt[:, 0:1], rt[:, 0:1], AF.Exp, scale=-0.5)
                    p.bump("se", j)
                block.scalar(sr2)
                s_r2 = p.at("se")

                def vr2(vector, s_r2=s_r2):
                    p.wait(vector, "se", s_r2)
                    j = vector.tensor_tensor(r2k2[:, 0:1], rt[:, 0:1], k2b[:, 0:1], ALU.mult)
                    p.bump("ve", j)
                block.vector(vr2)
                v_r2 = p.at("ve")

                for g4 in range(4):
                    pst = ps_tr if g4 % 2 == 0 else ps_tr2

                    def trh(tensor, g4=g4, pst=pst, s_hn=s_hn, fa=p.at("ve")):
                        p.wait(tensor, "se", s_hn)
                        p.wait(tensor, "ve", fa)
                        for k8 in range(8):
                            ch = g4 * 8 + k8
                            j = tensor.transpose(pst[:, k8 * 128:(k8 + 1) * 128],
                                                 hn[:, ch * 128:(ch + 1) * 128], identb[:])
                        p.bump("pe", j)
                    block.tensor(trh)
                    pvh = p.at("pe")

                    def evh(vector, g4=g4, pst=pst, pvh=pvh):
                        p.wait(vector, "pe", pvh)
                        p.bump("ve", vector.tensor_copy(hnT[:, g4 * 1024:(g4 + 1) * 1024],
                                                        pst[:]))
                    block.vector(evh)
                hnT_ready = p.at("ve")
                hn_free = p.at("pe")

                pv_o = [0, 0]
                for nh2 in range(2):
                    def mm2(tensor, nh2=nh2, hr=hnT_ready, fa=o_free[nh2]):
                        p.wait(tensor, "ve", max(hr, fa))
                        for ch in range(32):
                            j = tensor.matmul(
                                ps_o[nh2][:],
                                hnT[:, ch * 128:(ch + 1) * 128],
                                w2gT[:, ch * 1024 + nh2 * 512: ch * 1024 + (nh2 + 1) * 512],
                                start=(ch == 0), stop=(ch == 31))
                        p.bump("pe", j)
                    block.tensor(mm2)
                    pv_o[nh2] = p.at("pe")

                for nh2 in range(2):
                    def vep(vector, par=par, nh2=nh2, pv=pv_o[nh2], v_r2=v_r2, og=out_gate):
                        p.wait(vector, "pe", pv)
                        p.wait(vector, "ve", v_r2)
                        p.wait(vector, "sG", og)
                        vector.scalar_tensor_tensor(outb[:, nh2 * 512:(nh2 + 1) * 512],
                                                    ps_o[nh2][:], r2k2[:, 0:1],
                                                    x1b[par][:, nh2 * 512:(nh2 + 1) * 512],
                                                    ALU.mult, ALU.add)
                        j = vector.tensor_tensor(outb[:, nh2 * 512:(nh2 + 1) * 512],
                                                 outb[:, nh2 * 512:(nh2 + 1) * 512],
                                                 c2b[:, nh2 * 512:(nh2 + 1) * 512], ALU.add)
                        p.bump("ve", j)
                    block.vector(vep)
                    o_free[nh2] = p.at("ve")
                v_out = p.at("ve")
                st_pc_state["x1_free_v"][par] = v_out

                def gout(g, i=i, v_out=v_out):
                    p.wait(g, "ve", v_out)
                    p.bump("sG", g.dma_start(OUT[i * 128:(i + 1) * 128, :], outb[:]), 16)
                block.gpsimd(gout)
                out_gate = p.at("sG")
            if "pc" in taps:
                def tpc(g):
                    p.wait(g, "ve", p.at("ve"))
                    p.wait(g, "se", p.at("se"))
                    p.bump("sG", g.dma_start(tap_aps["t_xlnC"], xlnC[:]), 16)
                    p.bump("sG", g.dma_start(tap_aps["t_hg"], hg[:]), 16)
                    p.bump("sG", g.dma_start(tap_aps["t_hn"], hn[:]), 16)
                    p.bump("sG", g.dma_start(tap_aps["t_ac"][:, 0:NT128], aC[:]), 16)
                    p.bump("sG", g.dma_start(tap_aps["t_ac"][:, NT128:2 * NT128], cC[:]), 16)
                    p.bump("sG", g.dma_start(tap_aps["t_r2"], r2k2[:]), 16)
                    p.bump("sG", g.dma_start(tap_aps["t_hgs"], hgs[:]), 16)
                    p.bump("sG", g.dma_start(tap_aps["t_hgss"], hgss[:]), 16)
                block.gpsimd(tpc)
        st_pc.close()
        st_w.close()

        with nc.Block() as block:
            def _(g):
                p.wait(g, "sG", p.at("sG"))
            block.gpsimd(_)

    return nc, tap_aps


def _layout_inputs(L0_attn_x, L0_attn_w_qkv, L0_attn_proj_matrix, L0_attn_w_out,
                   L0_attn_b_out, L0_ffn_ln1_g, L0_ffn_ln1_b, L0_ffn_w1, L0_ffn_b1,
                   L0_ffn_ln2_g, L0_ffn_ln2_b, L0_ffn_w2, L0_ffn_b2):
    import ml_dtypes
    f32 = np.float32
    x = np.asarray(L0_attn_x, f32)
    wqkv = np.asarray(L0_attn_w_qkv, f32)
    proj = np.asarray(L0_attn_proj_matrix, f32)

    projp = np.zeros((8, 128, 512), f32)
    for j in range(8):
        projp[j, 0:64, 0:256] = proj[2 * j]
        projp[j, 64:128, 256:512] = proj[2 * j + 1]

    e16 = np.zeros((16, D), f32)
    for h in range(NH):
        e16[h, h * 64:(h + 1) * 64] = 1.0

    common = {
        "wq": np.ascontiguousarray(wqkv[:, 0:1024]),
        "wk": np.ascontiguousarray(wqkv[:, 1024:2048]),
        "wv": np.ascontiguousarray(wqkv[:, 2048:3072]),
        "projp": projp,
        "wout": np.ascontiguousarray(np.asarray(L0_attn_w_out, f32)),
        "boutrow": np.asarray(L0_attn_b_out, f32).reshape(1, D),
        "w1": np.ascontiguousarray(np.asarray(L0_ffn_w1, f32)),
        "w2": np.ascontiguousarray(np.asarray(L0_ffn_w2, f32)),
        "b1row": np.asarray(L0_ffn_b1, f32).reshape(1, HID),
        "b2row": np.asarray(L0_ffn_b2, f32).reshape(1, D),
        "g1F": np.ascontiguousarray(np.asarray(L0_ffn_ln1_g, f32).reshape(8, 128).T),
        "b1lnF": np.ascontiguousarray(np.asarray(L0_ffn_ln1_b, f32).reshape(8, 128).T),
        "g2F": np.ascontiguousarray(np.asarray(L0_ffn_ln2_g, f32).reshape(32, 128).T),
        "ln2bF": np.ascontiguousarray(np.asarray(L0_ffn_ln2_b, f32).reshape(32, 128).T),
        "identb": np.eye(128, dtype=f32).astype(ml_dtypes.bfloat16),
        "onesb": np.ones((1, 128), f32).astype(ml_dtypes.bfloat16),
        "onesf": np.ones((128, 1), f32),
        "onesrowf": np.ones((1, 128), f32),
        "e16": e16,
    }
    in_maps = []
    for core in range(8):
        b = core // 4
        r0 = (core % 4) * T
        m = dict(common)
        m["x"] = np.ascontiguousarray(x[b, r0:r0 + T, :])
        in_maps.append(m)
    return in_maps


_CACHED = {}


def kernel(**inputs):
    if "prog" not in _CACHED:
        _CACHED["prog"] = build_program()
    nc, _ = _CACHED["prog"]
    in_maps = _layout_inputs(**inputs)
    res = run_bass_kernel_spmd(nc, in_maps, core_ids=list(range(8)))
    out = np.empty((2, 8192, D), np.float32)
    for core in range(8):
        b = core // 4
        r0 = (core % 4) * T
        out[b, r0:r0 + T, :] = res.results[core]["out"]
    return out
